# version 18
# baseline (speedup 1.0000x reference)
import numpy as np
import ml_dtypes
import concourse.bass as bass
import concourse.mybir as mybir

F32 = mybir.dt.float32
BF16 = mybir.dt.bfloat16
AL = mybir.AluOpType
AF = mybir.ActivationFunctionType
AX = mybir.AxisListType
BF = ml_dtypes.bfloat16


class Ctx:

    def __init__(self, nc, stack):
        self.nc = nc
        self.stack = stack
        self.sems = {}
        self.ctr = {}
        self.last_wait = {}
        self.dma_queues = set()
        self.consts = {}
        self._cap = {}

    def sem(self, name):
        if name not in self.sems:
            self.sems[name] = self.stack.enter_context(self.nc.semaphore(name))
            self.ctr[name] = 0
        return self.sems[name]

    def dma(self, eng, out, in_, q="ld", slow=False):
        s = self.sem(q)
        self.dma_queues.add(q)
        kw = {"allow_slow_non_contiguous": True} if slow else {}
        inst = eng.dma_start(out=out, in_=in_, **kw)
        lw = self.last_wait.get(q, 0)
        if lw > 0:
            inst._wait_ge(s, lw)
        inst.then_inc(s, 16)
        self.ctr[q] += 16
        return self.ctr[q]

    def wait_lag(self, eng, q, lag):
        self.sem(q)
        v = max(0, self.ctr[q] - lag)
        self.last_wait[q] = max(self.last_wait.get(q, 0), v)
        eng.wait_ge(self.sem(q), v)

    def chain(self, q):
        self.last_wait[q] = max(self.last_wait.get(q, 0), self.ctr.get(q, 0))

    def mark(self, inst, q):
        s = self.sem(q)
        inst.then_inc(s, 1)
        self.ctr[q] += 1
        return self.ctr[q]

    def wait(self, eng, q, val=None):
        self.sem(q)
        v = self.ctr[q] if val is None else val
        self.last_wait[q] = max(self.last_wait.get(q, 0), v)
        eng.wait_ge(self.sem(q), v)

    def const(self, name, arr):
        if name in self.consts:
            return self._cap[name]
        arr = np.ascontiguousarray(arr)
        dt = {np.dtype(np.float32): F32, np.dtype(BF): BF16}[arr.dtype]
        ap = self.nc.dram_tensor("c_" + name, list(arr.shape), dt,
                                 kind="ExternalInput").ap()
        self.consts["c_" + name] = arr
        self._cap[name] = ap
        return ap

    def sbuf(self, name, shape, dtype=F32):
        if not hasattr(self, "_sbufs"):
            self._sbufs = {}
        if name not in self._sbufs:
            self._sbufs[name] = self.stack.enter_context(
                self.nc.sbuf_tensor(name, shape, dtype))
        return self._sbufs[name]

    def psum(self, name, banks=1):
        return self.stack.enter_context(
            self.nc.psum_tensor(name, [128, 512 * banks], F32))

    def barrier(self):
        self.nc.all_engine_barrier()


class Arena:

    def __init__(self, c, words, name="arena"):
        self.words = words
        self.t = c.sbuf(name, [128, words], F32)

    def f32(self, off, n, partitions=128):
        assert off + n <= self.words, (off, n, self.words)
        return self.t[0:partitions, off:off + n]

    def bf16(self, off, n_elem, partitions=128):
        assert n_elem % 2 == 0
        nw = n_elem // 2
        assert off + nw <= self.words, (off, n_elem, self.words)
        return self.t[0:partitions, off:off + nw].bitcast(BF16)

"""Megakernel: prologue + block 0 (LN -> RWKV-scan attn + MoE -> residual).

Parameterized by SEQ so small-shape sim tests and full-shape device runs share
code. mk1.py adds block 1 (S4 conv + quantized FFN).

Memory plan (per-partition words, SBUF ~53k words usable):
  arena CORE [0, AW): phase-scoped overlays
    block0: xn [0,XN_W) | xatt [XN_W,2*XN_W) | R [2*XN_W, +R_W) phase locals
    block1: chain regions [0, 32768) + locals [32768, +B1L_W)
  statics: identities/ones/cval/cols/rowscr/mask8/b2_b/... (small)
  big per-token rows live in DRAM (rowd/rowd_b/statsh) and are staged as
  [1 x 512] strips on demand.
"""
import numpy as np
import ml_dtypes
from contextlib import ExitStack

import concourse.bass as bass
import concourse.mybir as mybir
import concourse.bass_isa as bass_isa
from concourse import dve_ops
from concourse import library_config


LN_EPS = 1e-5

R_S, R_NEGM, R_S2, R_Q, R_NMQ, R_XSC, R_DS, R_NDMS = range(8)
B_NEGMS, B_NEGMSS2 = 0, 1


class MKBuild:
    def __init__(self, SEQ, D=1024, NCORES=8):
        self.SEQ, self.D, self.NC = SEQ, D, NCORES
        self.T = SEQ // 2
        self.DC = D // 128
        self.NJ = self.T // 128
        self.NT = self.T // 512
        self.SPAN = min(1024, self.T)
        self.NSPAN = self.T // self.SPAN
        self.E, self.H = 8, 256
        self.NFFT = 2 * SEQ
        self.QNZ = SEQ // 128
        self.QD = self.NFFT // 128
        self.KB = self.QD // 2
        self.PSTR = max(self.KB, 64)
        self.PACKE = 128 // self.PSTR
        self.NGRP = 128 // self.PACKE
        self.CPASS = 128
        self.NPASS = (D // 2) // self.CPASS

    def barrier_lite(self, tag='bar'):
        c, nc = self.c, self.nc
        for q in sorted(c.dma_queues - {'cw_ld', 'cr_ld'}):
            c.wait(nc.sync, q)
        sem = c.sem(tag)
        for e in self.engines:
            e.drain().then_inc(sem, 1)
        c.ctr[tag] += len(self.engines)
        for e in self.engines:
            e.wait_ge(sem, c.ctr[tag])

    def col(self, name, n=1):
        if name not in self.col_ix:
            o = self.col_used
            self.col_used += n
            assert self.col_used <= self.cols.shape[1]
            self.col_ix[name] = (o, n)
        o, n_ = self.col_ix[name]
        return self.cols[:, o:o + n_]

    def colb(self, name, n=1):
        if name not in self.colb_ix:
            o = self.colb_used
            self.colb_used += n
            assert self.colb_used <= self.colsb.shape[1]
            self.colb_ix[name] = (o, n)
        o, n_ = self.colb_ix[name]
        return self.colsb[:, o:o + n_]

    def xnc(self, ch, sl=None):
        a = self.xn_bf[:, ch * self.T:(ch + 1) * self.T]
        return a if sl is None else a[:, sl]

    def xac(self, ch, sl=None):
        a = self.xatt_bf[:, ch * self.T:(ch + 1) * self.T]
        return a if sl is None else a[:, sl]

    def cols_to_row(self, colap, rowap, ncols, pfx='', bf=False):
        c, nc = self.c, self.nc
        ps = self.pb[7]
        c.wait(nc.tensor, pfx + 'vv')
        c.mark(nc.tensor.transpose(ps[0:ncols, 0:128], colap, self.id_f[:, :]),
               pfx + 'mm')
        c.wait(nc.vector, pfx + 'mm')
        c.wait(nc.vector, pfx + 'ld')
        src = self.rowscr[0:ncols, 0:128]
        nc.vector.tensor_copy(src, ps[0:ncols, 0:128])
        if bf:
            nc.vector.drain()
            nc.vector.tensor_copy(self.rowscr_b[0:ncols, 0:128], src)
            src = self.rowscr_b[0:ncols, 0:128]
        c.mark(nc.vector.drain(), pfx + 'vv')
        c.wait(nc.sync, pfx + 'vv')
        c.dma(nc.sync, rowap, src, q=pfx + 'ld')
        c.wait(nc.sync, pfx + 'ld')

    def recip_nr(self, eng, out, in_, scratch):
        eng.reciprocal(out=out, in_=in_)
        eng.drain()

    def build(self, tail='block0'):
        s = self
        nc = bass.Bass("TRN2", target_bir_lowering=False, num_devices=s.NC)
        s.nc = nc
        s.stack = ExitStack()
        s.stack.__enter__()
        c = Ctx(nc, s.stack)
        s.c = c
        s.engines = [nc.sync, nc.vector, nc.scalar, nc.tensor, nc.gpsimd]
        T, D, DC = s.T, s.D, s.DC

        s.col_ix, s.col_used = {}, 0
        s.colb_ix, s.colb_used = {}, 0

        din = lambda n, sh: nc.dram_tensor(n, list(sh), F32, kind="ExternalInput").ap()
        s.x_in = din("x_in", [T, D])
        s.W_rkv = din("W_rkv", [D, 3 * D]); s.b_rkv = din("b_rkv", [3 * D])
        s.W_o = din("W_o", [D, D]); s.b_o = din("b_o", [D])
        s.nu_log = din("nu_log", [D]); s.gamma_log = din("gamma_log", [D])
        s.Wg = din("Wg", [8, D])
        s.ln_scale = din("ln_scale", [8, D]); s.ln_bias = din("ln_bias", [8, D])
        s.W1 = din("W1", [8, D, 256]); s.b1 = din("b1", [8, 256])
        s.W2 = din("W2", [8, 256, D]); s.b2 = din("b2", [8, D])
        s.l1w = din("l1w", [D, D]); s.l1b = din("l1b", [D])
        s.Dskip = din("Dskip", [1]); s.s4_step = din("s4_step", [1])
        s.s4_C_re = din("s4_C_re", [D, 64]); s.s4_C_im = din("s4_C_im", [D, 64])
        for t in ["B_re", "B_im", "P_re", "P_im", "Q_re", "Q_im",
                  "diag_re", "diag_im"]:
            setattr(s, "s4_" + t, din("s4_" + t, [64]))

        dram = lambda n, sh, dt=F32: nc.dram_tensor(n, sh, dt, kind="Internal").ap()
        shared = lambda n, sh, dt=F32: nc.dram_tensor(
            n, sh, dt, kind="Internal", addr_space="Shared").ap()
        s.wrkv_bf = dram("wrkv_bf", [D, 3 * D], BF16)
        s.wo_bf = dram("wo_bf", [D, D], BF16)
        s.w2_bf = dram("w2_bf", [8, 256, D], BF16)
        s.w1p = dram("w1p", [8, D, 256], BF16)
        s.xfeat = dram("xfeat", [D, T])
        s.rowd = dram("rowd", [8, T])
        s.rowd_b = dram("rowd_b", [2, T], BF16)
        s.rowtmp = dram("rowtmp", [2, 512])
        s.csrow_dram = dram("csrow_dram", [1, 3 * D], BF16)
        s.csw1_dram = dram("csw1_dram", [1, 8 * 256], BF16)
        s.wqt_dram = dram("wqt_dram", [D, D], BF16)
        s.kf_dram = dram("kf_dram", [128, s.NGRP, 2, D // 2], BF16)
        s.hfeat_dram = dram("hfeat_dram", [s.NPASS, 128, T], BF16)
        s.x1sh = shared("x1sh", [2, D, T])
        s.carrysh = shared("carrysh", [2, 128, 16])
        s.statsh = shared("statsh", [2, 2, T])
        s.hsh = shared("hsh", [2, s.NPASS, 128, T], BF16)
        s.y_out = nc.dram_tensor("y_out", [T, D], F32, kind="ExternalOutput").ap()

        s.ident_f = c.const("ident_f", np.eye(128, dtype=np.float32))
        s.id_f = c.sbuf("id_f", [128, 128])
        s.ident_b = c.const("ident_b", np.eye(128, dtype=BF))
        s.id_b = c.sbuf("id_b", [128, 128], BF16)
        s.ones_f = c.const("ones_f", np.ones((128, 128), np.float32))
        s.on_f = c.sbuf("on_f", [128, 128])
        s.ones_b = c.const("ones_b", np.ones((128, 128), dtype=BF))
        s.on_b = c.sbuf("on_b", [128, 128], BF16)
        oh = np.zeros((8, 8 * 128), dtype=BF)
        for e in range(8):
            oh[e, e * 128:(e + 1) * 128] = 1
        s.onehot_c = c.const("onehot", oh)
        s.onehot_sb = c.sbuf("onehot_sb", [8, 8 * 128], BF16)
        s.cvals = [1e-5, 1e-6, 0.0, 1.0, -1.0, 0.5]
        s.cval_const = c.const("cvals", np.tile(
            np.array(s.cvals, np.float32)[None, :], (128, 1)))
        s.cval_sb = c.sbuf("cval_sb", [128, len(s.cvals)])
        for _i, _v in enumerate(s.cvals):
            nc.const_aps.register(F32, _v, s.cval_sb[:, _i:_i + 1])

        def onr(p=0, n=128):
            return s.on_f[p:p + 1, 0:n]
        def onc(n=128):
            return s.on_f[0:n, 0:1]
        s.onr, s.onc = onr, onc
        def onrb(p=0, n=128):
            return s.on_b[p:p + 1, 0:n]
        def oncb(n=128):
            return s.on_b[0:n, 0:1]
        s.onrb, s.oncb = onrb, oncb

        s.XN_W = DC * T // 2
        s.R_W = 11264
        s.B1L_W = 7168
        s.AW = max(2 * s.XN_W + s.R_W, 32768 + s.B1L_W)
        s.arena = Arena(c, s.AW)
        A = s.arena
        s.O_R = 2 * s.XN_W
        s.xn_bf = A.bf16(0, 2 * s.XN_W)
        s.xatt_bf = A.bf16(s.XN_W, 2 * s.XN_W)

        def rf(off, n, partitions=128):
            assert off + n <= s.R_W, (off, n)
            return A.f32(s.O_R + off, n, partitions)
        def rb(off, n_elem, partitions=128):
            assert off + n_elem // 2 <= s.R_W, (off, n_elem)
            return A.bf16(s.O_R + off, n_elem, partitions)
        s.rf, s.rb = rf, rb

        s.rowscr = c.sbuf("rowscr", [128, 128])
        s.rowscr_b = c.sbuf("rowscr_b", [128, 128], BF16)
        s.stg_b = c.sbuf("stg_b", [1, 512], BF16)
        s.cols = c.sbuf("cols", [128, 768])
        s.colsb = c.sbuf("colsb", [128, 64], BF16)
        s.mask8 = c.sbuf("mask8", [8, T], BF16)
        s.cswg = c.sbuf("cswg", [1, 8])
        s.wgt_sb = c.sbuf("wgt_sb", [128, 8 * DC])
        s.b2_b = c.sbuf("b2_b", [8, D], BF16)
        s.pb = [c.psum(f"pb{i}") for i in range(8)]

        s.emit(tail)
        return nc, dict(c.consts)

    def emit(self, tail):
        s, c, nc = self, self.c, self.nc
        if True:
            self.emit_prologue()
            self.barrier_lite()
        if True:
            self.emit_stats()
            self.barrier_lite()
        if True:
            self.emit_rowmath()
            self.barrier_lite()
        if True:
            self.emit_xn_pass()
            self.barrier_lite()

        engines = self.engines
        idxs = [e.compute_val(e.partition_id() % 2) for e in engines]
        base = dict(c.ctr)
        base_lw = dict(c.last_wait)
        deltas = {}

        def run_arm(fn, tag):
            for k in list(c.ctr):
                c.ctr[k] = base.get(k, 0)
            c.last_wait.clear()
            c.last_wait.update(base_lw)
            fn()
            deltas[tag] = {k: v - base.get(k, 0) for k, v in c.ctr.items()
                           if v != base.get(k, 0)}
            for k in list(c.ctr):
                c.ctr[k] = base.get(k, 0)
            c.last_wait.clear()
            c.last_wait.update(base_lw)

        if True:
            for arm in nc.Switch(engines=engines, index=idxs, n=2):
                run_arm((lambda: self.emit_attn(0, True)) if arm == 0
                        else (lambda: self.emit_moe(1, True)),
                        'attn0' if arm == 0 else 'moe1')
            nc.all_core_barrier()
        if True:
            for arm in nc.Switch(engines=engines, index=idxs, n=2):
                run_arm((lambda: self.emit_moe(0, False)) if arm == 0
                        else (lambda: self.emit_attn(1, False)),
                        'moe0' if arm == 0 else 'attn1')

        skip = {'cw_ld', 'cr_ld'}
        da0 = {k: v for k, v in deltas['attn0'].items() if k not in skip}
        da1 = {k: v for k, v in deltas['attn1'].items() if k not in skip}
        assert da0 == da1, (da0, da1)
        assert deltas['moe0'] == deltas['moe1'], (deltas['moe0'], deltas['moe1'])
        allk = set(deltas['attn0']) | set(deltas['moe0']) | set(base)
        for k in allk:
            c.ctr[k] = base.get(k, 0) + deltas['attn0'].get(k, 0) + deltas['moe0'].get(k, 0)
            c.last_wait[k] = c.ctr[k]
        nc.all_core_barrier()

        if tail == 'block0':
            self.emit_debug_tail()
        else:
            tail(self)

    def emit_debug_tail(self):
        s, c, nc = self, self.c, self.nc
        dbg = nc.dram_tensor("dbg_x1", [s.D, s.T], F32, kind="ExternalOutput").ap()
        buf = s.rf(0, s.T)
        idxs = [nc.sync.compute_val(nc.sync.partition_id() % 2)]
        base = c.ctr.get('dbg', 0)
        delta = 0
        for arm in nc.Switch(engines=[nc.sync], index=idxs, n=2):
            c.ctr['dbg'] = base
            c.last_wait['dbg'] = base
            for ch in range(s.DC):
                c.dma(nc.sync, buf[:, :], s.x1sh[arm, ch * 128:(ch + 1) * 128, :], q='dbg')
                c.wait(nc.sync, 'dbg')
                c.dma(nc.sync, dbg[ch * 128:(ch + 1) * 128, :], buf[:, :], q='dbg')
                c.wait(nc.sync, 'dbg')
            delta = c.ctr['dbg'] - base
        c.ctr['dbg'] = base + delta
        c.last_wait['dbg'] = base + delta

    def emit_prologue(self):
        s, c, nc = self, self.c, self.nc
        D, T, DC = s.D, s.T, s.DC

        c.dma(nc.sync, s.id_f[:, :], s.ident_f[:, :])
        c.dma(nc.sync, s.id_b[:, :], s.ident_b[:, :])
        c.dma(nc.sync, s.on_f[:, :], s.ones_f[:, :])
        c.dma(nc.sync, s.on_b[:, :], s.ones_b[:, :])
        c.dma(nc.sync, s.onehot_sb[:, :], s.onehot_c[:, :])
        c.dma(nc.sync, s.cval_sb[:, :], s.cval_const[:, :])

        c.dma(nc.gpsimd, s.wrkv_bf[:, :], s.W_rkv[:, :], q='gd')
        c.dma(nc.gpsimd, s.wo_bf[:, :], s.W_o[:, :], q='gd')
        c.dma(nc.gpsimd, s.w2_bf[:, :, :], s.W2[:, :, :], q='gd')
        c.dma(nc.gpsimd, s.b2_b[:, :], s.b2[:, :], q='gd')

        nu_c = self.col("nu", 8); galog_c = self.col("galog", 8)
        s.lam_c = self.col("lam", 8); s.gam_c = self.col("gam", 8)
        nuv_c = self.col("nuv", 8)
        c.dma(nc.sync, nu_c, s.nu_log.rearrange("(c p) -> p c", p=128), slow=True)
        c.dma(nc.sync, galog_c, s.gamma_log.rearrange("(c p) -> p c", p=128), slow=True)
        s.brkv_c = self.col("brkv", 24)
        c.dma(nc.sync, s.brkv_c, s.b_rkv.rearrange("(c p) -> p c", p=128), slow=True)
        s.bo_c = self.col("bo", 8)
        c.dma(nc.sync, s.bo_c, s.b_o.rearrange("(c p) -> p c", p=128), slow=True)
        lnsc_c = self.col("lnsc", 64)
        lnbi_c = self.col("lnbi", 64)
        c.dma(nc.sync, lnsc_c, s.ln_scale.rearrange("e (c p) -> p (e c)", p=128), slow=True)
        c.dma(nc.sync, lnbi_c, s.ln_bias.rearrange("e (c p) -> p (e c)", p=128), slow=True)
        b1_c = self.col("b1c", 16)
        c.dma(nc.sync, b1_c, s.b1.rearrange("e (c p) -> p (e c)", p=128), slow=True)

        c.wait(nc.scalar, 'ld')
        nc.scalar.activation(nuv_c, nu_c, AF.Exp)
        nc.scalar.drain()
        nc.scalar.activation(s.lam_c, nuv_c, AF.Exp, scale=-1.0)
        c.mark(nc.scalar.drain(), 'ss')
        c.wait(nc.vector, 'ld')
        nc.vector.tensor_add(galog_c, galog_c, nu_c)
        c.mark(nc.vector.drain(), 'vv')
        c.wait(nc.scalar, 'vv')
        nc.scalar.activation(galog_c, galog_c, AF.Exp)
        c.mark(nc.scalar.drain(), 'ss')
        c.wait(nc.vector, 'ss')
        nc.vector.tensor_scalar_add(s.gam_c, galog_c, -1.0)
        nc.vector.drain()

        wg_sb = s.rf(0, D, 8)
        c.dma(nc.sync, wg_sb, s.Wg[:, :])
        c.wait(nc.tensor, 'ld')
        ps = s.pb[6]
        for kf in range(DC):
            c.mark(nc.tensor.transpose(ps[0:128, kf * 8:(kf + 1) * 8],
                                       wg_sb[0:8, kf * 128:(kf + 1) * 128],
                                       s.id_f[0:8, 0:8]), 'mm')
        c.wait(nc.vector, 'mm')
        nc.vector.tensor_copy(s.wgt_sb[:, :], ps[0:128, 0:8 * DC])
        c.mark(nc.vector.drain(), 'vv')
        ps2 = s.pb[5]
        c.wait(nc.tensor, 'vv')
        for kf in range(DC):
            m = nc.tensor.matmul(ps2[0:1, 0:8], s.onc(),
                                 s.wgt_sb[:, kf * 8:(kf + 1) * 8],
                                 start=(kf == 0), stop=(kf == DC - 1))
        c.mark(m, 'mm')
        c.wait(nc.vector, 'mm')
        nc.vector.tensor_copy(s.cswg[0:1, :], ps2[0:1, 0:8])
        c.mark(nc.vector.drain(), 'vv')

        self.barrier_lite()

        wbuf = [s.rb(0, 512), s.rb(256, 512)]
        ps3 = s.pb[4]
        for ob in range(3 * D // 512):
            c.wait(nc.tensor, 'vv')
            for kf in range(DC):
                b = wbuf[kf % 2]
                c.wait(nc.sync, 'mm')
                c.dma(nc.sync, b[:, :], s.wrkv_bf[kf * 128:(kf + 1) * 128,
                                                  ob * 512:(ob + 1) * 512])
                c.wait(nc.tensor, 'ld')
                m = nc.tensor.matmul(ps3[0:1, 0:512], s.oncb(), b[:, :],
                                     start=(kf == 0), stop=(kf == DC - 1))
                c.mark(m, 'mm')
            c.wait(nc.vector, 'mm')
            c.wait(nc.vector, 'ld')
            nc.vector.tensor_copy(s.stg_b[0:1, :], ps3[0:1, 0:512])
            c.mark(nc.vector.drain(), 'vv')
            c.wait(nc.sync, 'vv')
            c.dma(nc.sync, s.csrow_dram[0:1, ob * 512:(ob + 1) * 512],
                  s.stg_b[0:1, :])
        self.barrier_lite()

        w1buf2 = [s.rf(512, 256), s.rf(896, 256)]
        w1bufb2 = [s.rb(768, 256), s.rb(1152, 256)]
        s.beta_c = self.col("beta", 16)
        bia_cb = self.colb("lnbi_b", 64)
        nc.vector.tensor_copy(bia_cb, lnbi_c)
        c.mark(nc.vector.drain(), 'vv')
        c.wait(nc.tensor, 'vv')
        psb = [s.pb[3], s.pb[1]]
        psc = s.pb[2]
        for e in range(8):
            c.wait(nc.tensor, 'vv')
            for kf in range(DC):
                i = kf % 2
                c.wait_lag(nc.sync, 'vv', 1)
                c.dma(nc.sync, w1buf2[i][:, :], s.W1[e, kf * 128:(kf + 1) * 128, :])
                c.wait(nc.vector, 'ld')
                c.wait_lag(nc.vector, 'mm', 1)
                nc.vector.tensor_scalar(w1bufb2[i][:, :], w1buf2[i][:, :],
                                        lnsc_c[:, e * 8 + kf:e * 8 + kf + 1], None,
                                        op0=AL.mult)
                c.mark(nc.vector.drain(), 'vv')
                c.wait(nc.sync, 'vv')
                c.dma(nc.sync, s.w1p[e, kf * 128:(kf + 1) * 128, :], w1bufb2[i][:, :])
                c.wait(nc.tensor, 'vv')
                for hc in range(2):
                    nc.tensor.matmul(psb[hc][0:128, 0:1],
                                     w1bufb2[i][:, hc * 128:(hc + 1) * 128],
                                     bia_cb[:, e * 8 + kf:e * 8 + kf + 1],
                                     start=(kf == 0), stop=(kf == DC - 1))
                m = nc.tensor.matmul(psc[0:1, 0:256],
                                     s.oncb(), w1bufb2[i][:, :],
                                     start=(kf == 0), stop=(kf == DC - 1))
                c.mark(m, 'mm')
            c.wait(nc.vector, 'mm')
            c.wait(nc.vector, 'ld')
            nc.vector.tensor_copy(s.stg_b[0:1, 0:256], psc[0:1, 0:256])
            c.mark(nc.vector.drain(), 'vv')
            c.wait(nc.sync, 'vv')
            c.dma(nc.sync, s.csw1_dram[0:1, e * 256:(e + 1) * 256],
                  s.stg_b[0:1, 0:256])
            for hc in range(2):
                nc.vector.tensor_add(s.beta_c[:, e * 2 + hc:e * 2 + hc + 1],
                                     psb[hc][0:128, 0:1],
                                     b1_c[:, e * 2 + hc:e * 2 + hc + 1])
            c.mark(nc.vector.drain(), 'vv')
            c.wait(nc.sync, 'vv')
            c.wait(nc.tensor, 'vv')

    def emit_stats(self):
        s, c, nc = self, self.c, self.nc
        NJ, D = s.NJ, s.D
        s.mc = self.col("mc", NJ)
        s.qc = self.col("qc", NJ)
        xt = [s.rf(0, 1024), s.rf(1024, 1024)]
        sq = s.rf(2048, 1024)
        for j in range(NJ):
            b = xt[j % 2]
            if j >= 2:
                c.wait(nc.sync, 'ss', c.ctr['ss'])
                c.wait(nc.sync, 'vv', c.ctr['vv'])
            c.dma(nc.sync, b[:, :], s.x_in[j * 128:(j + 1) * 128, :])
            c.wait(nc.scalar, 'ld')
            nc.scalar.activation(sq[:, :], b[:, :], AF.Square,
                                 accum_out=s.qc[:, j:j + 1])
            c.mark(nc.scalar.drain(), 'ss')
            c.wait(nc.vector, 'ld')
            nc.vector.reduce_sum(s.mc[:, j:j + 1], b[:, :], axis=AX.X)
            c.mark(nc.vector.drain(), 'vv')

    def emit_rowmath(self):
        s, c, nc = self, self.c, self.nc
        NJ, D, T = s.NJ, s.D, s.T
        col = self.col
        m = col("m", NJ); v = col("v", NJ); sc = col("sc", NJ)
        s2c = col("s2c", NJ); t0 = col("t0", NJ); t1 = col("t1", NJ)
        V, S = nc.vector, nc.scalar
        V.tensor_scalar(m, s.mc, 1.0 / D, None, op0=AL.mult)
        V.drain()
        V.tensor_scalar(t0, s.qc, 1.0 / D, None, op0=AL.mult)
        V.drain()
        V.tensor_mul(t1, m, m)
        V.drain()
        V.tensor_sub(v, t0, t1)
        c.mark(V.drain(), 'vv')
        c.wait(S, 'vv')
        S.activation(t0, v, AF.Sqrt, bias=float(LN_EPS))
        c.mark(S.drain(), 'ss')
        c.wait(V, 'ss')
        self.recip_nr(V, sc, t0, t1)
        V.tensor_mul(t0, sc, sc)
        V.drain()
        V.tensor_mul(t0, t0, v)
        c.mark(V.drain(), 'vv')
        c.wait(S, 'vv')
        S.activation(t0, t0, AF.Sqrt, bias=float(LN_EPS))
        c.mark(S.drain(), 'ss')
        c.wait(V, 'ss')
        self.recip_nr(V, s2c, t0, t1)
        nms = col("nms", NJ); nmss2 = col("nmss2", NJ); nm = col("nm", NJ)
        ss2 = col("ss2", NJ)
        V.tensor_mul(nms, m, sc)
        V.drain()
        V.tensor_scalar(nms, nms, -1.0, None, op0=AL.mult)
        V.drain()
        V.tensor_mul(ss2, sc, s2c)
        V.drain()
        V.tensor_mul(nmss2, m, ss2)
        V.drain()
        V.tensor_scalar(nmss2, nmss2, -1.0, None, op0=AL.mult)
        V.drain()
        V.tensor_scalar(nm, m, -1.0, None, op0=AL.mult)
        c.mark(V.drain(), 'vv')
        self.barrier_lite()
        rr = lambda i: s.rowd[i].rearrange("(j p) -> j p", p=128)
        rrb = lambda i: s.rowd_b[i].rearrange("(j p) -> j p", p=128)
        self.cols_to_row(sc, rr(R_S), NJ)
        self.cols_to_row(s2c, rr(R_S2), NJ)
        self.cols_to_row(nm, rr(R_NEGM), NJ)
        self.cols_to_row(nms, rrb(B_NEGMS), NJ, bf=True)
        self.cols_to_row(nmss2, rrb(B_NEGMSS2), NJ, bf=True)

    def emit_xn_pass(self):
        s, c, nc = self, self.c, self.nc
        NJ, D, DC, T = s.NJ, s.D, s.DC, s.T
        V, G = nc.vector, nc.gpsimd
        xt = [s.rf(0, 1024), s.rf(1024, 1024)]
        sB = s.rf(2048, 128)
        xf = [s.rf(2176 + i * 128, 128) for i in range(4)]
        sseg = s.rf(2688, 128, 1)
        nseg = s.rf(2816, 128, 1)
        lg8 = s.rf(2944, 128, 8)
        t8 = s.rf(3072, 8)
        maxc = s.rf(3080, 1)
        psT = [s.pb[i] for i in range(4)]
        psS = s.pb[4]
        psL = s.pb[5]
        c.sem('gg')
        for j in range(NJ):
            b = xt[j % 2]
            c.dma(nc.sync, b[:, :], s.x_in[j * 128:(j + 1) * 128, :])
            c.dma(nc.sync, sseg, s.rowd[R_S, j * 128:(j + 1) * 128][None, :])
            c.dma(nc.sync, nseg, s.rowd[R_NEGM, j * 128:(j + 1) * 128][None, :])
            c.wait(nc.tensor, 'vv')
            c.wait(nc.tensor, 'ld')
            c.mark(nc.tensor.matmul(psS[0:128, 0:128], s.onr(), sseg,
                                    start=True, stop=True), 'mm')
            c.wait(nc.vector, 'mm')
            nc.vector.tensor_copy(sB[:, :], psS[0:128, 0:128])
            c.mark(nc.vector.drain(), 'vv')
            for ch in range(DC):
                pt = psT[ch % 4]
                c.wait(nc.tensor, 'vv')
                c.wait(nc.tensor, 'ss')
                c.mark(nc.tensor.transpose(pt[0:128, 0:128],
                                           b[:, ch * 128:(ch + 1) * 128],
                                           s.id_f[:, :]), 'mm')
                c.wait(nc.vector, 'mm')
                nc.vector.tensor_mul(s.xnc(ch)[:, j * 128:(j + 1) * 128],
                                     pt[0:128, 0:128], sB[:, :])
                c.mark(nc.vector.drain(), 'vv')
                c.wait(nc.scalar, 'mm')
                c.wait(nc.scalar, 'ld')
                xfb = xf[ch % 4]
                nc.scalar.copy(xfb[:, :], pt[0:128, 0:128])
                c.mark(nc.scalar.drain(), 'ss')
                c.wait(nc.sync, 'ss')
                c.dma(nc.sync, s.xfeat[ch * 128:(ch + 1) * 128,
                                       j * 128:(j + 1) * 128], xfb[:, :])
                c.wait(nc.tensor, 'ss')
                mmi = nc.tensor.matmul(psL[0:8, 0:128], s.wgt_sb[:, ch * 8:(ch + 1) * 8],
                                       xfb[:, :], start=(ch == 0), stop=False)
            mmi = nc.tensor.matmul(psL[0:8, 0:128], s.cswg[0:1, :],
                                   nseg, start=False, stop=True)
            c.mark(mmi, 'mm')
            c.wait(V, 'mm')
            V.tensor_copy(lg8[:, :], psL[0:8, 0:128])
            c.mark(V.drain(), 'vv')
            c.wait(nc.tensor, 'vv')
            c.mark(nc.tensor.transpose(psS[0:128, 0:8], lg8[:, :],
                                       s.id_f[0:8, 0:8]), 'mm')
            c.wait(V, 'mm')
            V.tensor_reduce(maxc, psS[0:128, 0:8], axis=AX.X, op=AL.max)
            V.drain()
            V.tensor_scalar(t8, psS[0:128, 0:8], maxc, None, op0=AL.is_ge)
            c.mark(V.drain(), 'vv')
            c.wait(nc.tensor, 'vv')
            c.mark(nc.tensor.transpose(psS[0:8, 0:128], t8,
                                       s.id_f[:, :]), 'mm')
            c.wait(V, 'mm')
            V.tensor_copy(s.mask8[:, j * 128:(j + 1) * 128], psS[0:8, 0:128])
            c.mark(V.drain(), 'vv')
            c.wait(nc.sync, 'vv')

    def emit_attn(self, parity, first):
        s, c, nc = self, self.c, self.nc
        T, D, DC, NT, SPAN, NSPAN = s.T, s.D, s.DC, s.NT, s.SPAN, s.NSPAN
        V, S = nc.vector, nc.scalar
        NTS = SPAN // 512 if SPAN >= 512 else 1
        SP2 = SPAN // 2

        o = 0
        def nx(n):
            nonlocal o
            r = o; o += n; return r
        wbuf = s.rb(nx(1536), 3072)
        csr = s.rb(nx(1536), 3 * D, 1)
        negms = s.rb(nx(T // 2), T, 1)
        kbf = s.rb(nx(SP2), SPAN)
        kvbf = s.rb(nx(SP2), SPAN)
        u1bf = s.rb(nx(SP2), SPAN)
        rbf = s.rb(nx(SP2), SPAN)
        u2f = s.rf(nx(SPAN), SPAN)
        rf1 = s.rf(nx(SPAN), SPAN)
        wob = s.rb(nx(512), 1024)
        xftile = s.rf(nx(512), 512)
        x1t = s.rf(nx(512), 512)
        cgat = s.rf(nx(16), 16)
        ctile = s.rf(nx(16), 16)
        cc = s.rf(nx(2), 2)
        assert o <= s.R_W, o

        c.dma(nc.sync, csr, s.csrow_dram[0:1, :], q='a_ld')
        c.dma(nc.sync, negms, s.rowd_b[B_NEGMS:B_NEGMS + 1, :], q='a_ld')
        if parity == 1:
            c.dma(nc.sync, ctile[:, :], s.carrysh[0, :, :], q='cr_ld')
            c.wait(nc.vector, 'cr_ld', c.ctr['cr_ld'])

        psRsets = [[s.pb[0], s.pb[1], s.pb[2]], [s.pb[3], s.pb[4], s.pb[5]]]

        til = 0
        for ch in range(DC):
            for oi, oc in enumerate([ch, 8 + ch, 16 + ch]):
                c.dma(nc.sync,
                      wbuf[:, oi * 8 * 128:(oi + 1) * 8 * 128].rearrange(
                          "p (kf o) -> p kf o", o=128),
                      s.wrkv_bf[:, oc * 128:(oc + 1) * 128].rearrange(
                          "(kf p) o -> p kf o", p=128), q='a_ld')
            for half in range(NSPAN):
                for ntl in range(NTS):
                    t0 = half * SPAN + ntl * 512
                    sl_ps = slice(0, 512)
                    sl_sp = slice(ntl * 512, ntl * 512 + 512)
                    psR = psRsets[til % 2]
                    til += 1
                    c.wait(nc.tensor, 'a_ld')
                    c.wait_lag(nc.tensor, 'a_vv', 1)
                    for oi in range(3):
                        for kf in range(DC):
                            mmi = nc.tensor.matmul(
                                psR[oi][:, sl_ps],
                                wbuf[:, (oi * 8 + kf) * 128:(oi * 8 + kf + 1) * 128],
                                s.xnc(kf)[:, t0:t0 + 512],
                                start=(kf == 0), stop=False)
                        mmi = nc.tensor.matmul(
                            psR[oi][:, sl_ps],
                            csr[0:1, (oi * 8 + ch) * 128:(oi * 8 + ch + 1) * 128],
                            negms[0:1, t0:t0 + 512],
                            start=False, stop=True)
                    c.mark(mmi, 'a_mm')
                    c.wait(S, 'a_mm')
                    c.wait(S, 'a_vv')
                    S.activation(rbf[:, sl_sp], psR[0][:, sl_ps], AF.Sigmoid,
                                 bias=s.brkv_c[:, ch:ch + 1])
                    S.drain()
                    S.activation(kbf[:, sl_sp], psR[1][:, sl_ps], AF.Exp,
                                 bias=s.brkv_c[:, 8 + ch:8 + ch + 1])
                    c.mark(S.drain(), 'a_ss')
                    c.wait(V, 'a_ss')
                    V.scalar_tensor_tensor(kvbf[:, sl_sp], psR[2][:, sl_ps],
                                           s.brkv_c[:, 16 + ch:16 + ch + 1],
                                           kbf[:, sl_sp], op0=AL.add, op1=AL.mult)
                    c.mark(V.drain(), 'a_vv')
                if half == 0:
                    if parity == 0:
                        init_kv, init_k = 0.0, 0.0
                    else:
                        init_kv = ctile[:, ch:ch + 1]
                        init_k = ctile[:, 8 + ch:8 + ch + 1]
                else:
                    init_kv = cc[:, 0:1]
                    init_k = cc[:, 1:2]
                lamb = s.lam_c[:, ch:ch + 1].broadcast_to([128, SPAN])
                V.tensor_tensor_scan(u1bf[:, :], lamb, kvbf[:, :], init_kv,
                                     op0=AL.mult, op1=AL.add)
                V.drain()
                V.tensor_tensor_scan(u2f[:, :], lamb, kbf[:, :], init_k,
                                     op0=AL.mult, op1=AL.add)
                V.drain()
                if half < NSPAN - 1:
                    V.tensor_copy(cc[:, 0:1], u1bf[:, SPAN - 1:SPAN])
                    V.tensor_copy(cc[:, 1:2], u2f[:, SPAN - 1:SPAN])
                    V.drain()
                if parity == 0 and half == NSPAN - 1:
                    V.tensor_copy(cgat[:, ch:ch + 1], u1bf[:, SPAN - 1:SPAN])
                    V.tensor_copy(cgat[:, 8 + ch:8 + ch + 1], u2f[:, SPAN - 1:SPAN])
                    V.drain()
                V.scalar_tensor_tensor(u1bf[:, :], kvbf[:, :],
                                       s.gam_c[:, ch:ch + 1], u1bf[:, :],
                                       op0=AL.mult, op1=AL.add)
                V.drain()
                V.scalar_tensor_tensor(u2f[:, :], kbf[:, :],
                                       s.gam_c[:, ch:ch + 1], u2f[:, :],
                                       op0=AL.mult, op1=AL.add)
                V.drain()
                V.tensor_scalar_add(u2f[:, :], u2f[:, :], 1e-6)
                V.drain()
                V.reciprocal(out=rf1[:, :], in_=u2f[:, :])
                V.drain()
                V.tensor_mul(u1bf[:, :], u1bf[:, :], rf1[:, :])
                V.drain()
                sl = slice(half * SPAN, (half + 1) * SPAN)
                V.tensor_mul(s.xac(ch)[:, sl], u1bf[:, :], rbf[:, :])
                c.mark(V.drain(), 'a_vv')
            c.wait(nc.sync, 'a_mm')

        if parity == 0:
            c.wait(nc.sync, 'a_vv')
            c.dma(nc.sync, s.carrysh[0, :, :], cgat[:, :], q='cw_ld')
            c.wait(nc.sync, 'cw_ld', c.ctr['cw_ld'])

        self.barrier_lite('a_bar')

        psW = s.pb[3]
        for dc in range(DC):
            c.dma(nc.sync, wob[:, :].rearrange("p (ku o) -> p ku o", o=128),
                  s.wo_bf[:, dc * 128:(dc + 1) * 128].rearrange(
                      "(ku p) o -> p ku o", p=128), q='a_ld')
            for nt in range(NT):
                src_ap = (s.xfeat[dc * 128:(dc + 1) * 128, nt * 512:(nt + 1) * 512]
                          if first else
                          s.x1sh[parity, dc * 128:(dc + 1) * 128, nt * 512:(nt + 1) * 512])
                c.dma(nc.sync, xftile[:, :], src_ap, q='a_ld')
                c.wait(nc.tensor, 'a_ld')
                c.wait(nc.tensor, 'a_vv')
                for ku in range(DC):
                    mmi = nc.tensor.matmul(psW[:, 0:512],
                                           wob[:, ku * 128:(ku + 1) * 128],
                                           s.xac(ku)[:, nt * 512:(nt + 1) * 512],
                                           start=(ku == 0), stop=(ku == DC - 1))
                c.mark(mmi, 'a_mm')
                c.wait(V, 'a_mm')
                c.wait(V, 'a_ld')
                V.scalar_tensor_tensor(x1t[:, :], psW[:, 0:512],
                                       s.bo_c[:, dc:dc + 1], xftile[:, :],
                                       op0=AL.add, op1=AL.add)
                c.mark(V.drain(), 'a_vv')
                c.wait(nc.sync, 'a_vv')
                c.dma(nc.sync, s.x1sh[parity, dc * 128:(dc + 1) * 128,
                                      nt * 512:(nt + 1) * 512], x1t[:, :], q='a_ld')
        self.barrier_lite('a_bar')

    def emit_moe(self, parity, first):
        s, c, nc = self, self.c, self.nc
        T, D, DC, NT = s.T, s.D, s.DC, s.NT
        V, S = nc.vector, nc.scalar

        o = 0
        def nx(n):
            nonlocal o
            r = o; o += n; return r
        xhat = s.rb(nx(DC * 256), DC * 512)
        w1b = s.rb(nx(1024), 2048)
        w2b = s.rb(nx(1024), 2048)
        mh = s.rb(nx(4096), 8192)
        s2B = s.rf(nx(512), 512)
        mskB = s.rb(nx(256), 512)
        x1t = s.rf(nx(512), 512)
        csw1 = s.rb(nx(1024), 2048, 1)
        s2seg = s.rf(nx(512), 512, 1)
        ns2seg = s.rb(nx(256), 512, 1)
        assert o <= s.R_W

        c.dma(nc.sync, csw1, s.csw1_dram[0:1, :], q='m_ld')

        psB = s.pb[0]
        psH = s.pb[1]
        psH2 = s.pb[2]
        psO = s.pb[3]

        for nt in range(NT):
            tsl = slice(nt * 512, (nt + 1) * 512)
            c.wait(nc.tensor, 'm_ld')
            c.dma(nc.sync, s2seg, s.rowd[R_S2, tsl][None, :], q='m_ld')
            c.dma(nc.sync, ns2seg, s.rowd_b[B_NEGMSS2, tsl][None, :], q='m_ld')
            c.wait(nc.tensor, 'm_vv')
            c.wait(nc.tensor, 'm_ld')
            c.mark(nc.tensor.matmul(psB[:, 0:512], s.onr(), s2seg,
                                    start=True, stop=True), 'm_mm')
            c.wait(V, 'm_mm')
            V.tensor_copy(s2B[:, :], psB[:, 0:512])
            c.mark(V.drain(), 'm_vv')
            c.wait(V, 'm_vv')
            for ch in range(DC):
                V.tensor_mul(xhat[:, ch * 512:(ch + 1) * 512],
                             s.xnc(ch)[:, tsl], s2B[:, :])
            c.mark(V.drain(), 'm_vv')
            def w1_load(e, hh):
                c.dma(nc.sync,
                      w1b[:, hh * 1024:(hh + 1) * 1024].rearrange(
                          "p (kf h) -> p kf h", h=256),
                      s.w1p[e, hh * 512:(hh + 1) * 512].rearrange(
                          "(kf p) h -> p kf h", p=128), q='m_ld')
            c.wait(nc.sync, 'm_mm')
            w1_load(0, 0)
            for e in range(8):
                c.wait(nc.tensor, 'm_vv')
                c.mark(nc.tensor.matmul(psB[:, 0:512],
                                        s.onehot_sb[:, e * 128:(e + 1) * 128],
                                        s.mask8[:, tsl],
                                        start=True, stop=True), 'm_mm')
                c.wait(V, 'm_mm')
                V.tensor_copy(mskB[:, :], psB[:, 0:512])
                c.mark(V.drain(), 'm_vv')
                for hh in range(2):
                    c.wait(nc.tensor, 'm_ld')
                    c.wait(nc.sync, 'm_mm')
                    if hh == 0:
                        w1_load(e, 1)
                    elif e < 7:
                        w1_load(e + 1, 0)
                    last = None
                    for kf in range(hh * 4, hh * 4 + 4):
                        for hc, psh in ((0, psH), (1, psH2)):
                            last = nc.tensor.matmul(
                                psh[:, 0:512],
                                w1b[:, (kf * 2 + hc) * 128:(kf * 2 + hc + 1) * 128],
                                xhat[:, kf * 512:(kf + 1) * 512],
                                start=(kf == 0), stop=False)
                    c.mark(last, 'm_mm')
                for hc, psh in ((0, psH), (1, psH2)):
                    mmi = nc.tensor.matmul(
                        psh[:, 0:512],
                        csw1[0:1, e * 256 + hc * 128:e * 256 + hc * 128 + 128],
                        ns2seg, start=False, stop=True)
                    c.mark(mmi, 'm_mm')
                    c.wait(S, 'm_mm')
                    S.activation(mh[:, (e * 2 + hc) * 512:(e * 2 + hc + 1) * 512],
                                 psh[:, 0:512], AF.Gelu,
                                 bias=s.beta_c[:, e * 2 + hc:e * 2 + hc + 1])
                    c.mark(S.drain(), 'm_ss')
                c.wait(V, 'm_ss')
                for hc in range(2):
                    sl = slice((e * 2 + hc) * 512, (e * 2 + hc + 1) * 512)
                    V.tensor_mul(mh[:, sl], mh[:, sl], mskB[:, :])
                c.mark(V.drain(), 'm_vv')
                c.wait(nc.tensor, 'm_vv')
            def w2_load(dc_, hh):
                c.dma(nc.sync,
                      w2b[:, hh * 1024:(hh + 1) * 1024].rearrange(
                          "p (e hc o) -> p e hc o", e=4, hc=2),
                      s.w2_bf[hh * 4:(hh + 1) * 4, :,
                              dc_ * 128:(dc_ + 1) * 128].rearrange(
                          "e (hc p) o -> p e hc o", p=128), q='m_ld')
            c.wait(nc.sync, 'm_mm')
            w2_load(0, 0)
            for dc in range(DC):
                rsrc = (s.xfeat[dc * 128:(dc + 1) * 128, tsl] if first else
                        s.x1sh[parity, dc * 128:(dc + 1) * 128, tsl])
                c.wait(nc.sync, 'm_ld')
                c.dma(nc.sync, x1t[:, :], rsrc, q='m_ld')
                c.wait(nc.tensor, 'm_vv')
                for hh in range(2):
                    c.wait(nc.tensor, 'm_ld')
                    c.wait(nc.sync, 'm_mm')
                    if hh == 0:
                        w2_load(dc, 1)
                    elif dc < DC - 1:
                        w2_load(dc + 1, 0)
                    last = None
                    for ei in range(hh * 8, hh * 8 + 8):
                        last = nc.tensor.matmul(psO[:, 0:512],
                                                w2b[:, ei * 128:(ei + 1) * 128],
                                                mh[:, ei * 512:(ei + 1) * 512],
                                                start=(ei == 0), stop=False)
                    c.mark(last, 'm_mm')
                mmi = nc.tensor.matmul(psO[:, 0:512],
                                       s.b2_b[:, dc * 128:(dc + 1) * 128],
                                       s.mask8[:, tsl], start=False, stop=True)
                c.mark(mmi, 'm_mm')
                c.wait(V, 'm_mm')
                c.wait(V, 'm_ld')
                V.tensor_add(x1t[:, :], psO[:, 0:512], x1t[:, :])
                c.mark(V.drain(), 'm_vv')
                c.wait(nc.sync, 'm_vv')
                c.dma(nc.sync, s.x1sh[parity, dc * 128:(dc + 1) * 128, tsl],
                      x1t[:, :], q='m_ld')
                c.wait(nc.tensor, 'm_vv')
        self.barrier_lite('m_bar')

"""Block 1 emitters: L1 stats, w_q prep, S4 kernel chain, FFT conv, epilogue.

Emitted as the tail of mk0.MKBuild. All big buffers are overlaid views into
s.arena CORE [0, 32768) plus a locals region [32768, +B1L_W); per-token rows
live in DRAM (rowd / statsh) and are staged as [1 x 512] strips.

Conventions (full-size values in brackets):
  QNZ = SEQ/128 [64]   slow time digit (nonzero range), also inverse q range
  KB  = NFFT/256 [64]  k2 range; k = k1 + 128*k2, k1 in [0,128), k2 in [0,KB)
  PACKE = 128//PSTR [2] k1 values packed per 128 partitions, PSTR=max(KB,32)
  CPASS [128]          conv channels per pass; NPASS = (D/2)/CPASS [4]
X layout: [128 x (NGRP x 2comp x CPASS)] bf16, k1 at partition offset
  (k1%PACKE)*PSTR, column group g=k1//PACKE, comp-major then channel.
Y layout: [128p x (ch x 256)] bf16 (per ch: k1re 0:128 | k1im 128:256).
G layout: [128ch x (k1 x 2comp x 128p)] bf16 -> GT[k1 x (p x ch)] via transpose.
"""
import numpy as np
import ml_dtypes
import concourse.bass as bass
import concourse.mybir as mybir
import concourse.bass_isa as bass_isa
from concourse import dve_ops


LN_EPS = 1e-5
R_S, R_NEGM, R_S2, R_Q, R_NMQ, R_XSC, R_DS, R_NDMS = range(8)


def mk1_constants(s, c):
    SEQ, NFFT = s.SEQ, 2 * s.SEQ
    QNZ = SEQ // 128
    QD = NFFT // 128
    KB = QD // 2
    QDs = SEQ // 128
    k1v = np.arange(128)
    pv = np.arange(128)
    qv = np.arange(QNZ)
    k2v = np.arange(KB)

    w1 = np.exp(-2j * np.pi * np.outer(qv, k1v % QD) / QD)
    cst = {}
    cst['f_s1'] = np.concatenate([w1.real, w1.imag], axis=1).astype(BF)

    m_all = (np.exp(-2j * np.pi * np.outer(pv, k1v) / NFFT)[:, :, None]
             * np.exp(-2j * np.pi * np.outer(pv, k2v) / QD)[:, None, :])
    _mre = np.transpose(m_all.real, (1, 0, 2))
    _mim = np.transpose(m_all.imag, (1, 0, 2))
    cst['f_mall'] = np.concatenate([_mre, _mim, -_mim], axis=2).astype(BF)
    cst['f_mre'] = _mre.astype(BF)
    cst['f_mim'] = _mim.astype(BF)
    cst['f_mnim'] = (-_mim).astype(BF)

    ia = (np.exp(2j * np.pi * np.outer(k2v, pv) / QD)[None, :, :]
          * np.exp(2j * np.pi * k1v[:, None, None] * pv[None, None, :] / NFFT))
    ia = ia * (2.0 / NFFT)
    ia[0, 0, :] *= 0.5
    PSTR = max(KB, 64)
    PACKE = 128 // PSTR

    def dup(a):
        out = np.zeros((a.shape[0], 128, a.shape[2]), np.float64)
        for g in range(PACKE):
            out[:, g * PSTR:g * PSTR + KB, :] = a
        return out
    cst['i_all'] = np.concatenate(
        [dup(ia.real), dup(ia.imag), dup(-ia.imag)], axis=2).astype(BF)
    cst['i_re'] = dup(ia.real).astype(BF)
    cst['i_im'] = dup(ia.imag).astype(BF)
    cst['i_nim'] = dup(-ia.imag).astype(BF)

    wb = np.exp(2j * np.pi * np.outer(k1v, np.arange(QNZ)) / QD)
    cst['b_re'] = wb.real.astype(BF)
    cst['b_im'] = (-wb.imag).astype(BF)
    cst['sign_col'] = (((-1.0) ** pv)[:, None]).astype(BF)
    cst['sign_row_n'] = ((((-1.0) ** pv) / NFFT)[None, :]).astype(BF)

    lv = np.arange(SEQ)
    ang = (-2 * np.pi * lv / SEQ).astype(np.float32)
    zl = (np.cos(ang) + 1j * np.sin(ang)).astype(np.complex64)
    cst['s4_timag'] = ((1 - zl) / (1 + zl)).imag.astype(np.float32)[None, :]
    cst['s4_i1z_re'] = (2.0 / (1 + zl)).real.astype(np.float32)[None, :]
    cst['s4_i1z_im'] = (2.0 / (1 + zl)).imag.astype(np.float32)[None, :]

    lqv = np.arange(QDs)
    t1v = np.arange(128)
    s4w1 = np.exp(2j * np.pi * np.outer(lqv, t1v % QDs) / QDs)
    cst['s4_s1a'] = np.concatenate([s4w1.real, s4w1.imag], axis=1).astype(BF)
    cst['s4_s1b'] = np.concatenate([-s4w1.imag, s4w1.real], axis=1).astype(BF)
    t2v = np.arange(QDs)
    m2 = (np.exp(2j * np.pi * np.outer(pv, t1v) / SEQ)[:, :, None]
          * np.exp(2j * np.pi * np.outer(pv, t2v) / QDs)[:, None, :]) / SEQ
    _gre = np.transpose(m2.real, (1, 0, 2))
    _gim = np.transpose(m2.imag, (1, 0, 2))
    cst['s4_gall'] = np.concatenate([_gre, _gim, -_gim], axis=2).astype(BF)
    cst['s4_g_re'] = _gre.astype(BF)
    cst['s4_g_im'] = _gim.astype(BF)
    cst['s4_g_nim'] = (-_gim).astype(BF)

    aps = {}
    for k, v in cst.items():
        aps[k] = c.const(k, v)
    return aps


def mk1_statics(s):
    c = s.c
    QDs = s.SEQ // 128
    s.fft_s1_sb = c.sbuf("fft_s1_sb", [QDs, 256], BF16)
    s.s4_s1a_sb = c.sbuf("s4_s1a_sb", [QDs, 256], BF16)
    s.s4_s1b_sb = c.sbuf("s4_s1b_sb", [QDs, 256], BF16)
    s.sgncol_sb = c.sbuf("sgncol_sb", [128, 1], BF16)
    s.sgnrow_sb = c.sbuf("sgnrow_sb", [1, 128], BF16)
    s.fftb_re_sb = c.sbuf("fftb_re_sb", [128, QDs], BF16)
    s.fftb_im_sb = c.sbuf("fftb_im_sb", [128, QDs], BF16)
    s.kfny_rowb = c.sbuf("kfny_rowb", [1, s.D // 2], BF16)
    s._fwd_mbuf = [c.sbuf(f"fwd_mbuf{i}", [128, 3 * s.KB], BF16) for i in range(2)]
    s._cv_iab = [c.sbuf(f"cv_ia{i}", [128, 3 * 128], BF16) for i in range(2)]
    s.sny_sb = c.sbuf("sny_sb", [1, s.D // 2])
    s.snyb_sb = c.sbuf("snyb_sb", [1, s.D // 2], BF16)
    s.pnyb_sb = c.sbuf("pnyb_sb", [1, s.D // 2], BF16)
    s.gfnyc_sb = c.sbuf("gfnyc_sb", [128, 1], BF16)


def emit_block1(s):
    c, nc = s.c, s.nc
    s.cst = mk1_constants(s, c)
    mk1_statics(s)

    engines = s.engines
    idxs = [e.compute_val(e.partition_id() % 2) for e in engines]

    def switched(fn_by_parity, tags):
        base = dict(c.ctr)
        base_lw = dict(c.last_wait)
        deltas = {}
        for arm in nc.Switch(engines=engines, index=idxs, n=2):
            for k in list(c.ctr):
                c.ctr[k] = base.get(k, 0)
            c.last_wait.clear(); c.last_wait.update(base_lw)
            fn_by_parity[arm]()
            deltas[arm] = {k: v - base.get(k, 0) for k, v in c.ctr.items()
                           if v != base.get(k, 0)}
            for k in list(c.ctr):
                c.ctr[k] = base.get(k, 0)
            c.last_wait.clear(); c.last_wait.update(base_lw)
        assert deltas.get(0, {}) == deltas.get(1, {}), (tags, deltas)
        for k in set(deltas.get(0, {})) | set(base):
            c.ctr[k] = base.get(k, 0) + deltas.get(0, {}).get(k, 0)
            c.last_wait[k] = c.ctr[k]

    if True:
        switched({0: lambda: b1_stats(s, 0), 1: lambda: b1_stats(s, 1)}, 'stats')
        nc.all_core_barrier()
    if True:
        switched({0: lambda: b1_conv(s, 0), 1: lambda: b1_conv(s, 1)}, 'conv')
        nc.all_core_barrier()
    if True:
        switched({0: lambda: b1_final(s, 0), 1: lambda: b1_final(s, 1)}, 'final')


def b1_stats(s, par):
    c, nc = s.c, s.nc
    T, D, DC, NT, NJ, SEQ = s.T, s.D, s.DC, s.NT, s.NJ, s.SEQ
    V, S, G = nc.vector, nc.scalar, nc.gpsimd
    A = s.arena
    Q = 'q_'

    xt = [A.f32(0, 512), A.f32(512, 512)]
    sq = A.f32(1024, 512)
    rmx = A.f32(1536, 512)
    rmn = A.f32(2048, 512)
    neg = A.f32(2560, 512)
    strip = [A.f32(3072, 512, 1), A.f32(3584, 512, 1)]
    coltile = A.f32(4096, 16 * NJ)
    W0 = 4096 + 16 * NJ
    wt = A.f32(W0, 1024)
    wq16 = A.bf16(W0 + 1024, 1024)
    wqt_t = A.bf16(W0 + 1536, 128)
    sqw = A.f32(W0 + 1600, 1024)
    rndf = A.f32(W0 + 2624, 1024)
    dskq = A.f32(W0 + 3648, 32, 1)

    def cq(i):
        return coltile[:, i * NJ:(i + 1) * NJ]
    SX, QX, MX, MN, M1, S1, T0, T1, QQ, NMQ, XSC, DS, NDMS, NMS1, DSKC = range(15)

    psA = s.pb[0]
    psB = s.pb[1]
    for nt in range(NT):
        tsl = slice(nt * 512, (nt + 1) * 512)
        for ch in range(DC):
            b = xt[ch % 2]
            c.wait(nc.sync, Q + 'ss')
            c.wait(nc.sync, Q + 'vv')
            c.dma(nc.sync, b[:, :], s.x1sh[par, ch * 128:(ch + 1) * 128, tsl],
                  q=Q + 'ld')
            c.wait(S, Q + 'ld')
            c.wait(S, Q + 'mm')
            S.activation(sq[:, :], b[:, :], AF.Square)
            c.mark(S.drain(), Q + 'ss')
            c.wait(nc.tensor, Q + 'ss')
            nc.tensor.matmul(psA[0:1, 0:512], s.onc(), b[:, :],
                             start=(ch == 0), stop=(ch == DC - 1))
            m = nc.tensor.matmul(psB[0:1, 0:512], s.onc(), sq[:, :],
                                 start=(ch == 0), stop=(ch == DC - 1))
            c.mark(m, Q + 'mm')
            c.wait(V, Q + 'ld')
            if ch == 0:
                V.tensor_copy(rmx[:, :], b[:, :])
                V.drain()
                V.tensor_scalar(rmn[:, :], b[:, :], -1.0, None, op0=AL.mult)
            else:
                V.tensor_tensor(rmx[:, :], rmx[:, :], b[:, :], op=AL.max)
                V.drain()
                V.tensor_scalar(neg[:, :], b[:, :], -1.0, None, op0=AL.mult)
                V.drain()
                V.tensor_tensor(rmn[:, :], rmn[:, :], neg[:, :], op=AL.max)
            c.mark(V.drain(), Q + 'vv')
        c.wait(V, Q + 'mm')
        c.wait(V, Q + 'ld')
        V.tensor_copy(strip[0], psA[0:1, 0:512])
        V.drain()
        V.tensor_copy(strip[1], psB[0:1, 0:512])
        c.mark(V.drain(), Q + 'vv')
        c.wait(nc.tensor, Q + 'vv')
        JB = 512 // 128

        def row_to_cols(st, dst, slot):
            c.wait(nc.sync, Q + 'vv')
            c.dma(nc.sync, s.rowtmp[slot][None, :], st, q=Q + 'ld')
            c.chain(Q + 'ld')
            c.dma(nc.sync, dst,
                  s.rowtmp[slot].rearrange("(j p) -> p j", p=128), q=Q + 'ld',
                  slow=True)
        row_to_cols(strip[0], cq(SX)[:, nt * JB:(nt + 1) * JB], 0)
        row_to_cols(strip[1], cq(QX)[:, nt * JB:(nt + 1) * JB], 1)
        psD = s.pb[2]
        for k, t in ((MX, rmx), (MN, rmn)):
            for jj in range(JB):
                c.wait(nc.tensor, Q + 'vv')
                c.mark(nc.tensor.transpose(psD[0:128, 0:128],
                                           t[:, jj * 128:(jj + 1) * 128],
                                           s.id_f[:, :]), Q + 'mm')
                c.wait(V, Q + 'mm')
                V.tensor_reduce(cq(k)[:, nt * JB + jj:nt * JB + jj + 1],
                                psD[0:128, 0:128], axis=AX.X, op=AL.max)
                c.mark(V.drain(), Q + 'vv')
        c.wait(nc.tensor, Q + 'vv')

    c.dma(nc.sync, dskq[0:1, 0:1], s.Dskip[None, :], q=Q + 'ld')
    c.wait(nc.tensor, Q + 'ld')
    psD = s.pb[2]
    c.mark(nc.tensor.matmul(psD[0:128, 0:1], s.onr(), dskq[0:1, 0:1],
                            start=True, stop=True), Q + 'mm')
    c.wait(V, Q + 'mm')
    V.tensor_copy(cq(DSKC)[:, 0:1], psD[0:128, 0:1])
    c.mark(V.drain(), Q + 'vv')
    V.tensor_scalar(cq(M1), cq(SX), 1.0 / D, None, op0=AL.mult)
    V.drain()
    V.tensor_scalar(cq(T0), cq(QX), 1.0 / D, None, op0=AL.mult)
    V.drain()
    V.tensor_mul(cq(T1), cq(M1), cq(M1))
    V.drain()
    V.tensor_sub(cq(T0), cq(T0), cq(T1))
    c.mark(V.drain(), Q + 'vv')
    c.wait(S, Q + 'vv')
    S.activation(cq(T0), cq(T0), AF.Sqrt, bias=float(LN_EPS))
    c.mark(S.drain(), Q + 'ss')
    c.wait(V, Q + 'ss')
    s.recip_nr(V, cq(S1), cq(T0), cq(T1))
    V.tensor_sub(cq(T0), cq(MX), cq(M1))
    V.drain()
    V.tensor_add(cq(T1), cq(MN), cq(M1))
    V.drain()
    V.tensor_tensor(cq(T0), cq(T0), cq(T1), op=AL.max)
    V.drain()
    V.tensor_mul(cq(T0), cq(T0), cq(S1))
    V.drain()
    V.tensor_scalar(cq(T0), cq(T0), 1e-8, None, op0=AL.max)
    V.drain()
    V.tensor_scalar(cq(XSC), cq(T0), 1.0 / 127.0, None, op0=AL.mult)
    V.drain()
    s.recip_nr(V, cq(T1), cq(XSC), cq(T0))
    V.tensor_mul(cq(QQ), cq(S1), cq(T1))
    V.drain()
    V.tensor_mul(cq(NMQ), cq(M1), cq(QQ))
    V.drain()
    V.tensor_scalar(cq(NMQ), cq(NMQ), -1.0, None, op0=AL.mult)
    V.drain()
    V.tensor_mul(cq(NMS1), cq(M1), cq(S1))
    V.drain()
    V.tensor_scalar(cq(NMS1), cq(NMS1), -1.0, None, op0=AL.mult)
    V.drain()
    V.tensor_scalar(cq(DS), cq(S1), cq(DSKC)[:, 0:1], None, op0=AL.mult)
    V.drain()
    V.tensor_mul(cq(NDMS), cq(DS), cq(M1))
    V.drain()
    V.tensor_scalar(cq(NDMS), cq(NDMS), -1.0, None, op0=AL.mult)
    c.mark(V.drain(), Q + 'vv')

    rr = lambda i: s.rowd[i].rearrange("(j p) -> j p", p=128)
    sh = lambda i: s.statsh[par, i].rearrange("(j p) -> j p", p=128)
    s.cols_to_row(cq(S1), sh(0), NJ, pfx=Q)
    s.cols_to_row(cq(NMS1), sh(1), NJ, pfx=Q)
    s.cols_to_row(cq(QQ), rr(R_Q), NJ, pfx=Q)
    s.cols_to_row(cq(NMQ), rr(R_NMQ), NJ, pfx=Q)
    s.cols_to_row(cq(XSC), rr(R_XSC), NJ, pfx=Q)
    s.cols_to_row(cq(DS), rr(R_DS), NJ, pfx=Q)
    s.cols_to_row(cq(NDMS), rr(R_NDMS), NJ, pfx=Q)

    wqs = s.col("b1_wsc", 8)
    wcol = s.col("b1_wcol", 1)
    wcol2 = s.col("b1_wcol2", 1)
    psT = s.pb[3]
    for oc in range(DC):
        c.wait(nc.sync, Q + 'mm')
        c.wait(nc.sync, Q + 'vv')
        c.dma(nc.sync, wt[:, :], s.l1w[oc * 128:(oc + 1) * 128, :], q=Q + 'ld')
        c.wait(V, Q + 'ld')
        V.tensor_reduce(wcol, wt[:, :], axis=AX.X, op=AL.max,
                        apply_absolute_value=True)
        V.drain()
        V.tensor_scalar(wcol, wcol, 1e-8, None, op0=AL.max)
        V.drain()
        V.tensor_scalar(wqs[:, oc:oc + 1], wcol, 1.0 / 127.0, None, op0=AL.mult)
        V.drain()
        s.recip_nr(V, wcol2, wqs[:, oc:oc + 1], wcol)
        V.tensor_scalar(wt[:, :], wt[:, :], wcol2, None, op0=AL.mult)
        V.drain()
        iv = sqw.bitcast(mybir.dt.int32)
        V.tensor_copy(iv, wt[:, :])
        V.drain()
        V.tensor_copy(rndf[:, :], iv)
        V.drain()
        V.tensor_sub(wt[:, :], wt[:, :], rndf[:, :])
        V.drain()
        V.tensor_single_scalar(sqw[:, :], wt[:, :], 0.5, op=AL.is_ge)
        V.drain()
        V.tensor_add(rndf[:, :], rndf[:, :], sqw[:, :])
        V.drain()
        V.tensor_single_scalar(sqw[:, :], wt[:, :], -0.5, op=AL.is_lt)
        V.drain()
        V.tensor_sub(wq16[:, :], rndf[:, :], sqw[:, :])
        c.mark(V.drain(), Q + 'vv')
        for kf in range(DC):
            c.wait(nc.tensor, Q + 'vv')
            c.mark(nc.tensor.transpose(psT[0:128, 0:128].bitcast(BF16)[:, 0:128],
                                       wq16[:, kf * 128:(kf + 1) * 128],
                                       s.id_b[:, :]), Q + 'mm')
            c.wait(V, Q + 'mm')
            c.wait(V, Q + 'ld')
            V.tensor_copy(wqt_t[:, :], psT[0:128, 0:128].bitcast(BF16)[:, 0:128])
            c.mark(V.drain(), Q + 'vv')
            c.wait(nc.sync, Q + 'vv')
            c.dma(nc.sync, s.wqt_dram[kf * 128:(kf + 1) * 128, oc * 128:(oc + 1) * 128],
                  wqt_t[:, :], q=Q + 'ld')
    s.wqs_col = wqs

    b1_s4_chain(s, par)
    s.barrier_lite(Q + 'bar')


def emit_fft_fwd(s, Q, rT, CH, Y, X, nyrow, kst=None):
    c, nc = s.c, s.nc
    V, S = nc.vector, nc.scalar
    QNZ = s.SEQ // 128
    KB = s.KB
    PSTR, PACKE = s.PSTR, s.PACKE
    s1c = s.fft_s1_sb
    psY = [s.pb[0], s.pb[3]]
    psX = [s.pb[1], s.pb[2]]
    mall = kst if kst is not None else s.cst['f_mall']
    mbuf = s._fwd_mbuf

    for chp in range(0, CH, 2):
        ps = psY[(chp // 2) % 2]
        c.wait_lag(nc.tensor, Q + 'ss', 1)
        for ci in range(2):
            m = nc.tensor.matmul(ps[0:128, ci * 256:(ci + 1) * 256],
                                 rT[0:QNZ, (chp + ci)::CH],
                                 s1c[0:QNZ, :], start=True, stop=True)
        c.mark(m, Q + 'mm')
        c.wait(S, Q + 'mm')
        S.copy(Y[:, chp * 256:(chp + 2) * 256], ps[0:128, 0:512])
        c.mark(S.drain(), Q + 'ss')
    c.wait(nc.tensor, Q + 'ss')
    m = nc.tensor.matmul(psX[0][0:1, 0:CH], s.sgncol_sb[:, :], Y[:, 0::256],
                         start=True, stop=True)
    c.mark(m, Q + 'mm')
    c.wait(V, Q + 'mm')
    V.tensor_copy(nyrow[0:1, 0:CH], psX[0][0:1, 0:CH])
    c.mark(V.drain(), Q + 'vv')
    c.wait(nc.tensor, Q + 'vv')

    for k1 in range(128):
        mb = mbuf[k1 % 2]
        c.wait_lag(nc.sync, Q + 'mm', 1)
        c.dma(nc.sync, mb[:, :], mall[k1], q=Q + 'ld')
        c.wait(nc.tensor, Q + 'ld')
        c.wait_lag(nc.tensor, Q + 'vv', 1)
        po = (k1 % PACKE) * PSTR
        ps = psX[k1 % 2]
        tp = (0, po)
        Yre = Y[:, k1::256]
        Yim = Y[:, 128 + k1::256]
        nc.tensor.matmul(ps[po:po + KB, 0:CH], mb[:, 0:KB], Yre,
                         start=True, stop=False, tile_position=tp)
        nc.tensor.matmul(ps[po:po + KB, 0:CH], mb[:, 2 * KB:3 * KB], Yim,
                         start=False, stop=True, tile_position=tp)
        nc.tensor.matmul(ps[po:po + KB, CH:2 * CH], mb[:, KB:2 * KB], Yre,
                         start=True, stop=False, tile_position=tp)
        m = nc.tensor.matmul(ps[po:po + KB, CH:2 * CH], mb[:, 0:KB], Yim,
                             start=False, stop=True, tile_position=tp)
        c.mark(m, Q + 'mm')
        g = k1 // PACKE
        c.wait(V, Q + 'mm')
        V.tensor_copy(X[po:po + KB, g * 2 * CH:(g + 1) * 2 * CH],
                      ps[po:po + KB, 0:2 * CH])
        c.mark(V.drain(), Q + 'vv')
    c.wait(nc.tensor, Q + 'vv')


def b1_conv(s, par):
    c, nc = s.c, s.nc
    V, S = nc.vector, nc.scalar
    A = s.arena
    SEQ, T, DC, D = s.SEQ, s.T, s.DC, s.D
    QNZ = SEQ // 128
    KB, PSTR, PACKE = s.KB, s.PSTR, s.PACKE
    NGRP = s.NGRP
    CP = s.CPASS
    NPASS = s.NPASS
    Q = 'c_'

    stage = A.f32(0, SEQ)
    xn1 = A.bf16(8192, SEQ)
    rT = A.bf16(0, 128 * CP)
    Ybuf = A.bf16(8192, CP * 256)
    Xbuf = A.bf16(0, NGRP * 2 * CP)
    kfull = A.bf16(8192, NGRP * 2 * CP)
    scr1 = A.bf16(16384, NGRP * CP)
    scr2 = A.bf16(20480, NGRP * CP)
    Gbuf = A.bf16(8192, 128 * 2 * 128)
    GTre = A.bf16(0, CP * 128)
    GTim = A.bf16(24576, CP * 128)
    hown = A.bf16(8192, T)
    hpeer = A.bf16(8192 + T // 2, T)
    lnA = A.f32(12288, 512, 1)
    lnB = A.f32(12800, 512, 1)
    iab = s._cv_iab
    sny = s.sny_sb
    snyb = s.snyb_sb
    pnyb = s.pnyb_sb

    psG = [s.pb[4], s.pb[5]]
    psH = s.pb[6]
    psB2 = s.pb[7]
    psT = s.pb[3]

    for pp in range(NPASS):
        ch0 = par * (D // 2) + pp * CP
        s.barrier_lite(Q + 'bar')
        c.dma(nc.sync, stage[:, 0:T], s.x1sh[0, ch0:ch0 + 128, :], q=Q + 'ld')
        c.dma(nc.sync, stage[:, T:SEQ], s.x1sh[1, ch0:ch0 + 128, :], q=Q + 'ld')
        NTT = T // 512
        for jb in range(SEQ // 512):
            half, jo = jb // NTT, jb % NTT
            c.wait(nc.sync, Q + 'mm')
            c.dma(nc.sync, lnA, s.statsh[half, 0, jo * 512:(jo + 1) * 512][None, :],
                  q=Q + 'ld')
            c.dma(nc.sync, lnB, s.statsh[half, 1, jo * 512:(jo + 1) * 512][None, :],
                  q=Q + 'ld')
            sl = slice(jb * 512, jb * 512 + 512)
            c.wait(nc.tensor, Q + 'vv')
            c.wait(nc.tensor, Q + 'ld')
            m = nc.tensor.matmul(psB2[0:128, 0:512], s.onr(), lnA,
                                 start=True, stop=True)
            c.mark(m, Q + 'mm')
            c.wait(V, Q + 'mm')
            c.wait(V, Q + 'ld')
            V.tensor_mul(xn1[:, sl], stage[:, sl], psB2[0:128, 0:512])
            c.mark(V.drain(), Q + 'vv')
            c.wait(nc.tensor, Q + 'vv')
            m = nc.tensor.matmul(psB2[0:128, 0:512], s.onr(), lnB,
                                 start=True, stop=True)
            c.mark(m, Q + 'mm')
            c.wait(V, Q + 'mm')
            V.tensor_add(xn1[:, sl], xn1[:, sl], psB2[0:128, 0:512])
            c.mark(V.drain(), Q + 'vv')
            c.wait(nc.tensor, Q + 'vv')
        for p8 in range(16):
            for pi in range(8):
                p = p8 * 8 + pi
                c.mark(nc.tensor.transpose(
                    psT[0:QNZ, 0:512].bitcast(BF16)[0:QNZ, pi * 128:(pi + 1) * 128],
                    xn1[:, p::128], s.id_b[:, :]), Q + 'mm')
            c.wait(V, Q + 'mm')
            V.tensor_copy(rT[0:QNZ, p8 * 1024:(p8 + 1) * 1024],
                          psT[0:QNZ, 0:512].bitcast(BF16)[0:QNZ, 0:1024])
            c.mark(V.drain(), Q + 'vv')
            c.wait(nc.tensor, Q + 'vv')
        emit_fft_fwd(s, Q, rT, CP, Ybuf, Xbuf, sny[0:1, pp * CP:(pp + 1) * CP])
        c.wait(V, Q + 'vv')
        V.tensor_copy(snyb[0:1, pp * CP:(pp + 1) * CP],
                      sny[0:1, pp * CP:(pp + 1) * CP])
        c.mark(V.drain(), Q + 'vv')

        bands = ([(0, 128)] if KB == PSTR else
                 [(po, po + KB) for po in (0, PSTR)])
        c.wait(nc.sync, Q + 'vv')
        for a, b_ in bands:
            c.dma(nc.sync,
                  kfull[a:b_, :].rearrange("p (g c h) -> p g c h", g=NGRP, c=2),
                  s.kf_dram[a:b_, :, :, pp * CP:(pp + 1) * CP], q=Q + 'ld')
        c.wait(V, Q + 'ld')
        def gch(ap):
            return ap.rearrange("p (g c h) -> p g c h", g=NGRP, c=2)
        for a, b_ in bands:
            Xr = gch(Xbuf[:, :])[a:b_, :, 0, :]
            Xi = gch(Xbuf[:, :])[a:b_, :, 1, :]
            Kr = gch(kfull[:, :])[a:b_, :, 0, :]
            Ki = gch(kfull[:, :])[a:b_, :, 1, :]
            t1 = scr1[:, :].rearrange("p (g h) -> p g h", g=NGRP)[a:b_]
            t2 = scr2[:, :].rearrange("p (g h) -> p g h", g=NGRP)[a:b_]
            V.tensor_mul(t1, Xr, Ki)
            V.drain()
            V.tensor_mul(t2, Xi, Kr)
            V.drain()
            V.tensor_mul(Xr, Xr, Kr)
            V.drain()
            V.tensor_mul(Kr, Xi, Ki)
            V.drain()
            V.tensor_sub(Xr, Xr, Kr)
            V.drain()
            V.tensor_add(Xi, t1, t2)
            V.drain()
        V.tensor_mul(pnyb[0:1, pp * CP:(pp + 1) * CP],
                     snyb[0:1, pp * CP:(pp + 1) * CP],
                     s.kfny_rowb[0:1, pp * CP:(pp + 1) * CP])
        c.mark(V.drain(), Q + 'vv')

        c.wait(nc.tensor, Q + 'vv')
        for k1 in range(128):
            ia = iab[k1 % 2]
            c.wait(nc.sync, Q + 'mm')
            c.dma(nc.sync, ia[:, 0:128], s.cst['i_re'][k1], q=Q + 'ld')
            c.dma(nc.sync, ia[:, 128:256], s.cst['i_im'][k1], q=Q + 'ld')
            c.dma(nc.sync, ia[:, 256:384], s.cst['i_nim'][k1], q=Q + 'ld')
            c.wait(nc.tensor, Q + 'ld')
            if k1 >= 1:
                c.wait(nc.tensor, Q + 'vv')
            po = (k1 % PACKE) * PSTR
            g = k1 // PACKE
            tpr = (po, 0)
            Pre = Xbuf[po:po + KB, (g * 2 + 0) * CP:(g * 2 + 1) * CP]
            Pim = Xbuf[po:po + KB, (g * 2 + 1) * CP:(g * 2 + 2) * CP]
            nc.tensor.matmul(psG[0][0:CP, 0:128], Pre, ia[po:po + KB, 0:128],
                             start=True, stop=False, tile_position=tpr)
            nc.tensor.matmul(psG[1][0:CP, 0:128], Pre, ia[po:po + KB, 128:256],
                             start=True, stop=False, tile_position=tpr)
            stop0 = (k1 != 0)
            nc.tensor.matmul(psG[0][0:CP, 0:128], Pim, ia[po:po + KB, 256:384],
                             start=False, stop=stop0, tile_position=tpr)
            m = nc.tensor.matmul(psG[1][0:CP, 0:128], Pim, ia[po:po + KB, 0:128],
                                 start=False, stop=True, tile_position=tpr)
            if k1 == 0:
                m2 = nc.tensor.matmul(psG[0][0:CP, 0:128],
                                      pnyb[0:1, pp * CP:(pp + 1) * CP],
                                      s.sgnrow_sb[:, :], start=False, stop=True)
            c.mark(m if k1 != 0 else m2, Q + 'mm')
            c.wait(V, Q + 'mm')
            V.tensor_copy(Gbuf[0:CP, (k1 * 2 + 0) * 128:(k1 * 2 + 1) * 128],
                          psG[0][0:CP, 0:128])
            V.drain()
            V.tensor_copy(Gbuf[0:CP, (k1 * 2 + 1) * 128:(k1 * 2 + 2) * 128],
                          psG[1][0:CP, 0:128])
            c.mark(V.drain(), Q + 'vv')
        c.wait(nc.tensor, Q + 'vv')

        psTr = [s.pb[3], s.pb[0]]
        for comp in range(2):
            GT = GTre if comp == 0 else GTim
            for p4 in range(32):
                pt = psTr[p4 % 2]
                ev = V if p4 % 2 == 0 else S
                evq = Q + ('vv' if p4 % 2 == 0 else 'ss')
                c.wait(nc.tensor, evq)
                for pi in range(4):
                    p = p4 * 4 + pi
                    c.mark(nc.tensor.transpose(
                        pt[0:128, 0:512].bitcast(BF16)[0:128, pi * 128:(pi + 1) * 128],
                        Gbuf[0:CP, (comp * 128 + p)::256], s.id_b[0:CP, 0:CP]),
                        Q + 'mm')
                c.wait(ev, Q + 'mm')
                if ev is V:
                    V.tensor_copy(GT[:, p4 * 4 * CP:(p4 * 4 + 4) * CP],
                                  pt[0:128, 0:512].bitcast(BF16)[0:128, 0:512])
                else:
                    S.copy(GT[:, p4 * 4 * CP:(p4 * 4 + 4) * CP],
                           pt[0:128, 0:512].bitcast(BF16)[0:128, 0:512])
                c.mark(ev.drain(), evq)
        c.wait(nc.tensor, Q + 'vv')
        c.wait(nc.tensor, Q + 'ss')

        qo = par * (QNZ // 2)
        qp = (1 - par) * (QNZ // 2)
        for p in range(128):
            c.wait(nc.tensor, Q + 'vv')
            nc.tensor.matmul(psH[0:128, 0:QNZ], GTre[:, p * CP:(p + 1) * CP],
                             s.fftb_re_sb[:, 0:QNZ], start=True, stop=False)
            m = nc.tensor.matmul(psH[0:128, 0:QNZ], GTim[:, p * CP:(p + 1) * CP],
                                 s.fftb_im_sb[:, 0:QNZ], start=False, stop=True)
            c.mark(m, Q + 'mm')
            c.wait(V, Q + 'mm')
            V.tensor_copy(hown[:, p::128], psH[0:128, qo:qo + QNZ // 2])
            V.drain()
            V.tensor_copy(hpeer[:, p::128], psH[0:128, qp:qp + QNZ // 2])
            c.mark(V.drain(), Q + 'vv')
        c.wait(nc.sync, Q + 'vv')
        c.dma(nc.sync, s.hfeat_dram[pp], hown[:, :], q=Q + 'ld')
        c.dma(nc.sync, s.hsh[par, pp], hpeer[:, :], q=Q + 'ld')
        c.wait(nc.sync, Q + 'ld')
    s.barrier_lite(Q + 'bar')


def b1_final(s, par):
    c, nc = s.c, s.nc
    V, S = nc.vector, nc.scalar
    T, D, DC, NT = s.T, s.D, s.DC, s.NT
    A = s.arena
    Q = 'f_'

    xq = A.bf16(0, DC * 512)
    htile = A.bf16(2048, 512)
    y1 = A.f32(2304, 512)
    gt = A.f32(2816, 512)
    outt = A.f32(3328, 512)
    vt = A.f32(3840, 512)
    wqcol = A.bf16(4352, DC * 128)
    x1t = [A.f32(4864, 512), A.f32(5376, 512)]
    bD = A.f32(5888, 512)
    bq = A.f32(6400, 512)
    mdt = A.f32(6912, 512)
    stq = A.f32(7424, 512, 1)
    stnmq = A.f32(7936, 512, 1)
    stDs = A.f32(8448, 512, 1)
    stnDms = A.f32(8960, 512, 1)
    stxsc = A.f32(9472, 512, 1)

    psF = s.pb[0]
    psB = s.pb[1]
    psO = s.pb[2]
    l1b_c = s.col("fn_l1b", 8)
    c.dma(nc.sync, l1b_c, s.l1b.rearrange("(c p) -> p c", p=128), q=Q + 'ld',
          slow=True)

    def bcast(dst, strip):
        c.wait(nc.tensor, Q + 'vv')
        m = nc.tensor.matmul(psB[0:128, 0:512], s.onr(), strip,
                             start=True, stop=True)
        c.mark(m, Q + 'mm')
        c.wait(V, Q + 'mm')
        V.tensor_copy(dst[:, :], psB[0:128, 0:512])
        c.mark(V.drain(), Q + 'vv')

    for nt in range(NT):
        tsl = slice(nt * 512, (nt + 1) * 512)
        c.wait(nc.sync, Q + 'mm')
        c.dma(nc.sync, stq, s.rowd[R_Q, tsl][None, :], q=Q + 'ld')
        c.dma(nc.sync, stnmq, s.rowd[R_NMQ, tsl][None, :], q=Q + 'ld')
        c.dma(nc.sync, stDs, s.rowd[R_DS, tsl][None, :], q=Q + 'ld')
        c.dma(nc.sync, stnDms, s.rowd[R_NDMS, tsl][None, :], q=Q + 'ld')
        c.dma(nc.sync, stxsc, s.rowd[R_XSC, tsl][None, :], q=Q + 'ld')
        c.wait(nc.tensor, Q + 'ld')
        bcast(bq, stq)
        bcast(mdt, stnmq)
        for kf in range(DC):
            b = x1t[kf % 2]
            c.wait(nc.sync, Q + 'vv')
            c.wait(nc.sync, Q + 'mm')
            c.dma(nc.sync, b[:, :], s.x1sh[par, kf * 128:(kf + 1) * 128, tsl],
                  q=Q + 'ld')
            c.wait(V, Q + 'ld')
            c.wait(V, Q + 'st')
            V.tensor_mul(vt[:, :], b[:, :], bq[:, :])
            V.drain()
            V.tensor_add(vt[:, :], vt[:, :], mdt[:, :])
            V.drain()
            ivf = y1.bitcast(mybir.dt.int32)
            V.tensor_copy(ivf, vt[:, :])
            V.drain()
            V.tensor_copy(gt[:, :], ivf)
            V.drain()
            V.tensor_sub(vt[:, :], vt[:, :], gt[:, :])
            V.drain()
            V.tensor_single_scalar(y1[:, :], vt[:, :], 0.5, op=AL.is_ge)
            V.drain()
            V.tensor_add(gt[:, :], gt[:, :], y1[:, :])
            V.drain()
            V.tensor_single_scalar(y1[:, :], vt[:, :], -0.5, op=AL.is_lt)
            V.drain()
            V.tensor_sub(xq[:, kf * 512:(kf + 1) * 512], gt[:, :], y1[:, :])
            c.mark(V.drain(), Q + 'vv')
        bcast(bD, stDs)
        bcast(mdt, stnDms)
        bcast(bq, stxsc)

        for dc in range(DC):
            my_half = (dc // (DC // 2)) == par
            ppx = dc % (DC // 2)
            hsrc = s.hfeat_dram[ppx] if my_half else s.hsh[1 - par, ppx]
            c.wait(nc.sync, Q + 'vv')
            c.dma(nc.sync, htile[:, :], hsrc[:, tsl], q=Q + 'ld')
            b = x1t[dc % 2]
            c.wait(nc.sync, Q + 'mm')
            c.dma(nc.sync, b[:, :], s.x1sh[par, dc * 128:(dc + 1) * 128, tsl],
                  q=Q + 'ld')
            c.wait(V, Q + 'ld')
            V.tensor_mul(y1[:, :], b[:, :], bD[:, :])
            V.drain()
            V.tensor_add(y1[:, :], y1[:, :], mdt[:, :])
            V.drain()
            V.tensor_add(y1[:, :], y1[:, :], htile[:, :])
            c.mark(V.drain(), Q + 'vv')
            c.wait(S, Q + 'vv')
            S.activation(gt[:, :], y1[:, :], AF.Gelu)
            c.mark(S.drain(), Q + 'ss')
            c.wait(nc.sync, Q + 'mm')
            c.dma(nc.sync,
                  wqcol[:, :].rearrange("p (k o) -> p k o", o=128),
                  s.wqt_dram[:, dc * 128:(dc + 1) * 128].rearrange(
                      "(k p) o -> p k o", p=128), q=Q + 'wld')
            c.wait(nc.tensor, Q + 'vv')
            c.wait(nc.tensor, Q + 'wld')
            for kf in range(DC):
                mm = nc.tensor.matmul(psF[0:128, 0:512],
                                      wqcol[:, kf * 128:(kf + 1) * 128],
                                      xq[:, kf * 512:(kf + 1) * 512],
                                      start=(kf == 0), stop=(kf == DC - 1))
            c.mark(mm, Q + 'mm')
            c.wait(V, Q + 'mm')
            c.wait(V, Q + 'ss')
            V.tensor_scalar(outt[:, :], psF[0:128, 0:512],
                            s.wqs_col[:, dc:dc + 1], None, op0=AL.mult)
            V.drain()
            V.tensor_mul(outt[:, :], outt[:, :], bq[:, :])
            V.drain()
            V.scalar_tensor_tensor(outt[:, :], outt[:, :],
                                   l1b_c[:, dc:dc + 1], gt[:, :],
                                   op0=AL.add, op1=AL.add)
            V.drain()
            V.tensor_add(outt[:, :], outt[:, :], b[:, :])
            c.mark(V.drain(), Q + 'vv')
            c.wait(nc.tensor, Q + 'vv')
            for j in range(4):
                c.mark(nc.tensor.transpose(psO[0:128, j * 128:(j + 1) * 128],
                                           outt[:, j * 128:(j + 1) * 128],
                                           s.id_f[:, :]), Q + 'mm')
            c.wait(V, Q + 'mm')
            c.wait(V, Q + 'st')
            V.tensor_copy(vt[:, :], psO[0:128, 0:512])
            c.mark(V.drain(), Q + 'vv')
            c.wait(nc.sync, Q + 'vv')
            c.dma(nc.sync,
                  s.y_out[nt * 512:(nt + 1) * 512, dc * 128:(dc + 1) * 128]
                      .rearrange("(j p) d -> p j d", p=128),
                  vt[:, :].rearrange("p (j d) -> p j d", j=4), q=Q + 'st')
    c.wait(nc.sync, Q + 'st')
    s.barrier_lite(Q + 'bar')


def b1_s4_chain(s, par):
    c, nc = s.c, s.nc
    V, S, G = nc.vector, nc.scalar, nc.gpsimd
    A = s.arena
    SEQ, D = s.SEQ, s.D
    QDs = SEQ // 128
    KB, PSTR, PACKE = s.KB, s.PSTR, s.PACKE
    NGRP = s.NGRP
    NCH = SEQ // 512
    nw = SEQ // 64
    APC = 512 // nw
    Q = 'q_'
    LB = 32768

    cre = A.f32(0, SEQ, 64)
    cim = A.f32(SEQ, SEQ, 64)
    gmre = A.bf16(2 * SEQ, SEQ, 64)
    gmim = A.bf16(2 * SEQ + SEQ // 2, SEQ, 64)
    if SEQ >= 4096:
        O_GTm, O_Yg, O_gbuf, O_Ymode, O_Xmode, O_gfT = 0, 8192, 16384, 0, 16384, 24576
    else:
        O_GTm, O_Yg, O_gbuf = 4 * SEQ, 4 * SEQ + 8192, 4 * SEQ + 16384
        O_Ymode, O_Xmode, O_gfT = 0, 16384, 0
    GTmre = A.bf16(O_GTm, 128 * 64)
    GTmim = A.bf16(O_GTm + 4096, 128 * 64)
    Ygbuf = A.bf16(O_Yg, 64 * 256)
    gbuf = A.bf16(O_gbuf, 128 * 2 * 64, QDs)
    Ymode = A.bf16(O_Ymode, 128 * 256)
    Xmode = A.bf16(O_Xmode, NGRP * 2 * 128)
    gfT = A.bf16(O_gfT, 128 * 2 * KB)

    assert nw <= 128
    s4t = [A.f32(LB + i * 512, 512, 64) for i in range(5)]
    s4g = [A.bf16(LB + 2560, 3 * QDs),
           A.bf16(LB + 2560 + (3 * QDs + 1) // 2, 3 * QDs)]
    cl = A.f32(LB + 2944, 64)
    ct_sb = A.bf16(LB + 3008, D // 2)
    kfs = A.bf16(LB + 3264, 512, 64)
    ctt = A.f32(LB + 3520, 128, 64)
    k4 = A.f32(LB + 3648, 4 * nw, 64)
    tstg = A.f32(LB + 4160, 512, 1)
    i1r = A.f32(LB + 4672, 512, 1)
    i1i = A.f32(LB + 5184, 512, 1)
    wr = A.f32(LB + 5696, 512, 1)
    wi = A.f32(LB + 6208, 512, 1)
    dsk2 = A.f32(LB + 6720, 2, 1)
    assert LB + 7008 <= s.AW

    c.dma(nc.sync, s.fft_s1_sb[:, :], s.cst['f_s1'][:, :], q=Q + 'ld')
    c.dma(nc.sync, s.s4_s1a_sb[:, :], s.cst['s4_s1a'][:, :], q=Q + 'ld')
    c.dma(nc.sync, s.s4_s1b_sb[:, :], s.cst['s4_s1b'][:, :], q=Q + 'ld')
    c.dma(nc.sync, s.sgncol_sb[:, :], s.cst['sign_col'][:, :], q=Q + 'ld')
    c.dma(nc.sync, s.sgnrow_sb[:, :], s.cst['sign_row_n'][:, :], q=Q + 'ld')
    c.dma(nc.sync, s.fftb_re_sb[:, :], s.cst['b_re'][:, :], q=Q + 'ld')
    c.dma(nc.sync, s.fftb_im_sb[:, :], s.cst['b_im'][:, :], q=Q + 'ld')

    vc = s.col("s4_vc", 24)
    for i, t in enumerate(["B_re", "B_im", "P_re", "P_im", "Q_re", "Q_im",
                           "diag_re", "diag_im"]):
        c.dma(nc.sync, vc[0:64, i:i + 1], getattr(s, "s4_" + t)[:, None], q=Q + 'ld')
    Bre, Bim, Pre, Pim, Qre, Qim, dre, dim = (vc[0:64, i:i + 1] for i in range(8))
    QBre, QBim, QPre, QPim = (vc[0:64, i:i + 1] for i in range(8, 12))
    nQBim, nQPim = (vc[0:64, i:i + 1] for i in range(12, 14))
    acol, a2col, i2dc = (vc[0:64, i:i + 1] for i in range(14, 17))
    c.dma(nc.sync, dsk2[0:1, 0:1], s.s4_step[:, None], q=Q + 'ld')
    c.wait(V, Q + 'ld')
    V.tensor_scalar(dsk2[0:1, 0:1], dsk2[0:1, 0:1], 1e-6, 0.5, op0=AL.max, op1=AL.mult)
    V.drain()
    s.recip_nr(V, dsk2[0:1, 1:2], dsk2[0:1, 0:1], dsk2[0:1, 0:1])
    c.mark(V.drain(), Q + 'vv')
    c.wait(nc.tensor, Q + 'vv')
    c.mark(nc.tensor.matmul(s.pb[7][0:64, 0:1], s.onr(n=64),
                            dsk2[0:1, 1:2], start=True, stop=True), Q + 'mm')
    c.wait(V, Q + 'mm')
    V.tensor_copy(i2dc, s.pb[7][0:64, 0:1])
    c.mark(V.drain(), Q + 'gg')
    tA, tB = (vc[0:64, i:i + 1] for i in range(19, 21))
    def cmul(outre, outim, ar, ai, br, bi):
        V.tensor_mul(tA, ar, br); V.drain()
        V.tensor_mul(tB, ai, bi); V.drain()
        V.tensor_sub(outre, tA, tB); V.drain()
        V.tensor_mul(tA, ar, bi); V.drain()
        V.tensor_mul(tB, ai, br); V.drain()
        V.tensor_add(outim, tA, tB); V.drain()
    cmul(QBre, QBim, Qre, Qim, Bre, Bim)
    cmul(QPre, QPim, Qre, Qim, Pre, Pim)
    V.tensor_scalar(nQBim, QBim, -1.0, None, op0=AL.mult); V.drain()
    V.tensor_scalar(nQPim, QPim, -1.0, None, op0=AL.mult); V.drain()
    V.tensor_scalar(acol, dre, -1.0, None, op0=AL.mult); V.drain()
    V.tensor_mul(a2col, dre, dre)
    c.mark(V.drain(), Q + 'vv')

    bt, dt_, rc, t4, t5 = s4t
    psB = s.pb[7]
    psK = s.pb[6]
    c.wait(G, Q + 'ld')
    for lc in range(NCH):
        sl = slice(lc * 512, (lc + 1) * 512)
        c.wait(nc.sync, Q + 'mm')
        c.dma(nc.sync, tstg, s.cst['s4_timag'][0:1, sl], q=Q + 'ld')
        c.wait(nc.tensor, Q + 'vv')
        c.wait(nc.tensor, Q + 'gg')
        c.wait(nc.tensor, Q + 'ld')
        m = nc.tensor.matmul(psB[0:64, 0:512], s.onr(n=64), tstg,
                             start=True, stop=True)
        c.mark(m, Q + 'mm')
        c.wait(V, Q + 'mm')
        V.tensor_scalar(bt[:, :], psB[0:64, 0:512], i2dc, dim,
                        op0=AL.mult, op1=AL.subtract)
        c.mark(V.drain(), Q + 'vv')
        c.wait(S, Q + 'vv')
        S.activation(dt_[:, :], bt[:, :], AF.Square)
        c.mark(S.drain(), Q + 'ss')
        c.wait(V, Q + 'ss')
        V.tensor_scalar(dt_[:, :], dt_[:, :], a2col, None, op0=AL.add)
        V.drain()
        s.recip_nr(V, rc[:, :], dt_[:, :], t4[:, :])
        V.tensor_scalar(cre[:, sl], rc[:, :], acol, None, op0=AL.mult)
        V.drain()
        V.tensor_mul(t4[:, :], bt[:, :], rc[:, :])
        V.drain()
        V.tensor_scalar(cim[:, sl], t4[:, :], -1.0, None, op0=AL.mult)
        c.mark(V.drain(), Q + 'vv')
        for ki, (ca, cb, aa, bb) in enumerate([
            (QBre, nQBim, cre, cim), (QBre, QBim, cim, cre),
            (QPre, nQPim, cre, cim), (QPre, QPim, cim, cre),
        ]):
            c.wait(nc.tensor, Q + 'vv')
            nc.tensor.matmul(psK[0:1, 0:512], ca, aa[:, sl], start=True, stop=False)
            m = nc.tensor.matmul(psK[0:1, 0:512], cb, bb[:, sl], start=False, stop=True)
            c.mark(m, Q + 'mm')
            c.wait(V, Q + 'mm')
            c.wait(V, Q + 'ld')
            V.tensor_copy(wr, psK[0:1, 0:512])
            c.mark(V.drain(), Q + 'vv')
            c.wait(nc.sync, Q + 'vv')
            c.dma(nc.sync, s.rowtmp[0][None, :], wr, q=Q + 'ld')
            c.chain(Q + 'ld')
            c.dma(nc.sync, k4[lc * APC:(lc + 1) * APC, ki * nw:(ki + 1) * nw],
                  s.rowtmp[0].rearrange("(a b) -> a b", b=nw), q=Q + 'ld',
                  slow=True)

    ka = k4[0:64, 0:nw]
    kb = k4[0:64, nw:2 * nw]
    kc = k4[0:64, 2 * nw:3 * nw]
    kd = k4[0:64, 3 * nw:4 * nw]
    tden = ctt[0:64, 0:nw]
    tden2 = t4[0:64, 0:nw]
    trec = rc[0:64, 0:nw]
    c.wait(V, Q + 'ld')
    V.tensor_scalar_add(kc, kc, 1.0)
    V.drain()
    V.tensor_mul(tden, kc, kc)
    V.drain()
    V.tensor_mul(tden2, kd, kd)
    V.drain()
    V.tensor_add(tden, tden, tden2)
    V.drain()
    s.recip_nr(V, trec, tden, tden2)
    V.tensor_mul(tden, ka, kc)
    V.drain()
    V.tensor_mul(tden2, kb, kd)
    V.drain()
    V.tensor_add(tden, tden, tden2)
    V.drain()
    V.tensor_mul(tden2, kb, kc)
    V.drain()
    V.tensor_mul(kb, ka, kd)
    V.drain()
    V.tensor_sub(tden2, tden2, kb)
    V.drain()
    V.tensor_mul(ka, tden, trec)
    V.drain()
    V.tensor_mul(kb, tden2, trec)
    c.mark(V.drain(), Q + 'vv')

    for lc in range(NCH):
        sl = slice(lc * 512, (lc + 1) * 512)
        c.wait(nc.sync, Q + 'vv')
        c.wait(nc.sync, Q + 'mm')
        c.dma(nc.sync, s.rowtmp[0].rearrange("(a b) -> a b", b=nw),
              ka[lc * APC:(lc + 1) * APC, :], q=Q + 'ld', slow=True)
        c.dma(nc.sync, s.rowtmp[1].rearrange("(a b) -> a b", b=nw),
              kb[lc * APC:(lc + 1) * APC, :], q=Q + 'ld', slow=True)
        c.chain(Q + 'ld')
        c.dma(nc.sync, wr, s.rowtmp[0][None, :], q=Q + 'ld')
        c.dma(nc.sync, wi, s.rowtmp[1][None, :], q=Q + 'ld')
        c.dma(nc.sync, i1r, s.cst['s4_i1z_re'][0:1, sl], q=Q + 'ld')
        c.dma(nc.sync, i1i, s.cst['s4_i1z_im'][0:1, sl], q=Q + 'ld')
        c.wait(nc.tensor, Q + 'vv')
        c.wait(nc.tensor, Q + 'ld')
        m = nc.tensor.matmul(psB[0:64, 0:512], s.onr(n=64), wr,
                             start=True, stop=True)
        c.mark(m, Q + 'mm')
        c.wait(V, Q + 'mm')
        V.tensor_copy(bt[:, :], psB[0:64, 0:512])
        c.mark(V.drain(), Q + 'vv')
        c.wait(nc.tensor, Q + 'vv')
        m = nc.tensor.matmul(psB[0:64, 0:512], s.onr(n=64), wi,
                             start=True, stop=True)
        c.mark(m, Q + 'mm')
        c.wait(V, Q + 'mm')
        V.tensor_copy(dt_[:, :], psB[0:64, 0:512])
        c.mark(V.drain(), Q + 'vv')
        V.tensor_scalar(rc[:, :], bt[:, :], Pre, None, op0=AL.mult)
        V.drain()
        V.tensor_scalar(t4[:, :], dt_[:, :], Pim, None, op0=AL.mult)
        V.drain()
        V.tensor_sub(rc[:, :], rc[:, :], t4[:, :])
        V.drain()
        V.tensor_scalar(rc[:, :], rc[:, :], -1.0, Bre, op0=AL.mult, op1=AL.add)
        V.drain()
        V.tensor_scalar(t4[:, :], dt_[:, :], Pre, None, op0=AL.mult)
        V.drain()
        V.tensor_scalar(bt[:, :], bt[:, :], Pim, None, op0=AL.mult)
        V.drain()
        V.tensor_add(t4[:, :], t4[:, :], bt[:, :])
        V.drain()
        V.tensor_scalar(t4[:, :], t4[:, :], -1.0, Bim, op0=AL.mult, op1=AL.add)
        V.drain()
        V.tensor_mul(bt[:, :], cre[:, sl], rc[:, :])
        V.drain()
        V.tensor_mul(dt_[:, :], cim[:, sl], t4[:, :])
        V.drain()
        V.tensor_sub(bt[:, :], bt[:, :], dt_[:, :])
        V.drain()
        V.tensor_mul(dt_[:, :], cre[:, sl], t4[:, :])
        V.drain()
        V.tensor_mul(rc[:, :], cim[:, sl], rc[:, :])
        V.drain()
        V.tensor_add(dt_[:, :], dt_[:, :], rc[:, :])
        c.mark(V.drain(), Q + 'vv')
        c.wait(nc.tensor, Q + 'vv')
        m = nc.tensor.matmul(psB[0:64, 0:512], s.onr(n=64), i1r,
                             start=True, stop=True)
        c.mark(m, Q + 'mm')
        c.wait(V, Q + 'mm')
        V.tensor_mul(rc[:, :], psB[0:64, 0:512], bt[:, :])
        V.drain()
        V.tensor_mul(t4[:, :], psB[0:64, 0:512], dt_[:, :])
        c.mark(V.drain(), Q + 'vv')
        c.wait(nc.tensor, Q + 'vv')
        m = nc.tensor.matmul(psB[0:64, 0:512], s.onr(n=64), i1i,
                             start=True, stop=True)
        c.mark(m, Q + 'mm')
        c.wait(V, Q + 'mm')
        V.tensor_mul(t5[:, :], psB[0:64, 0:512], dt_[:, :])
        V.drain()
        V.tensor_mul(dt_[:, :], psB[0:64, 0:512], bt[:, :])
        V.drain()
        V.tensor_sub(gmre[:, sl], rc[:, :], t5[:, :])
        V.drain()
        V.tensor_add(gmim[:, sl], t4[:, :], dt_[:, :])
        c.mark(V.drain(), Q + 'vv')

    psTr = [s.pb[3], s.pb[0]]
    c.wait(nc.tensor, Q + 'vv')
    for comp, (gmc, GTm) in enumerate(((gmre, GTmre), (gmim, GTmim))):
        for p8 in range(16):
            pt = psTr[p8 % 2]
            ev = V if p8 % 2 == 0 else S
            evq = Q + ('vv' if p8 % 2 == 0 else 'ss')
            c.wait(nc.tensor, evq)
            for pi in range(8):
                lp = p8 * 8 + pi
                c.mark(nc.tensor.transpose(
                    pt[0:QDs, 0:512].bitcast(BF16)[0:QDs, pi * 64:(pi + 1) * 64],
                    gmc[0:64, lp::128], s.id_b[0:64, 0:64]), Q + 'mm')
            c.wait(ev, Q + 'mm')
            if ev is V:
                V.tensor_copy(GTm[0:QDs, p8 * 512:(p8 + 1) * 512],
                              pt[0:QDs, 0:512].bitcast(BF16)[0:QDs, 0:512])
            else:
                S.copy(GTm[0:QDs, p8 * 512:(p8 + 1) * 512],
                       pt[0:QDs, 0:512].bitcast(BF16)[0:QDs, 0:512])
            c.mark(ev.drain(), evq)
    c.wait(nc.tensor, Q + 'vv')
    c.wait(nc.tensor, Q + 'ss')

    psYr = [s.pb[0], s.pb[3]]
    for n in range(64):
        ps = psYr[n % 2]
        ev = V if n % 2 == 0 else S
        evq = Q + ('vv' if n % 2 == 0 else 'ss')
        c.wait(nc.tensor, evq)
        nc.tensor.matmul(ps[0:128, 0:256], GTmre[0:QDs, n::64],
                         s.s4_s1a_sb[0:QDs, :], start=True, stop=False)
        m = nc.tensor.matmul(ps[0:128, 0:256], GTmim[0:QDs, n::64],
                             s.s4_s1b_sb[0:QDs, :], start=False, stop=True)
        c.mark(m, Q + 'mm')
        c.wait(ev, Q + 'mm')
        if ev is V:
            V.tensor_copy(Ygbuf[:, n * 256:(n + 1) * 256], ps[0:128, 0:256])
        else:
            S.copy(Ygbuf[:, n * 256:(n + 1) * 256], ps[0:128, 0:256])
        c.mark(ev.drain(), evq)
    c.wait(nc.tensor, Q + 'vv')
    c.wait(nc.tensor, Q + 'ss')

    psG2r = [s.pb[1], s.pb[2]]
    for t1 in range(128):
        gb = s4g[t1 % 2]
        c.wait_lag(nc.sync, Q + 'mm', 1)
        c.dma(nc.sync, gb[:, :], s.cst['s4_gall'][t1], q=Q + 'ld')
        c.wait(nc.tensor, Q + 'ld')
        ps2 = psG2r[t1 % 2]
        ev = V if t1 % 2 == 0 else S
        evq = Q + ('vv' if t1 % 2 == 0 else 'ss')
        c.wait(nc.tensor, evq)
        Ygre = Ygbuf[:, t1::256]
        Ygim = Ygbuf[:, 128 + t1::256]
        nc.tensor.matmul(ps2[0:QDs, 0:64], gb[:, 0:QDs], Ygre,
                         start=True, stop=False)
        nc.tensor.matmul(ps2[0:QDs, 0:64], gb[:, 2 * QDs:3 * QDs], Ygim,
                         start=False, stop=True)
        nc.tensor.matmul(ps2[0:QDs, 64:128], gb[:, QDs:2 * QDs], Ygre,
                         start=True, stop=False)
        m = nc.tensor.matmul(ps2[0:QDs, 64:128], gb[:, 0:QDs], Ygim,
                             start=False, stop=True)
        c.mark(m, Q + 'mm')
        c.wait(ev, Q + 'mm')
        if ev is V:
            V.tensor_copy(gbuf[0:QDs, t1 * 128:(t1 + 1) * 128], ps2[0:QDs, 0:128])
        else:
            S.copy(gbuf[0:QDs, t1 * 128:(t1 + 1) * 128], ps2[0:QDs, 0:128])
        c.mark(ev.drain(), evq)
    c.wait(nc.tensor, Q + 'vv')
    c.wait(nc.tensor, Q + 'ss')

    gfny = tstg
    emit_fft_fwd(s, Q, gbuf, 128, Ymode, Xmode, gfny[0:1, 0:128])

    psC = s.pb[2]
    ch0 = par * (D // 2)
    for cc in range(4):
        rows = slice(ch0 + cc * 128, ch0 + (cc + 1) * 128)
        c.wait(nc.sync, Q + 'mm')
        c.dma(nc.sync, cl[:, :], s.s4_C_re[rows, :], q=Q + 'ld')
        c.wait(nc.tensor, Q + 'ld')
        c.wait(nc.tensor, Q + 'vv')
        c.mark(nc.tensor.transpose(psC[0:64, 0:128], cl[:, :], s.id_f[:, :]),
               Q + 'mm')
        c.wait(V, Q + 'mm')
        V.tensor_copy(ct_sb[0:64, cc * 128:(cc + 1) * 128], psC[0:64, 0:128])
        c.mark(V.drain(), Q + 'vv')
        c.wait(nc.sync, Q + 'mm')
        c.dma(nc.sync, cl[:, :], s.s4_C_im[rows, :], q=Q + 'ld')
        c.wait(nc.tensor, Q + 'ld')
        c.wait(nc.tensor, Q + 'vv')
        c.mark(nc.tensor.transpose(psC[0:64, 0:128], cl[:, :], s.id_f[:, :]),
               Q + 'mm')
        c.wait(V, Q + 'mm')
        V.tensor_scalar(kfs[0:64, 0:128], psC[0:64, 0:128], -1.0, None,
                        op0=AL.mult)
        c.mark(V.drain(), Q + 'vv')
        c.wait(nc.sync, Q + 'vv')
        c.dma(nc.sync, ct_sb[64:128, cc * 128:(cc + 1) * 128],
              kfs[0:64, 0:128], q=Q + 'ld')
    c.wait(nc.tensor, Q + 'ld')

    psCr = [s.pb[2], s.pb[0]]
    for k1 in range(128):
        po = (k1 % PACKE) * PSTR
        g = k1 // PACKE
        for comp in range(2):
            pc2 = psCr[comp]
            ev = V if comp == 0 else S
            evq = Q + ('vv' if comp == 0 else 'ss')
            c.wait(nc.tensor, evq)
            c.mark(nc.tensor.transpose(
                pc2[0:128, 0:128].bitcast(BF16)[0:128, 0:KB],
                Xmode[po:po + KB, (g * 2 + comp) * 128:(g * 2 + comp + 1) * 128],
                s.id_b[po:po + KB, po:po + KB]), Q + 'mm')
            c.wait(ev, Q + 'mm')
            if ev is V:
                V.tensor_copy(gfT[:, (k1 * 2 + comp) * KB:(k1 * 2 + comp + 1) * KB],
                              pc2[0:128, 0:128].bitcast(BF16)[0:128, 0:KB])
            else:
                S.copy(gfT[:, (k1 * 2 + comp) * KB:(k1 * 2 + comp + 1) * KB],
                       pc2[0:128, 0:128].bitcast(BF16)[0:128, 0:KB])
            c.mark(ev.drain(), evq)
    c.wait(nc.tensor, Q + 'vv')
    c.wait(nc.tensor, Q + 'ss')
    kfs2 = A.bf16(LB + 6752, 512, 64)
    kfsr = [kfs, kfs2]
    for k1 in range(128):
        po = (k1 % PACKE) * PSTR
        g = k1 // PACKE
        for comp in range(2):
            pc2 = psCr[comp]
            ev = V if comp == 0 else S
            evq = Q + ('vv' if comp == 0 else 'ss')
            kf_b = kfsr[comp]
            c.wait(nc.tensor, evq)
            m = nc.tensor.matmul(pc2[0:KB, 0:D // 2],
                                 gfT[:, (k1 * 2 + comp) * KB:(k1 * 2 + comp + 1) * KB],
                                 ct_sb[:, :], start=True, stop=True)
            c.mark(m, Q + 'mm')
            c.wait(ev, Q + 'mm')
            c.wait(ev, Q + 'ld')
            if ev is V:
                V.tensor_copy(kf_b[0:KB, 0:D // 2], pc2[0:KB, 0:D // 2])
            else:
                S.copy(kf_b[0:KB, 0:D // 2], pc2[0:KB, 0:D // 2])
            c.mark(ev.drain(), evq)
            c.wait(nc.sync, evq)
            c.dma(nc.sync, s.kf_dram[po:po + KB, g, comp, :], kf_b[0:KB, 0:D // 2],
                  q=Q + 'ld')
    gfnyc_f = cl[:, 0:1]
    gfnyc = s.gfnyc_sb
    c.wait(nc.sync, Q + 'ld')
    c.dma(nc.sync, s.rowtmp[0][None, 0:128], gfny[0:1, 0:128], q=Q + 'ld')
    c.chain(Q + 'ld')
    c.dma(nc.sync, gfnyc_f,
          s.rowtmp[0, 0:128].rearrange("(p x) -> p x", x=1),
          q=Q + 'ld', slow=True)
    c.wait(V, Q + 'ld')
    V.tensor_copy(gfnyc[:, :], gfnyc_f)
    c.mark(V.drain(), Q + 'vv')
    c.wait(nc.tensor, Q + 'vv')
    m = nc.tensor.matmul(psC[0:1, 0:D // 2], gfnyc[:, :], ct_sb[:, :],
                         start=True, stop=True)
    c.mark(m, Q + 'mm')
    c.wait(V, Q + 'mm')
    V.tensor_copy(s.kfny_rowb[0:1, :], psC[0:1, 0:D // 2])
    c.mark(V.drain(), Q + 'vv')

_BUILD_CACHE = None


def _build():
    global _BUILD_CACHE
    if _BUILD_CACHE is None:
        mk = MKBuild(8192, 1024, NCORES=8)
        nc, consts = mk.build(tail=emit_block1)
        _BUILD_CACHE = (mk, nc, consts)
    return _BUILD_CACHE


_IN_MAP = {
    "W_rkv": "L0_attn_W_rkv", "b_rkv": "L0_attn_b_rkv",
    "W_o": "L0_attn_W_o", "b_o": "L0_attn_b_o",
    "nu_log": "L0_attn_nu_log", "gamma_log": "L0_attn_gamma_log",
    "Wg": "L0_ffn_Wg", "ln_scale": "L0_ffn_ln_scale",
    "ln_bias": "L0_ffn_ln_bias", "W1": "L0_ffn_W1", "b1": "L0_ffn_b1",
    "W2": "L0_ffn_W2", "b2": "L0_ffn_b2",
    "l1w": "L1_ffn_weight", "l1b": "L1_ffn_bias",
    "Dskip": "L1_attn_D", "s4_step": "L1_attn_step",
    "s4_C_re": "L1_attn_C_re", "s4_C_im": "L1_attn_C_im",
    "s4_B_re": "L1_attn_B_re", "s4_B_im": "L1_attn_B_im",
    "s4_P_re": "L1_attn_P_re", "s4_P_im": "L1_attn_P_im",
    "s4_Q_re": "L1_attn_Q_re", "s4_Q_im": "L1_attn_Q_im",
    "s4_diag_re": "L1_attn_diag_re", "s4_diag_im": "L1_attn_diag_im",
}


def kernel(**inputs):
    from concourse.bass_utils import run_bass_kernel_spmd
    mk, nc, consts = _build()
    x = np.ascontiguousarray(np.asarray(inputs["L0_attn_inputs"], np.float32))
    B, SEQ, D = x.shape
    T = SEQ // 2
    common = {k: np.ascontiguousarray(np.asarray(inputs[v], np.float32))
              for k, v in _IN_MAP.items()}
    common.update(consts)
    in_maps = []
    for c in range(8):
        b, half = c // 2, c % 2
        m = dict(common)
        m["x_in"] = np.ascontiguousarray(x[b, half * T:(half + 1) * T, :])
        in_maps.append(m)
    res = run_bass_kernel_spmd(nc, in_maps, core_ids=list(range(8)))
    out = np.empty((B, SEQ, D), np.float32)
    for c in range(8):
        b, half = c // 2, c % 2
        out[b, half * T:(half + 1) * T, :] = res.results[c]["y_out"]
    return out


# version 21
# speedup vs baseline: 1.0863x; 1.0863x over previous
import numpy as np
import ml_dtypes
import concourse.bass as bass
import concourse.mybir as mybir

F32 = mybir.dt.float32
BF16 = mybir.dt.bfloat16
AL = mybir.AluOpType
AF = mybir.ActivationFunctionType
AX = mybir.AxisListType
BF = ml_dtypes.bfloat16


class Ctx:

    def __init__(self, nc, stack):
        self.nc = nc
        self.stack = stack
        self.sems = {}
        self.ctr = {}
        self.last_wait = {}
        self.dma_queues = set()
        self.consts = {}
        self._cap = {}

    def sem(self, name):
        if name not in self.sems:
            self.sems[name] = self.stack.enter_context(self.nc.semaphore(name))
            self.ctr[name] = 0
        return self.sems[name]

    def dma(self, eng, out, in_, q="ld", slow=False):
        s = self.sem(q)
        self.dma_queues.add(q)
        kw = {"allow_slow_non_contiguous": True} if slow else {}
        inst = eng.dma_start(out=out, in_=in_, **kw)
        lw = self.last_wait.get(q, 0)
        if lw > 0:
            inst._wait_ge(s, lw)
        inst.then_inc(s, 16)
        self.ctr[q] += 16
        return self.ctr[q]

    def wait_lag(self, eng, q, lag):
        self.sem(q)
        v = max(0, self.ctr[q] - lag)
        self.last_wait[q] = max(self.last_wait.get(q, 0), v)
        eng.wait_ge(self.sem(q), v)

    def chain(self, q):
        self.last_wait[q] = max(self.last_wait.get(q, 0), self.ctr.get(q, 0))

    def mark(self, inst, q):
        s = self.sem(q)
        inst.then_inc(s, 1)
        self.ctr[q] += 1
        return self.ctr[q]

    def wait(self, eng, q, val=None):
        self.sem(q)
        v = self.ctr[q] if val is None else val
        self.last_wait[q] = max(self.last_wait.get(q, 0), v)
        eng.wait_ge(self.sem(q), v)

    def const(self, name, arr):
        if name in self.consts:
            return self._cap[name]
        arr = np.ascontiguousarray(arr)
        dt = {np.dtype(np.float32): F32, np.dtype(BF): BF16}[arr.dtype]
        ap = self.nc.dram_tensor("c_" + name, list(arr.shape), dt,
                                 kind="ExternalInput").ap()
        self.consts["c_" + name] = arr
        self._cap[name] = ap
        return ap

    def sbuf(self, name, shape, dtype=F32):
        if not hasattr(self, "_sbufs"):
            self._sbufs = {}
        if name not in self._sbufs:
            self._sbufs[name] = self.stack.enter_context(
                self.nc.sbuf_tensor(name, shape, dtype))
        return self._sbufs[name]

    def psum(self, name, banks=1):
        return self.stack.enter_context(
            self.nc.psum_tensor(name, [128, 512 * banks], F32))

    def barrier(self):
        self.nc.all_engine_barrier()


class Arena:

    def __init__(self, c, words, name="arena"):
        self.words = words
        self.t = c.sbuf(name, [128, words], F32)

    def f32(self, off, n, partitions=128):
        assert off + n <= self.words, (off, n, self.words)
        return self.t[0:partitions, off:off + n]

    def bf16(self, off, n_elem, partitions=128):
        assert n_elem % 2 == 0
        nw = n_elem // 2
        assert off + nw <= self.words, (off, n_elem, self.words)
        return self.t[0:partitions, off:off + nw].bitcast(BF16)

"""Megakernel: prologue + block 0 (LN -> RWKV-scan attn + MoE -> residual).

Parameterized by SEQ so small-shape sim tests and full-shape device runs share
code. mk1.py adds block 1 (S4 conv + quantized FFN).

Memory plan (per-partition words, SBUF ~53k words usable):
  arena CORE [0, AW): phase-scoped overlays
    block0: xn [0,XN_W) | xatt [XN_W,2*XN_W) | R [2*XN_W, +R_W) phase locals
    block1: chain regions [0, 32768) + locals [32768, +B1L_W)
  statics: identities/ones/cval/cols/rowscr/mask8/b2_b/... (small)
  big per-token rows live in DRAM (rowd/rowd_b/statsh) and are staged as
  [1 x 512] strips on demand.
"""
import numpy as np
import ml_dtypes
from contextlib import ExitStack

import concourse.bass as bass
import concourse.mybir as mybir
import concourse.bass_isa as bass_isa
from concourse import dve_ops
from concourse import library_config


LN_EPS = 1e-5

R_S, R_NEGM, R_S2, R_Q, R_NMQ, R_XSC, R_DS, R_NDMS = range(8)
B_NEGMS, B_NEGMSS2 = 0, 1


class MKBuild:
    def __init__(self, SEQ, D=1024, NCORES=8):
        self.SEQ, self.D, self.NC = SEQ, D, NCORES
        self.T = SEQ // 2
        self.DC = D // 128
        self.NJ = self.T // 128
        self.NT = self.T // 512
        self.SPAN = min(1024, self.T)
        self.NSPAN = self.T // self.SPAN
        self.E, self.H = 8, 256
        self.NFFT = 2 * SEQ
        self.QNZ = SEQ // 128
        self.QD = self.NFFT // 128
        self.KB = self.QD // 2
        self.PSTR = max(self.KB, 64)
        self.PACKE = 128 // self.PSTR
        self.NGRP = 128 // self.PACKE
        self.CPASS = 128
        self.NPASS = (D // 2) // self.CPASS

    def barrier_lite(self, tag='bar'):
        c, nc = self.c, self.nc
        for q in sorted(c.dma_queues - {'cw_ld', 'cr_ld'}):
            c.wait(nc.sync, q)
        sem = c.sem(tag)
        for e in self.engines:
            e.drain().then_inc(sem, 1)
        c.ctr[tag] += len(self.engines)
        for e in self.engines:
            e.wait_ge(sem, c.ctr[tag])

    def col(self, name, n=1):
        if name not in self.col_ix:
            o = self.col_used
            self.col_used += n
            assert self.col_used <= self.cols.shape[1]
            self.col_ix[name] = (o, n)
        o, n_ = self.col_ix[name]
        return self.cols[:, o:o + n_]

    def colb(self, name, n=1):
        if name not in self.colb_ix:
            o = self.colb_used
            self.colb_used += n
            assert self.colb_used <= self.colsb.shape[1]
            self.colb_ix[name] = (o, n)
        o, n_ = self.colb_ix[name]
        return self.colsb[:, o:o + n_]

    def xnc(self, ch, sl=None):
        a = self.xn_bf[:, ch * self.T:(ch + 1) * self.T]
        return a if sl is None else a[:, sl]

    def xac(self, ch, sl=None):
        a = self.xatt_bf[:, ch * self.T:(ch + 1) * self.T]
        return a if sl is None else a[:, sl]

    def cols_to_row(self, colap, rowap, ncols, pfx='', bf=False):
        c, nc = self.c, self.nc
        ps = self.pb[7]
        c.wait(nc.tensor, pfx + 'vv')
        c.mark(nc.tensor.transpose(ps[0:ncols, 0:128], colap, self.id_f[:, :]),
               pfx + 'mm')
        c.wait(nc.vector, pfx + 'mm')
        c.wait(nc.vector, pfx + 'ld')
        src = self.rowscr[0:ncols, 0:128]
        nc.vector.tensor_copy(src, ps[0:ncols, 0:128])
        if bf:
            nc.vector.drain()
            nc.vector.tensor_copy(self.rowscr_b[0:ncols, 0:128], src)
            src = self.rowscr_b[0:ncols, 0:128]
        c.mark(nc.vector.drain(), pfx + 'vv')
        c.wait(nc.sync, pfx + 'vv')
        c.dma(nc.sync, rowap, src, q=pfx + 'ld')
        c.wait(nc.sync, pfx + 'ld')

    def recip_nr(self, eng, out, in_, scratch):
        eng.reciprocal(out=out, in_=in_)
        eng.drain()

    def build(self, tail='block0'):
        s = self
        nc = bass.Bass("TRN2", target_bir_lowering=False, num_devices=s.NC)
        s.nc = nc
        s.stack = ExitStack()
        s.stack.__enter__()
        c = Ctx(nc, s.stack)
        s.c = c
        s.engines = [nc.sync, nc.vector, nc.scalar, nc.tensor, nc.gpsimd]
        T, D, DC = s.T, s.D, s.DC

        s.col_ix, s.col_used = {}, 0
        s.colb_ix, s.colb_used = {}, 0

        din = lambda n, sh: nc.dram_tensor(n, list(sh), F32, kind="ExternalInput").ap()
        s.x_in = din("x_in", [T, D])
        s.W_rkv = din("W_rkv", [D, 3 * D]); s.b_rkv = din("b_rkv", [3 * D])
        s.W_o = din("W_o", [D, D]); s.b_o = din("b_o", [D])
        s.nu_log = din("nu_log", [D]); s.gamma_log = din("gamma_log", [D])
        s.Wg = din("Wg", [8, D])
        s.ln_scale = din("ln_scale", [8, D]); s.ln_bias = din("ln_bias", [8, D])
        s.W1 = din("W1", [8, D, 256]); s.b1 = din("b1", [8, 256])
        s.W2 = din("W2", [8, 256, D]); s.b2 = din("b2", [8, D])
        s.l1w = din("l1w", [D, D]); s.l1b = din("l1b", [D])
        s.Dskip = din("Dskip", [1]); s.s4_step = din("s4_step", [1])
        s.s4_C_re = din("s4_C_re", [D, 64]); s.s4_C_im = din("s4_C_im", [D, 64])
        for t in ["B_re", "B_im", "P_re", "P_im", "Q_re", "Q_im",
                  "diag_re", "diag_im"]:
            setattr(s, "s4_" + t, din("s4_" + t, [64]))

        dram = lambda n, sh, dt=F32: nc.dram_tensor(n, sh, dt, kind="Internal").ap()
        shared = lambda n, sh, dt=F32: nc.dram_tensor(
            n, sh, dt, kind="Internal", addr_space="Shared").ap()
        s.wrkv_bf = dram("wrkv_bf", [D, 3 * D], BF16)
        s.wo_bf = dram("wo_bf", [D, D], BF16)
        s.w2_bf = dram("w2_bf", [8, 256, D], BF16)
        s.w1p = dram("w1p", [8, D, 256], BF16)
        s.xfeat = dram("xfeat", [D, T])
        s.rowd = dram("rowd", [8, T])
        s.rowd_b = dram("rowd_b", [2, T], BF16)
        s.rowtmp = dram("rowtmp", [2, 512])
        s.csrow_dram = dram("csrow_dram", [1, 3 * D], BF16)
        s.csw1_dram = dram("csw1_dram", [1, 8 * 256], BF16)
        s.wqt_dram = dram("wqt_dram", [D, D], BF16)
        s.kf_dram = dram("kf_dram", [128, s.NGRP, 2, D // 2], BF16)
        s.hfeat_dram = dram("hfeat_dram", [s.NPASS, 128, T], BF16)
        s.x1sh = shared("x1sh", [2, D, T])
        s.carrysh = shared("carrysh", [2, 128, 16])
        s.statsh = shared("statsh", [2, 2, T])
        s.hsh = shared("hsh", [2, s.NPASS, 128, T], BF16)
        s.y_out = nc.dram_tensor("y_out", [T, D], F32, kind="ExternalOutput").ap()

        s.ident_f = c.const("ident_f", np.eye(128, dtype=np.float32))
        s.id_f = c.sbuf("id_f", [128, 128])
        s.ident_b = c.const("ident_b", np.eye(128, dtype=BF))
        s.id_b = c.sbuf("id_b", [128, 128], BF16)
        s.ones_f = c.const("ones_f", np.ones((128, 128), np.float32))
        s.on_f = c.sbuf("on_f", [128, 128])
        s.ones_b = c.const("ones_b", np.ones((128, 128), dtype=BF))
        s.on_b = c.sbuf("on_b", [128, 128], BF16)
        oh = np.zeros((8, 8 * 128), dtype=BF)
        for e in range(8):
            oh[e, e * 128:(e + 1) * 128] = 1
        s.onehot_c = c.const("onehot", oh)
        s.onehot_sb = c.sbuf("onehot_sb", [8, 8 * 128], BF16)
        s.cvals = [1e-5, 1e-6, 0.0, 1.0, -1.0, 0.5]
        s.cval_const = c.const("cvals", np.tile(
            np.array(s.cvals, np.float32)[None, :], (128, 1)))
        s.cval_sb = c.sbuf("cval_sb", [128, len(s.cvals)])
        for _i, _v in enumerate(s.cvals):
            nc.const_aps.register(F32, _v, s.cval_sb[:, _i:_i + 1])

        def onr(p=0, n=128):
            return s.on_f[p:p + 1, 0:n]
        def onc(n=128):
            return s.on_f[0:n, 0:1]
        s.onr, s.onc = onr, onc
        def onrb(p=0, n=128):
            return s.on_b[p:p + 1, 0:n]
        def oncb(n=128):
            return s.on_b[0:n, 0:1]
        s.onrb, s.oncb = onrb, oncb

        s.XN_W = DC * T // 2
        s.R_W = 11264
        s.B1L_W = 7168
        s.AW = max(2 * s.XN_W + s.R_W, 32768 + s.B1L_W)
        s.arena = Arena(c, s.AW)
        A = s.arena
        s.O_R = 2 * s.XN_W
        s.xn_bf = A.bf16(0, 2 * s.XN_W)
        s.xatt_bf = A.bf16(s.XN_W, 2 * s.XN_W)

        def rf(off, n, partitions=128):
            assert off + n <= s.R_W, (off, n)
            return A.f32(s.O_R + off, n, partitions)
        def rb(off, n_elem, partitions=128):
            assert off + n_elem // 2 <= s.R_W, (off, n_elem)
            return A.bf16(s.O_R + off, n_elem, partitions)
        s.rf, s.rb = rf, rb

        s.rowscr = c.sbuf("rowscr", [128, 128])
        s.rowscr_b = c.sbuf("rowscr_b", [128, 128], BF16)
        s.stg_b = c.sbuf("stg_b", [1, 512], BF16)
        s.cols = c.sbuf("cols", [128, 768])
        s.colsb = c.sbuf("colsb", [128, 64], BF16)
        s.mask8 = c.sbuf("mask8", [8, T], BF16)
        s.cswg = c.sbuf("cswg", [1, 8])
        s.wgt_sb = c.sbuf("wgt_sb", [128, 8 * DC])
        s.b2_b = c.sbuf("b2_b", [8, D], BF16)
        s.pb = [c.psum(f"pb{i}") for i in range(8)]

        s.emit(tail)
        return nc, dict(c.consts)

    def emit(self, tail):
        s, c, nc = self, self.c, self.nc
        if True:
            self.emit_prologue()
            self.barrier_lite()
        if True:
            self.emit_stats()
            self.barrier_lite()
        if True:
            self.emit_rowmath()
            self.barrier_lite()
        if True:
            self.emit_xn_pass()
            self.barrier_lite()

        engines = self.engines
        idxs = [e.compute_val(e.partition_id() % 2) for e in engines]
        base = dict(c.ctr)
        base_lw = dict(c.last_wait)
        deltas = {}

        def run_arm(fn, tag):
            for k in list(c.ctr):
                c.ctr[k] = base.get(k, 0)
            c.last_wait.clear()
            c.last_wait.update(base_lw)
            fn()
            deltas[tag] = {k: v - base.get(k, 0) for k, v in c.ctr.items()
                           if v != base.get(k, 0)}
            for k in list(c.ctr):
                c.ctr[k] = base.get(k, 0)
            c.last_wait.clear()
            c.last_wait.update(base_lw)

        if True:
            for arm in nc.Switch(engines=engines, index=idxs, n=2):
                run_arm((lambda: self.emit_attn(0, True)) if arm == 0
                        else (lambda: self.emit_moe(1, True)),
                        'attn0' if arm == 0 else 'moe1')
            nc.all_core_barrier()
        if True:
            for arm in nc.Switch(engines=engines, index=idxs, n=2):
                run_arm((lambda: self.emit_moe(0, False)) if arm == 0
                        else (lambda: self.emit_attn(1, False)),
                        'moe0' if arm == 0 else 'attn1')

        skip = {'cw_ld', 'cr_ld'}
        da0 = {k: v for k, v in deltas['attn0'].items() if k not in skip}
        da1 = {k: v for k, v in deltas['attn1'].items() if k not in skip}
        assert da0 == da1, (da0, da1)
        assert deltas['moe0'] == deltas['moe1'], (deltas['moe0'], deltas['moe1'])
        allk = set(deltas['attn0']) | set(deltas['moe0']) | set(base)
        for k in allk:
            c.ctr[k] = base.get(k, 0) + deltas['attn0'].get(k, 0) + deltas['moe0'].get(k, 0)
            c.last_wait[k] = c.ctr[k]
        nc.all_core_barrier()

        if tail == 'block0':
            self.emit_debug_tail()
        else:
            tail(self)

    def emit_debug_tail(self):
        s, c, nc = self, self.c, self.nc
        dbg = nc.dram_tensor("dbg_x1", [s.D, s.T], F32, kind="ExternalOutput").ap()
        buf = s.rf(0, s.T)
        idxs = [nc.sync.compute_val(nc.sync.partition_id() % 2)]
        base = c.ctr.get('dbg', 0)
        delta = 0
        for arm in nc.Switch(engines=[nc.sync], index=idxs, n=2):
            c.ctr['dbg'] = base
            c.last_wait['dbg'] = base
            for ch in range(s.DC):
                c.dma(nc.sync, buf[:, :], s.x1sh[arm, ch * 128:(ch + 1) * 128, :], q='dbg')
                c.wait(nc.sync, 'dbg')
                c.dma(nc.sync, dbg[ch * 128:(ch + 1) * 128, :], buf[:, :], q='dbg')
                c.wait(nc.sync, 'dbg')
            delta = c.ctr['dbg'] - base
        c.ctr['dbg'] = base + delta
        c.last_wait['dbg'] = base + delta

    def emit_prologue(self):
        s, c, nc = self, self.c, self.nc
        D, T, DC = s.D, s.T, s.DC

        c.dma(nc.sync, s.id_f[:, :], s.ident_f[:, :])
        c.dma(nc.sync, s.id_b[:, :], s.ident_b[:, :])
        c.dma(nc.sync, s.on_f[:, :], s.ones_f[:, :])
        c.dma(nc.sync, s.on_b[:, :], s.ones_b[:, :])
        c.dma(nc.sync, s.onehot_sb[:, :], s.onehot_c[:, :])
        c.dma(nc.sync, s.cval_sb[:, :], s.cval_const[:, :])

        c.dma(nc.gpsimd, s.wrkv_bf[:, :], s.W_rkv[:, :], q='gd')
        c.dma(nc.gpsimd, s.wo_bf[:, :], s.W_o[:, :], q='gd')
        c.dma(nc.gpsimd, s.w2_bf[:, :, :], s.W2[:, :, :], q='gd')
        c.dma(nc.gpsimd, s.b2_b[:, :], s.b2[:, :], q='gd')

        nu_c = self.col("nu", 8); galog_c = self.col("galog", 8)
        s.lam_c = self.col("lam", 8); s.gam_c = self.col("gam", 8)
        nuv_c = self.col("nuv", 8)
        c.dma(nc.sync, nu_c, s.nu_log.rearrange("(c p) -> p c", p=128), slow=True)
        c.dma(nc.sync, galog_c, s.gamma_log.rearrange("(c p) -> p c", p=128), slow=True)
        s.brkv_c = self.col("brkv", 24)
        c.dma(nc.sync, s.brkv_c, s.b_rkv.rearrange("(c p) -> p c", p=128), slow=True)
        s.bo_c = self.col("bo", 8)
        c.dma(nc.sync, s.bo_c, s.b_o.rearrange("(c p) -> p c", p=128), slow=True)
        lnsc_c = self.col("lnsc", 64)
        lnbi_c = self.col("lnbi", 64)
        c.dma(nc.sync, lnsc_c, s.ln_scale.rearrange("e (c p) -> p (e c)", p=128), slow=True)
        c.dma(nc.sync, lnbi_c, s.ln_bias.rearrange("e (c p) -> p (e c)", p=128), slow=True)
        b1_c = self.col("b1c", 16)
        c.dma(nc.sync, b1_c, s.b1.rearrange("e (c p) -> p (e c)", p=128), slow=True)

        c.wait(nc.scalar, 'ld')
        nc.scalar.activation(nuv_c, nu_c, AF.Exp)
        nc.scalar.drain()
        nc.scalar.activation(s.lam_c, nuv_c, AF.Exp, scale=-1.0)
        c.mark(nc.scalar.drain(), 'ss')
        c.wait(nc.vector, 'ld')
        nc.vector.tensor_add(galog_c, galog_c, nu_c)
        c.mark(nc.vector.drain(), 'vv')
        c.wait(nc.scalar, 'vv')
        nc.scalar.activation(galog_c, galog_c, AF.Exp)
        c.mark(nc.scalar.drain(), 'ss')
        c.wait(nc.vector, 'ss')
        nc.vector.tensor_scalar_add(s.gam_c, galog_c, -1.0)
        nc.vector.drain()

        wg_sb = s.rf(0, D, 8)
        c.dma(nc.sync, wg_sb, s.Wg[:, :])
        c.wait(nc.tensor, 'ld')
        ps = s.pb[6]
        for kf in range(DC):
            c.mark(nc.tensor.transpose(ps[0:128, kf * 8:(kf + 1) * 8],
                                       wg_sb[0:8, kf * 128:(kf + 1) * 128],
                                       s.id_f[0:8, 0:8]), 'mm')
        c.wait(nc.vector, 'mm')
        nc.vector.tensor_copy(s.wgt_sb[:, :], ps[0:128, 0:8 * DC])
        c.mark(nc.vector.drain(), 'vv')
        ps2 = s.pb[5]
        c.wait(nc.tensor, 'vv')
        for kf in range(DC):
            m = nc.tensor.matmul(ps2[0:1, 0:8], s.onc(),
                                 s.wgt_sb[:, kf * 8:(kf + 1) * 8],
                                 start=(kf == 0), stop=(kf == DC - 1))
        c.mark(m, 'mm')
        c.wait(nc.vector, 'mm')
        nc.vector.tensor_copy(s.cswg[0:1, :], ps2[0:1, 0:8])
        c.mark(nc.vector.drain(), 'vv')

        self.barrier_lite()

        wbuf = [s.rb(0, 512), s.rb(256, 512)]
        ps3 = s.pb[4]
        for ob in range(3 * D // 512):
            c.wait(nc.tensor, 'vv')
            for kf in range(DC):
                b = wbuf[kf % 2]
                c.wait(nc.sync, 'mm')
                c.dma(nc.sync, b[:, :], s.wrkv_bf[kf * 128:(kf + 1) * 128,
                                                  ob * 512:(ob + 1) * 512])
                c.wait(nc.tensor, 'ld')
                m = nc.tensor.matmul(ps3[0:1, 0:512], s.oncb(), b[:, :],
                                     start=(kf == 0), stop=(kf == DC - 1))
                c.mark(m, 'mm')
            c.wait(nc.vector, 'mm')
            c.wait(nc.vector, 'ld')
            nc.vector.tensor_copy(s.stg_b[0:1, :], ps3[0:1, 0:512])
            c.mark(nc.vector.drain(), 'vv')
            c.wait(nc.sync, 'vv')
            c.dma(nc.sync, s.csrow_dram[0:1, ob * 512:(ob + 1) * 512],
                  s.stg_b[0:1, :])
        self.barrier_lite()

        w1buf2 = [s.rf(512, 256), s.rf(896, 256)]
        w1bufb2 = [s.rb(768, 256), s.rb(1152, 256)]
        s.beta_c = self.col("beta", 16)
        bia_cb = self.colb("lnbi_b", 64)
        nc.vector.tensor_copy(bia_cb, lnbi_c)
        c.mark(nc.vector.drain(), 'vv')
        c.wait(nc.tensor, 'vv')
        psb = [s.pb[3], s.pb[1]]
        psc = s.pb[2]
        for e in range(8):
            c.wait(nc.tensor, 'vv')
            for kf in range(DC):
                i = kf % 2
                c.wait_lag(nc.sync, 'vv', 1)
                c.dma(nc.sync, w1buf2[i][:, :], s.W1[e, kf * 128:(kf + 1) * 128, :])
                c.wait(nc.vector, 'ld')
                c.wait_lag(nc.vector, 'mm', 1)
                nc.vector.tensor_scalar(w1bufb2[i][:, :], w1buf2[i][:, :],
                                        lnsc_c[:, e * 8 + kf:e * 8 + kf + 1], None,
                                        op0=AL.mult)
                c.mark(nc.vector.drain(), 'vv')
                c.wait(nc.sync, 'vv')
                c.dma(nc.sync, s.w1p[e, kf * 128:(kf + 1) * 128, :], w1bufb2[i][:, :])
                c.wait(nc.tensor, 'vv')
                for hc in range(2):
                    nc.tensor.matmul(psb[hc][0:128, 0:1],
                                     w1bufb2[i][:, hc * 128:(hc + 1) * 128],
                                     bia_cb[:, e * 8 + kf:e * 8 + kf + 1],
                                     start=(kf == 0), stop=(kf == DC - 1))
                m = nc.tensor.matmul(psc[0:1, 0:256],
                                     s.oncb(), w1bufb2[i][:, :],
                                     start=(kf == 0), stop=(kf == DC - 1))
                c.mark(m, 'mm')
            c.wait(nc.vector, 'mm')
            c.wait(nc.vector, 'ld')
            nc.vector.tensor_copy(s.stg_b[0:1, 0:256], psc[0:1, 0:256])
            c.mark(nc.vector.drain(), 'vv')
            c.wait(nc.sync, 'vv')
            c.dma(nc.sync, s.csw1_dram[0:1, e * 256:(e + 1) * 256],
                  s.stg_b[0:1, 0:256])
            for hc in range(2):
                nc.vector.tensor_add(s.beta_c[:, e * 2 + hc:e * 2 + hc + 1],
                                     psb[hc][0:128, 0:1],
                                     b1_c[:, e * 2 + hc:e * 2 + hc + 1])
            c.mark(nc.vector.drain(), 'vv')
            c.wait(nc.sync, 'vv')
            c.wait(nc.tensor, 'vv')

    def emit_stats(self):
        s, c, nc = self, self.c, self.nc
        NJ, D = s.NJ, s.D
        s.mc = self.col("mc", NJ)
        s.qc = self.col("qc", NJ)
        xt = [s.rf(0, 1024), s.rf(1024, 1024)]
        sq = s.rf(2048, 1024)
        for j in range(NJ):
            b = xt[j % 2]
            if j >= 2:
                c.wait(nc.sync, 'ss', c.ctr['ss'])
                c.wait(nc.sync, 'vv', c.ctr['vv'])
            c.dma(nc.sync, b[:, :], s.x_in[j * 128:(j + 1) * 128, :])
            c.wait(nc.scalar, 'ld')
            nc.scalar.activation(sq[:, :], b[:, :], AF.Square,
                                 accum_out=s.qc[:, j:j + 1])
            c.mark(nc.scalar.drain(), 'ss')
            c.wait(nc.vector, 'ld')
            nc.vector.reduce_sum(s.mc[:, j:j + 1], b[:, :], axis=AX.X)
            c.mark(nc.vector.drain(), 'vv')

    def emit_rowmath(self):
        s, c, nc = self, self.c, self.nc
        NJ, D, T = s.NJ, s.D, s.T
        col = self.col
        m = col("m", NJ); v = col("v", NJ); sc = col("sc", NJ)
        s2c = col("s2c", NJ); t0 = col("t0", NJ); t1 = col("t1", NJ)
        V, S = nc.vector, nc.scalar
        V.tensor_scalar(m, s.mc, 1.0 / D, None, op0=AL.mult)
        V.drain()
        V.tensor_scalar(t0, s.qc, 1.0 / D, None, op0=AL.mult)
        V.drain()
        V.tensor_mul(t1, m, m)
        V.drain()
        V.tensor_sub(v, t0, t1)
        c.mark(V.drain(), 'vv')
        c.wait(S, 'vv')
        S.activation(t0, v, AF.Sqrt, bias=float(LN_EPS))
        c.mark(S.drain(), 'ss')
        c.wait(V, 'ss')
        self.recip_nr(V, sc, t0, t1)
        V.tensor_mul(t0, sc, sc)
        V.drain()
        V.tensor_mul(t0, t0, v)
        c.mark(V.drain(), 'vv')
        c.wait(S, 'vv')
        S.activation(t0, t0, AF.Sqrt, bias=float(LN_EPS))
        c.mark(S.drain(), 'ss')
        c.wait(V, 'ss')
        self.recip_nr(V, s2c, t0, t1)
        nms = col("nms", NJ); nmss2 = col("nmss2", NJ); nm = col("nm", NJ)
        ss2 = col("ss2", NJ)
        V.tensor_mul(nms, m, sc)
        V.drain()
        V.tensor_scalar(nms, nms, -1.0, None, op0=AL.mult)
        V.drain()
        V.tensor_mul(ss2, sc, s2c)
        V.drain()
        V.tensor_mul(nmss2, m, ss2)
        V.drain()
        V.tensor_scalar(nmss2, nmss2, -1.0, None, op0=AL.mult)
        V.drain()
        V.tensor_scalar(nm, m, -1.0, None, op0=AL.mult)
        c.mark(V.drain(), 'vv')
        self.barrier_lite()
        rr = lambda i: s.rowd[i].rearrange("(j p) -> j p", p=128)
        rrb = lambda i: s.rowd_b[i].rearrange("(j p) -> j p", p=128)
        self.cols_to_row(sc, rr(R_S), NJ)
        self.cols_to_row(s2c, rr(R_S2), NJ)
        self.cols_to_row(nm, rr(R_NEGM), NJ)
        self.cols_to_row(nms, rrb(B_NEGMS), NJ, bf=True)
        self.cols_to_row(nmss2, rrb(B_NEGMSS2), NJ, bf=True)

    def emit_xn_pass(self):
        s, c, nc = self, self.c, self.nc
        NJ, D, DC, T = s.NJ, s.D, s.DC, s.T
        V, G = nc.vector, nc.gpsimd
        xt = [s.rf(0, 1024), s.rf(1024, 1024)]
        sB = s.rf(2048, 128)
        xf = [s.rf(2176 + i * 128, 128) for i in range(4)]
        sseg = s.rf(2688, 128, 1)
        nseg = s.rf(2816, 128, 1)
        lg8 = s.rf(2944, 128, 8)
        t8 = s.rf(3072, 8)
        maxc = s.rf(3080, 1)
        psT = [s.pb[i] for i in range(4)]
        psS = s.pb[4]
        psL = s.pb[5]
        c.sem('gg')
        for j in range(NJ):
            b = xt[j % 2]
            c.dma(nc.sync, b[:, :], s.x_in[j * 128:(j + 1) * 128, :])
            c.dma(nc.sync, sseg, s.rowd[R_S, j * 128:(j + 1) * 128][None, :])
            c.dma(nc.sync, nseg, s.rowd[R_NEGM, j * 128:(j + 1) * 128][None, :])
            c.wait(nc.tensor, 'vv')
            c.wait(nc.tensor, 'ld')
            c.mark(nc.tensor.matmul(psS[0:128, 0:128], s.onr(), sseg,
                                    start=True, stop=True), 'mm')
            c.wait(nc.vector, 'mm')
            nc.vector.tensor_copy(sB[:, :], psS[0:128, 0:128])
            c.mark(nc.vector.drain(), 'vv')
            for ch in range(DC):
                pt = psT[ch % 4]
                c.wait(nc.tensor, 'vv')
                c.wait(nc.tensor, 'ss')
                c.mark(nc.tensor.transpose(pt[0:128, 0:128],
                                           b[:, ch * 128:(ch + 1) * 128],
                                           s.id_f[:, :]), 'mm')
                c.wait(nc.vector, 'mm')
                nc.vector.tensor_mul(s.xnc(ch)[:, j * 128:(j + 1) * 128],
                                     pt[0:128, 0:128], sB[:, :])
                c.mark(nc.vector.drain(), 'vv')
                c.wait(nc.scalar, 'mm')
                c.wait(nc.scalar, 'ld')
                xfb = xf[ch % 4]
                nc.scalar.copy(xfb[:, :], pt[0:128, 0:128])
                c.mark(nc.scalar.drain(), 'ss')
                c.wait(nc.sync, 'ss')
                c.dma(nc.sync, s.xfeat[ch * 128:(ch + 1) * 128,
                                       j * 128:(j + 1) * 128], xfb[:, :])
                c.wait(nc.tensor, 'ss')
                mmi = nc.tensor.matmul(psL[0:8, 0:128], s.wgt_sb[:, ch * 8:(ch + 1) * 8],
                                       xfb[:, :], start=(ch == 0), stop=False)
            mmi = nc.tensor.matmul(psL[0:8, 0:128], s.cswg[0:1, :],
                                   nseg, start=False, stop=True)
            c.mark(mmi, 'mm')
            c.wait(V, 'mm')
            V.tensor_copy(lg8[:, :], psL[0:8, 0:128])
            c.mark(V.drain(), 'vv')
            c.wait(nc.tensor, 'vv')
            c.mark(nc.tensor.transpose(psS[0:128, 0:8], lg8[:, :],
                                       s.id_f[0:8, 0:8]), 'mm')
            c.wait(V, 'mm')
            V.tensor_reduce(maxc, psS[0:128, 0:8], axis=AX.X, op=AL.max)
            V.drain()
            V.tensor_scalar(t8, psS[0:128, 0:8], maxc, None, op0=AL.is_ge)
            c.mark(V.drain(), 'vv')
            c.wait(nc.tensor, 'vv')
            c.mark(nc.tensor.transpose(psS[0:8, 0:128], t8,
                                       s.id_f[:, :]), 'mm')
            c.wait(V, 'mm')
            V.tensor_copy(s.mask8[:, j * 128:(j + 1) * 128], psS[0:8, 0:128])
            c.mark(V.drain(), 'vv')
            c.wait(nc.sync, 'vv')

    def emit_attn(self, parity, first):
        s, c, nc = self, self.c, self.nc
        T, D, DC, NT, SPAN, NSPAN = s.T, s.D, s.DC, s.NT, s.SPAN, s.NSPAN
        V, S = nc.vector, nc.scalar
        NTS = SPAN // 512 if SPAN >= 512 else 1
        SP2 = SPAN // 2

        o = 0
        def nx(n):
            nonlocal o
            r = o; o += n; return r
        wbuf = s.rb(nx(1536), 3072)
        csr = s.rb(nx(1536), 3 * D, 1)
        negms = s.rb(nx(T // 2), T, 1)
        kbf = s.rb(nx(SP2), SPAN)
        kvbf = s.rb(nx(SP2), SPAN)
        u1bf = s.rb(nx(SP2), SPAN)
        rbf = s.rb(nx(SP2), SPAN)
        u2f = s.rf(nx(SPAN), SPAN)
        rf1 = s.rf(nx(SPAN), SPAN)
        wob = s.rb(nx(512), 1024)
        xftile = s.rf(nx(512), 512)
        x1t = s.rf(nx(512), 512)
        cgat = s.rf(nx(16), 16)
        ctile = s.rf(nx(16), 16)
        cc = s.rf(nx(2), 2)
        assert o <= s.R_W, o

        c.dma(nc.sync, csr, s.csrow_dram[0:1, :], q='a_ld')
        c.dma(nc.sync, negms, s.rowd_b[B_NEGMS:B_NEGMS + 1, :], q='a_ld')
        if parity == 1:
            c.dma(nc.sync, ctile[:, :], s.carrysh[0, :, :], q='cr_ld')
            c.wait(nc.vector, 'cr_ld', c.ctr['cr_ld'])

        psRsets = [[s.pb[0], s.pb[1], s.pb[2]], [s.pb[3], s.pb[4], s.pb[5]]]

        til = 0
        for ch in range(DC):
            for oi, oc in enumerate([ch, 8 + ch, 16 + ch]):
                c.dma(nc.sync,
                      wbuf[:, oi * 8 * 128:(oi + 1) * 8 * 128].rearrange(
                          "p (kf o) -> p kf o", o=128),
                      s.wrkv_bf[:, oc * 128:(oc + 1) * 128].rearrange(
                          "(kf p) o -> p kf o", p=128), q='a_ld')
            for half in range(NSPAN):
                for ntl in range(NTS):
                    t0 = half * SPAN + ntl * 512
                    sl_ps = slice(0, 512)
                    sl_sp = slice(ntl * 512, ntl * 512 + 512)
                    psR = psRsets[til % 2]
                    til += 1
                    c.wait(nc.tensor, 'a_ld')
                    c.wait_lag(nc.tensor, 'a_vv', 1)
                    for oi in range(3):
                        for kf in range(DC):
                            mmi = nc.tensor.matmul(
                                psR[oi][:, sl_ps],
                                wbuf[:, (oi * 8 + kf) * 128:(oi * 8 + kf + 1) * 128],
                                s.xnc(kf)[:, t0:t0 + 512],
                                start=(kf == 0), stop=False)
                        mmi = nc.tensor.matmul(
                            psR[oi][:, sl_ps],
                            csr[0:1, (oi * 8 + ch) * 128:(oi * 8 + ch + 1) * 128],
                            negms[0:1, t0:t0 + 512],
                            start=False, stop=True)
                    c.mark(mmi, 'a_mm')
                    c.wait(S, 'a_mm')
                    c.wait(S, 'a_vv')
                    S.activation(rbf[:, sl_sp], psR[0][:, sl_ps], AF.Sigmoid,
                                 bias=s.brkv_c[:, ch:ch + 1])
                    S.drain()
                    S.activation(kbf[:, sl_sp], psR[1][:, sl_ps], AF.Exp,
                                 bias=s.brkv_c[:, 8 + ch:8 + ch + 1])
                    c.mark(S.drain(), 'a_ss')
                    c.wait(V, 'a_ss')
                    V.scalar_tensor_tensor(kvbf[:, sl_sp], psR[2][:, sl_ps],
                                           s.brkv_c[:, 16 + ch:16 + ch + 1],
                                           kbf[:, sl_sp], op0=AL.add, op1=AL.mult)
                    c.mark(V.drain(), 'a_vv')
                if half == 0:
                    if parity == 0:
                        init_kv, init_k = 0.0, 0.0
                    else:
                        init_kv = ctile[:, ch:ch + 1]
                        init_k = ctile[:, 8 + ch:8 + ch + 1]
                else:
                    init_kv = cc[:, 0:1]
                    init_k = cc[:, 1:2]
                lamb = s.lam_c[:, ch:ch + 1].broadcast_to([128, SPAN])
                V.tensor_tensor_scan(u1bf[:, :], lamb, kvbf[:, :], init_kv,
                                     op0=AL.mult, op1=AL.add)
                V.drain()
                V.tensor_tensor_scan(u2f[:, :], lamb, kbf[:, :], init_k,
                                     op0=AL.mult, op1=AL.add)
                V.drain()
                if half < NSPAN - 1:
                    V.tensor_copy(cc[:, 0:1], u1bf[:, SPAN - 1:SPAN])
                    V.tensor_copy(cc[:, 1:2], u2f[:, SPAN - 1:SPAN])
                    V.drain()
                if parity == 0 and half == NSPAN - 1:
                    V.tensor_copy(cgat[:, ch:ch + 1], u1bf[:, SPAN - 1:SPAN])
                    V.tensor_copy(cgat[:, 8 + ch:8 + ch + 1], u2f[:, SPAN - 1:SPAN])
                    V.drain()
                V.scalar_tensor_tensor(u1bf[:, :], kvbf[:, :],
                                       s.gam_c[:, ch:ch + 1], u1bf[:, :],
                                       op0=AL.mult, op1=AL.add)
                V.drain()
                V.scalar_tensor_tensor(u2f[:, :], kbf[:, :],
                                       s.gam_c[:, ch:ch + 1], u2f[:, :],
                                       op0=AL.mult, op1=AL.add)
                V.drain()
                V.tensor_scalar_add(u2f[:, :], u2f[:, :], 1e-6)
                V.drain()
                V.reciprocal(out=rf1[:, :], in_=u2f[:, :])
                V.drain()
                V.tensor_mul(u1bf[:, :], u1bf[:, :], rf1[:, :])
                V.drain()
                sl = slice(half * SPAN, (half + 1) * SPAN)
                V.tensor_mul(s.xac(ch)[:, sl], u1bf[:, :], rbf[:, :])
                c.mark(V.drain(), 'a_vv')
            c.wait(nc.sync, 'a_mm')

        if parity == 0:
            c.wait(nc.sync, 'a_vv')
            c.dma(nc.sync, s.carrysh[0, :, :], cgat[:, :], q='cw_ld')
            c.wait(nc.sync, 'cw_ld', c.ctr['cw_ld'])

        self.barrier_lite('a_bar')

        psW = s.pb[3]
        for dc in range(DC):
            c.dma(nc.sync, wob[:, :].rearrange("p (ku o) -> p ku o", o=128),
                  s.wo_bf[:, dc * 128:(dc + 1) * 128].rearrange(
                      "(ku p) o -> p ku o", p=128), q='a_ld')
            for nt in range(NT):
                src_ap = (s.xfeat[dc * 128:(dc + 1) * 128, nt * 512:(nt + 1) * 512]
                          if first else
                          s.x1sh[parity, dc * 128:(dc + 1) * 128, nt * 512:(nt + 1) * 512])
                c.dma(nc.sync, xftile[:, :], src_ap, q='a_ld')
                c.wait(nc.tensor, 'a_ld')
                c.wait(nc.tensor, 'a_vv')
                for ku in range(DC):
                    mmi = nc.tensor.matmul(psW[:, 0:512],
                                           wob[:, ku * 128:(ku + 1) * 128],
                                           s.xac(ku)[:, nt * 512:(nt + 1) * 512],
                                           start=(ku == 0), stop=(ku == DC - 1))
                c.mark(mmi, 'a_mm')
                c.wait(V, 'a_mm')
                c.wait(V, 'a_ld')
                V.scalar_tensor_tensor(x1t[:, :], psW[:, 0:512],
                                       s.bo_c[:, dc:dc + 1], xftile[:, :],
                                       op0=AL.add, op1=AL.add)
                c.mark(V.drain(), 'a_vv')
                c.wait(nc.sync, 'a_vv')
                c.dma(nc.sync, s.x1sh[parity, dc * 128:(dc + 1) * 128,
                                      nt * 512:(nt + 1) * 512], x1t[:, :], q='a_ld')
        self.barrier_lite('a_bar')

    def emit_moe(self, parity, first):
        s, c, nc = self, self.c, self.nc
        T, D, DC, NT = s.T, s.D, s.DC, s.NT
        V, S = nc.vector, nc.scalar

        o = 0
        def nx(n):
            nonlocal o
            r = o; o += n; return r
        xhat = s.rb(nx(DC * 256), DC * 512)
        w1b = s.rb(nx(1024), 2048)
        w2b = s.rb(nx(1024), 2048)
        mh = s.rb(nx(4096), 8192)
        s2B = s.rf(nx(512), 512)
        mskB = s.rb(nx(256), 512)
        x1t = s.rf(nx(512), 512)
        csw1 = s.rb(nx(1024), 2048, 1)
        s2seg = s.rf(nx(512), 512, 1)
        ns2seg = s.rb(nx(256), 512, 1)
        assert o <= s.R_W

        c.dma(nc.sync, csw1, s.csw1_dram[0:1, :], q='m_ld')

        psB = s.pb[0]
        psH = s.pb[1]
        psH2 = s.pb[2]
        psO = s.pb[3]

        for nt in range(NT):
            tsl = slice(nt * 512, (nt + 1) * 512)
            c.wait(nc.tensor, 'm_ld')
            c.dma(nc.sync, s2seg, s.rowd[R_S2, tsl][None, :], q='m_ld')
            c.dma(nc.sync, ns2seg, s.rowd_b[B_NEGMSS2, tsl][None, :], q='m_ld')
            c.wait(nc.tensor, 'm_vv')
            c.wait(nc.tensor, 'm_ld')
            c.mark(nc.tensor.matmul(psB[:, 0:512], s.onr(), s2seg,
                                    start=True, stop=True), 'm_mm')
            c.wait(V, 'm_mm')
            V.tensor_copy(s2B[:, :], psB[:, 0:512])
            c.mark(V.drain(), 'm_vv')
            c.wait(V, 'm_vv')
            for ch in range(DC):
                V.tensor_mul(xhat[:, ch * 512:(ch + 1) * 512],
                             s.xnc(ch)[:, tsl], s2B[:, :])
            c.mark(V.drain(), 'm_vv')
            def w1_load(e, hh):
                c.dma(nc.sync,
                      w1b[:, hh * 1024:(hh + 1) * 1024].rearrange(
                          "p (kf h) -> p kf h", h=256),
                      s.w1p[e, hh * 512:(hh + 1) * 512].rearrange(
                          "(kf p) h -> p kf h", p=128), q='m_ld')
            c.wait(nc.sync, 'm_mm')
            w1_load(0, 0)
            for e in range(8):
                c.wait(nc.tensor, 'm_vv')
                c.mark(nc.tensor.matmul(psB[:, 0:512],
                                        s.onehot_sb[:, e * 128:(e + 1) * 128],
                                        s.mask8[:, tsl],
                                        start=True, stop=True), 'm_mm')
                c.wait(V, 'm_mm')
                V.tensor_copy(mskB[:, :], psB[:, 0:512])
                c.mark(V.drain(), 'm_vv')
                for hh in range(2):
                    c.wait(nc.tensor, 'm_ld')
                    c.wait(nc.sync, 'm_mm')
                    if hh == 0:
                        w1_load(e, 1)
                    elif e < 7:
                        w1_load(e + 1, 0)
                    last = None
                    for kf in range(hh * 4, hh * 4 + 4):
                        for hc, psh in ((0, psH), (1, psH2)):
                            last = nc.tensor.matmul(
                                psh[:, 0:512],
                                w1b[:, (kf * 2 + hc) * 128:(kf * 2 + hc + 1) * 128],
                                xhat[:, kf * 512:(kf + 1) * 512],
                                start=(kf == 0), stop=False)
                    c.mark(last, 'm_mm')
                for hc, psh in ((0, psH), (1, psH2)):
                    mmi = nc.tensor.matmul(
                        psh[:, 0:512],
                        csw1[0:1, e * 256 + hc * 128:e * 256 + hc * 128 + 128],
                        ns2seg, start=False, stop=True)
                    c.mark(mmi, 'm_mm')
                    c.wait(S, 'm_mm')
                    S.activation(mh[:, (e * 2 + hc) * 512:(e * 2 + hc + 1) * 512],
                                 psh[:, 0:512], AF.Gelu,
                                 bias=s.beta_c[:, e * 2 + hc:e * 2 + hc + 1])
                    c.mark(S.drain(), 'm_ss')
                c.wait(V, 'm_ss')
                for hc in range(2):
                    sl = slice((e * 2 + hc) * 512, (e * 2 + hc + 1) * 512)
                    V.tensor_mul(mh[:, sl], mh[:, sl], mskB[:, :])
                c.mark(V.drain(), 'm_vv')
                c.wait(nc.tensor, 'm_vv')
            def w2_load(dc_, hh):
                c.dma(nc.sync,
                      w2b[:, hh * 1024:(hh + 1) * 1024].rearrange(
                          "p (e hc o) -> p e hc o", e=4, hc=2),
                      s.w2_bf[hh * 4:(hh + 1) * 4, :,
                              dc_ * 128:(dc_ + 1) * 128].rearrange(
                          "e (hc p) o -> p e hc o", p=128), q='m_ld')
            c.wait(nc.sync, 'm_mm')
            w2_load(0, 0)
            for dc in range(DC):
                rsrc = (s.xfeat[dc * 128:(dc + 1) * 128, tsl] if first else
                        s.x1sh[parity, dc * 128:(dc + 1) * 128, tsl])
                c.wait(nc.sync, 'm_ld')
                c.dma(nc.sync, x1t[:, :], rsrc, q='m_ld')
                c.wait(nc.tensor, 'm_vv')
                for hh in range(2):
                    c.wait(nc.tensor, 'm_ld')
                    c.wait(nc.sync, 'm_mm')
                    if hh == 0:
                        w2_load(dc, 1)
                    elif dc < DC - 1:
                        w2_load(dc + 1, 0)
                    last = None
                    for ei in range(hh * 8, hh * 8 + 8):
                        last = nc.tensor.matmul(psO[:, 0:512],
                                                w2b[:, ei * 128:(ei + 1) * 128],
                                                mh[:, ei * 512:(ei + 1) * 512],
                                                start=(ei == 0), stop=False)
                    c.mark(last, 'm_mm')
                mmi = nc.tensor.matmul(psO[:, 0:512],
                                       s.b2_b[:, dc * 128:(dc + 1) * 128],
                                       s.mask8[:, tsl], start=False, stop=True)
                c.mark(mmi, 'm_mm')
                c.wait(V, 'm_mm')
                c.wait(V, 'm_ld')
                V.tensor_add(x1t[:, :], psO[:, 0:512], x1t[:, :])
                c.mark(V.drain(), 'm_vv')
                c.wait(nc.sync, 'm_vv')
                c.dma(nc.sync, s.x1sh[parity, dc * 128:(dc + 1) * 128, tsl],
                      x1t[:, :], q='m_ld')
                c.wait(nc.tensor, 'm_vv')
        self.barrier_lite('m_bar')

"""Block 1 emitters: L1 stats, w_q prep, S4 kernel chain, FFT conv, epilogue.

Emitted as the tail of mk0.MKBuild. All big buffers are overlaid views into
s.arena CORE [0, 32768) plus a locals region [32768, +B1L_W); per-token rows
live in DRAM (rowd / statsh) and are staged as [1 x 512] strips.

Conventions (full-size values in brackets):
  QNZ = SEQ/128 [64]   slow time digit (nonzero range), also inverse q range
  KB  = NFFT/256 [64]  k2 range; k = k1 + 128*k2, k1 in [0,128), k2 in [0,KB)
  PACKE = 128//PSTR [2] k1 values packed per 128 partitions, PSTR=max(KB,32)
  CPASS [128]          conv channels per pass; NPASS = (D/2)/CPASS [4]
X layout: [128 x (NGRP x 2comp x CPASS)] bf16, k1 at partition offset
  (k1%PACKE)*PSTR, column group g=k1//PACKE, comp-major then channel.
Y layout: [128p x (ch x 256)] bf16 (per ch: k1re 0:128 | k1im 128:256).
G layout: [128ch x (k1 x 2comp x 128p)] bf16 -> GT[k1 x (p x ch)] via transpose.
"""
import numpy as np
import ml_dtypes
import concourse.bass as bass
import concourse.mybir as mybir
import concourse.bass_isa as bass_isa
from concourse import dve_ops


LN_EPS = 1e-5
R_S, R_NEGM, R_S2, R_Q, R_NMQ, R_XSC, R_DS, R_NDMS = range(8)


def mk1_constants(s, c):
    SEQ, NFFT = s.SEQ, 2 * s.SEQ
    QNZ = SEQ // 128
    QD = NFFT // 128
    KB = QD // 2
    QDs = SEQ // 128
    k1v = np.arange(128)
    pv = np.arange(128)
    qv = np.arange(QNZ)
    k2v = np.arange(KB)

    w1 = np.exp(-2j * np.pi * np.outer(qv, k1v % QD) / QD)
    cst = {}
    cst['f_s1'] = np.concatenate([w1.real, w1.imag], axis=1).astype(BF)

    m_all = (np.exp(-2j * np.pi * np.outer(pv, k1v) / NFFT)[:, :, None]
             * np.exp(-2j * np.pi * np.outer(pv, k2v) / QD)[:, None, :])
    _mre = np.transpose(m_all.real, (1, 0, 2))
    _mim = np.transpose(m_all.imag, (1, 0, 2))
    cst['f_mall'] = np.concatenate([_mre, _mim, -_mim], axis=2).astype(BF)
    cst['f_mre'] = _mre.astype(BF)
    cst['f_mim'] = _mim.astype(BF)
    cst['f_mnim'] = (-_mim).astype(BF)

    ia = (np.exp(2j * np.pi * np.outer(k2v, pv) / QD)[None, :, :]
          * np.exp(2j * np.pi * k1v[:, None, None] * pv[None, None, :] / NFFT))
    ia = ia * (2.0 / NFFT)
    ia[0, 0, :] *= 0.5
    PSTR = max(KB, 64)
    PACKE = 128 // PSTR

    def dup(a):
        out = np.zeros((a.shape[0], 128, a.shape[2]), np.float64)
        for g in range(PACKE):
            out[:, g * PSTR:g * PSTR + KB, :] = a
        return out
    cst['i_all'] = np.concatenate(
        [dup(ia.real), dup(ia.imag), dup(-ia.imag)], axis=2).astype(BF)
    cst['i_re'] = dup(ia.real).astype(BF)
    cst['i_im'] = dup(ia.imag).astype(BF)
    cst['i_nim'] = dup(-ia.imag).astype(BF)

    wb = np.exp(2j * np.pi * np.outer(k1v, np.arange(QNZ)) / QD)
    cst['b_re'] = wb.real.astype(BF)
    cst['b_im'] = (-wb.imag).astype(BF)
    cst['sign_col'] = (((-1.0) ** pv)[:, None]).astype(BF)
    cst['sign_row_n'] = ((((-1.0) ** pv) / NFFT)[None, :]).astype(BF)

    lv = np.arange(SEQ)
    ang = (-2 * np.pi * lv / SEQ).astype(np.float32)
    zl = (np.cos(ang) + 1j * np.sin(ang)).astype(np.complex64)
    cst['s4_timag'] = ((1 - zl) / (1 + zl)).imag.astype(np.float32)[None, :]
    cst['s4_i1z_re'] = (2.0 / (1 + zl)).real.astype(np.float32)[None, :]
    cst['s4_i1z_im'] = (2.0 / (1 + zl)).imag.astype(np.float32)[None, :]

    lqv = np.arange(QDs)
    t1v = np.arange(128)
    s4w1 = np.exp(2j * np.pi * np.outer(lqv, t1v % QDs) / QDs)
    cst['s4_s1a'] = np.concatenate([s4w1.real, s4w1.imag], axis=1).astype(BF)
    cst['s4_s1b'] = np.concatenate([-s4w1.imag, s4w1.real], axis=1).astype(BF)
    t2v = np.arange(QDs)
    m2 = (np.exp(2j * np.pi * np.outer(pv, t1v) / SEQ)[:, :, None]
          * np.exp(2j * np.pi * np.outer(pv, t2v) / QDs)[:, None, :]) / SEQ
    _gre = np.transpose(m2.real, (1, 0, 2))
    _gim = np.transpose(m2.imag, (1, 0, 2))
    cst['s4_gall'] = np.concatenate([_gre, _gim, -_gim], axis=2).astype(BF)
    cst['s4_g_re'] = _gre.astype(BF)
    cst['s4_g_im'] = _gim.astype(BF)
    cst['s4_g_nim'] = (-_gim).astype(BF)

    aps = {}
    for k, v in cst.items():
        aps[k] = c.const(k, v)
    return aps


def mk1_statics(s):
    c = s.c
    QDs = s.SEQ // 128
    s.fft_s1_sb = c.sbuf("fft_s1_sb", [QDs, 256], BF16)
    s.s4_s1a_sb = c.sbuf("s4_s1a_sb", [QDs, 256], BF16)
    s.s4_s1b_sb = c.sbuf("s4_s1b_sb", [QDs, 256], BF16)
    s.sgncol_sb = c.sbuf("sgncol_sb", [128, 1], BF16)
    s.sgnrow_sb = c.sbuf("sgnrow_sb", [1, 128], BF16)
    s.fftb_re_sb = c.sbuf("fftb_re_sb", [128, QDs], BF16)
    s.fftb_im_sb = c.sbuf("fftb_im_sb", [128, QDs], BF16)
    s.kfny_rowb = c.sbuf("kfny_rowb", [1, s.D // 2], BF16)
    s._fwd_mbuf = [c.sbuf(f"fwd_mbuf{i}", [128, 3 * s.KB], BF16) for i in range(2)]
    s._cv_iab = [c.sbuf(f"cv_ia{i}", [128, 3 * 128], BF16) for i in range(2)]
    s.sny_sb = c.sbuf("sny_sb", [1, s.D // 2])
    s.snyb_sb = c.sbuf("snyb_sb", [1, s.D // 2], BF16)
    s.pnyb_sb = c.sbuf("pnyb_sb", [1, s.D // 2], BF16)
    s.gfnyc_sb = c.sbuf("gfnyc_sb", [128, 1], BF16)


def emit_block1(s):
    c, nc = s.c, s.nc
    s.cst = mk1_constants(s, c)
    mk1_statics(s)

    engines = s.engines
    idxs = [e.compute_val(e.partition_id() % 2) for e in engines]

    def switched(fn_by_parity, tags):
        base = dict(c.ctr)
        base_lw = dict(c.last_wait)
        deltas = {}
        for arm in nc.Switch(engines=engines, index=idxs, n=2):
            for k in list(c.ctr):
                c.ctr[k] = base.get(k, 0)
            c.last_wait.clear(); c.last_wait.update(base_lw)
            fn_by_parity[arm]()
            deltas[arm] = {k: v - base.get(k, 0) for k, v in c.ctr.items()
                           if v != base.get(k, 0)}
            for k in list(c.ctr):
                c.ctr[k] = base.get(k, 0)
            c.last_wait.clear(); c.last_wait.update(base_lw)
        assert deltas.get(0, {}) == deltas.get(1, {}), (tags, deltas)
        for k in set(deltas.get(0, {})) | set(base):
            c.ctr[k] = base.get(k, 0) + deltas.get(0, {}).get(k, 0)
            c.last_wait[k] = c.ctr[k]

    if True:
        switched({0: lambda: b1_stats(s, 0), 1: lambda: b1_stats(s, 1)}, 'stats')
        nc.all_core_barrier()
    if True:
        switched({0: lambda: b1_conv(s, 0), 1: lambda: b1_conv(s, 1)}, 'conv')
        nc.all_core_barrier()
    if True:
        switched({0: lambda: b1_final(s, 0), 1: lambda: b1_final(s, 1)}, 'final')


def b1_stats(s, par):
    c, nc = s.c, s.nc
    T, D, DC, NT, NJ, SEQ = s.T, s.D, s.DC, s.NT, s.NJ, s.SEQ
    V, S, G = nc.vector, nc.scalar, nc.gpsimd
    A = s.arena
    Q = 'q_'

    xt = [A.f32(0, 512), A.f32(512, 512)]
    sq = A.f32(1024, 512)
    rmx = A.f32(1536, 512)
    rmn = A.f32(2048, 512)
    neg = A.f32(2560, 512)
    strip = [A.f32(3072, 512, 1), A.f32(3584, 512, 1)]
    coltile = A.f32(4096, 16 * NJ)
    W0 = 4096 + 16 * NJ
    wt = A.f32(W0, 1024)
    wq16 = A.bf16(W0 + 1024, 1024)
    wqt_t = A.bf16(W0 + 1536, 128)
    sqw = A.f32(W0 + 1600, 1024)
    rndf = A.f32(W0 + 2624, 1024)
    dskq = A.f32(W0 + 3648, 32, 1)

    def cq(i):
        return coltile[:, i * NJ:(i + 1) * NJ]
    SX, QX, MX, MN, M1, S1, T0, T1, QQ, NMQ, XSC, DS, NDMS, NMS1, DSKC = range(15)

    psA = s.pb[0]
    psB = s.pb[1]
    for nt in range(NT):
        tsl = slice(nt * 512, (nt + 1) * 512)
        for ch in range(DC):
            b = xt[ch % 2]
            c.wait(nc.sync, Q + 'ss')
            c.wait(nc.sync, Q + 'vv')
            c.dma(nc.sync, b[:, :], s.x1sh[par, ch * 128:(ch + 1) * 128, tsl],
                  q=Q + 'ld')
            c.wait(S, Q + 'ld')
            c.wait(S, Q + 'mm')
            S.activation(sq[:, :], b[:, :], AF.Square)
            c.mark(S.drain(), Q + 'ss')
            c.wait(nc.tensor, Q + 'ss')
            nc.tensor.matmul(psA[0:1, 0:512], s.onc(), b[:, :],
                             start=(ch == 0), stop=(ch == DC - 1))
            m = nc.tensor.matmul(psB[0:1, 0:512], s.onc(), sq[:, :],
                                 start=(ch == 0), stop=(ch == DC - 1))
            c.mark(m, Q + 'mm')
            c.wait(V, Q + 'ld')
            if ch == 0:
                V.tensor_copy(rmx[:, :], b[:, :])
                V.drain()
                V.tensor_scalar(rmn[:, :], b[:, :], -1.0, None, op0=AL.mult)
            else:
                V.tensor_tensor(rmx[:, :], rmx[:, :], b[:, :], op=AL.max)
                V.drain()
                V.tensor_scalar(neg[:, :], b[:, :], -1.0, None, op0=AL.mult)
                V.drain()
                V.tensor_tensor(rmn[:, :], rmn[:, :], neg[:, :], op=AL.max)
            c.mark(V.drain(), Q + 'vv')
        c.wait(V, Q + 'mm')
        c.wait(V, Q + 'ld')
        V.tensor_copy(strip[0], psA[0:1, 0:512])
        V.drain()
        V.tensor_copy(strip[1], psB[0:1, 0:512])
        c.mark(V.drain(), Q + 'vv')
        c.wait(nc.tensor, Q + 'vv')
        JB = 512 // 128

        def row_to_cols(st, dst, slot):
            c.wait(nc.sync, Q + 'vv')
            c.dma(nc.sync, s.rowtmp[slot][None, :], st, q=Q + 'ld')
            c.chain(Q + 'ld')
            c.dma(nc.sync, dst,
                  s.rowtmp[slot].rearrange("(j p) -> p j", p=128), q=Q + 'ld',
                  slow=True)
        row_to_cols(strip[0], cq(SX)[:, nt * JB:(nt + 1) * JB], 0)
        row_to_cols(strip[1], cq(QX)[:, nt * JB:(nt + 1) * JB], 1)
        psD = s.pb[2]
        for k, t in ((MX, rmx), (MN, rmn)):
            for jj in range(JB):
                c.wait(nc.tensor, Q + 'vv')
                c.mark(nc.tensor.transpose(psD[0:128, 0:128],
                                           t[:, jj * 128:(jj + 1) * 128],
                                           s.id_f[:, :]), Q + 'mm')
                c.wait(V, Q + 'mm')
                V.tensor_reduce(cq(k)[:, nt * JB + jj:nt * JB + jj + 1],
                                psD[0:128, 0:128], axis=AX.X, op=AL.max)
                c.mark(V.drain(), Q + 'vv')
        c.wait(nc.tensor, Q + 'vv')

    c.dma(nc.sync, dskq[0:1, 0:1], s.Dskip[None, :], q=Q + 'ld')
    c.wait(nc.tensor, Q + 'ld')
    psD = s.pb[2]
    c.mark(nc.tensor.matmul(psD[0:128, 0:1], s.onr(), dskq[0:1, 0:1],
                            start=True, stop=True), Q + 'mm')
    c.wait(V, Q + 'mm')
    V.tensor_copy(cq(DSKC)[:, 0:1], psD[0:128, 0:1])
    c.mark(V.drain(), Q + 'vv')
    V.tensor_scalar(cq(M1), cq(SX), 1.0 / D, None, op0=AL.mult)
    V.drain()
    V.tensor_scalar(cq(T0), cq(QX), 1.0 / D, None, op0=AL.mult)
    V.drain()
    V.tensor_mul(cq(T1), cq(M1), cq(M1))
    V.drain()
    V.tensor_sub(cq(T0), cq(T0), cq(T1))
    c.mark(V.drain(), Q + 'vv')
    c.wait(S, Q + 'vv')
    S.activation(cq(T0), cq(T0), AF.Sqrt, bias=float(LN_EPS))
    c.mark(S.drain(), Q + 'ss')
    c.wait(V, Q + 'ss')
    s.recip_nr(V, cq(S1), cq(T0), cq(T1))
    V.tensor_sub(cq(T0), cq(MX), cq(M1))
    V.drain()
    V.tensor_add(cq(T1), cq(MN), cq(M1))
    V.drain()
    V.tensor_tensor(cq(T0), cq(T0), cq(T1), op=AL.max)
    V.drain()
    V.tensor_mul(cq(T0), cq(T0), cq(S1))
    V.drain()
    V.tensor_scalar(cq(T0), cq(T0), 1e-8, None, op0=AL.max)
    V.drain()
    V.tensor_scalar(cq(XSC), cq(T0), 1.0 / 127.0, None, op0=AL.mult)
    V.drain()
    s.recip_nr(V, cq(T1), cq(XSC), cq(T0))
    V.tensor_mul(cq(QQ), cq(S1), cq(T1))
    V.drain()
    V.tensor_mul(cq(NMQ), cq(M1), cq(QQ))
    V.drain()
    V.tensor_scalar(cq(NMQ), cq(NMQ), -1.0, None, op0=AL.mult)
    V.drain()
    V.tensor_mul(cq(NMS1), cq(M1), cq(S1))
    V.drain()
    V.tensor_scalar(cq(NMS1), cq(NMS1), -1.0, None, op0=AL.mult)
    V.drain()
    V.tensor_scalar(cq(DS), cq(S1), cq(DSKC)[:, 0:1], None, op0=AL.mult)
    V.drain()
    V.tensor_mul(cq(NDMS), cq(DS), cq(M1))
    V.drain()
    V.tensor_scalar(cq(NDMS), cq(NDMS), -1.0, None, op0=AL.mult)
    c.mark(V.drain(), Q + 'vv')

    rr = lambda i: s.rowd[i].rearrange("(j p) -> j p", p=128)
    sh = lambda i: s.statsh[par, i].rearrange("(j p) -> j p", p=128)
    s.cols_to_row(cq(S1), sh(0), NJ, pfx=Q)
    s.cols_to_row(cq(NMS1), sh(1), NJ, pfx=Q)
    s.cols_to_row(cq(QQ), rr(R_Q), NJ, pfx=Q)
    s.cols_to_row(cq(NMQ), rr(R_NMQ), NJ, pfx=Q)
    s.cols_to_row(cq(XSC), rr(R_XSC), NJ, pfx=Q)
    s.cols_to_row(cq(DS), rr(R_DS), NJ, pfx=Q)
    s.cols_to_row(cq(NDMS), rr(R_NDMS), NJ, pfx=Q)

    wqs = s.col("b1_wsc", 8)
    wcol = s.col("b1_wcol", 1)
    wcol2 = s.col("b1_wcol2", 1)
    psT = s.pb[3]
    for oc in range(DC):
        c.wait(nc.sync, Q + 'mm')
        c.wait(nc.sync, Q + 'vv')
        c.dma(nc.sync, wt[:, :], s.l1w[oc * 128:(oc + 1) * 128, :], q=Q + 'ld')
        c.wait(V, Q + 'ld')
        V.tensor_reduce(wcol, wt[:, :], axis=AX.X, op=AL.max,
                        apply_absolute_value=True)
        V.drain()
        V.tensor_scalar(wcol, wcol, 1e-8, None, op0=AL.max)
        V.drain()
        V.tensor_scalar(wqs[:, oc:oc + 1], wcol, 1.0 / 127.0, None, op0=AL.mult)
        V.drain()
        s.recip_nr(V, wcol2, wqs[:, oc:oc + 1], wcol)
        V.tensor_scalar(wt[:, :], wt[:, :], wcol2, None, op0=AL.mult)
        V.drain()
        iv = sqw.bitcast(mybir.dt.int32)
        V.tensor_copy(iv, wt[:, :])
        V.drain()
        V.tensor_copy(rndf[:, :], iv)
        V.drain()
        V.tensor_sub(wt[:, :], wt[:, :], rndf[:, :])
        V.drain()
        V.tensor_single_scalar(sqw[:, :], wt[:, :], 0.5, op=AL.is_ge)
        V.drain()
        V.tensor_add(rndf[:, :], rndf[:, :], sqw[:, :])
        V.drain()
        V.tensor_single_scalar(sqw[:, :], wt[:, :], -0.5, op=AL.is_lt)
        V.drain()
        V.tensor_sub(wq16[:, :], rndf[:, :], sqw[:, :])
        c.mark(V.drain(), Q + 'vv')
        for kf in range(DC):
            c.wait(nc.tensor, Q + 'vv')
            c.mark(nc.tensor.transpose(psT[0:128, 0:128].bitcast(BF16)[:, 0:128],
                                       wq16[:, kf * 128:(kf + 1) * 128],
                                       s.id_b[:, :]), Q + 'mm')
            c.wait(V, Q + 'mm')
            c.wait(V, Q + 'ld')
            V.tensor_copy(wqt_t[:, :], psT[0:128, 0:128].bitcast(BF16)[:, 0:128])
            c.mark(V.drain(), Q + 'vv')
            c.wait(nc.sync, Q + 'vv')
            c.dma(nc.sync, s.wqt_dram[kf * 128:(kf + 1) * 128, oc * 128:(oc + 1) * 128],
                  wqt_t[:, :], q=Q + 'ld')
    s.wqs_col = wqs

    b1_s4_chain(s, par)
    s.barrier_lite(Q + 'bar')


def emit_fft_fwd(s, Q, rT, CH, Y, X, nyrow, kst=None):
    c, nc = s.c, s.nc
    V, S = nc.vector, nc.scalar
    QNZ = s.SEQ // 128
    KB = s.KB
    PSTR, PACKE = s.PSTR, s.PACKE
    s1c = s.fft_s1_sb
    psY = [s.pb[0], s.pb[3]]
    psX = [s.pb[1], s.pb[2]]
    mall = kst if kst is not None else s.cst['f_mall']
    mbuf = s._fwd_mbuf

    for chp in range(0, CH, 2):
        ps = psY[(chp // 2) % 2]
        c.wait_lag(nc.tensor, Q + 'ss', 1)
        for ci in range(2):
            m = nc.tensor.matmul(ps[0:128, ci * 256:(ci + 1) * 256],
                                 rT[0:QNZ, (chp + ci)::CH],
                                 s1c[0:QNZ, :], start=True, stop=True)
        c.mark(m, Q + 'mm')
        c.wait(S, Q + 'mm')
        S.copy(Y[:, chp * 256:(chp + 2) * 256], ps[0:128, 0:512])
        c.mark(S.drain(), Q + 'ss')
    c.wait(nc.tensor, Q + 'ss')
    m = nc.tensor.matmul(psX[0][0:1, 0:CH], s.sgncol_sb[:, :], Y[:, 0::256],
                         start=True, stop=True)
    c.mark(m, Q + 'mm')
    c.wait(V, Q + 'mm')
    V.tensor_copy(nyrow[0:1, 0:CH], psX[0][0:1, 0:CH])
    c.mark(V.drain(), Q + 'vv')
    c.wait(nc.tensor, Q + 'vv')

    for k1 in range(128):
        mb = mbuf[k1 % 2]
        c.wait_lag(nc.sync, Q + 'mm', 1)
        c.dma(nc.sync, mb[:, :], mall[k1], q=Q + 'ld')
        c.wait(nc.tensor, Q + 'ld')
        c.wait_lag(nc.tensor, Q + 'vv', 1)
        po = (k1 % PACKE) * PSTR
        ps = psX[k1 % 2]
        tp = (0, po)
        Yre = Y[:, k1::256]
        Yim = Y[:, 128 + k1::256]
        nc.tensor.matmul(ps[po:po + KB, 0:CH], mb[:, 0:KB], Yre,
                         start=True, stop=False, tile_position=tp)
        nc.tensor.matmul(ps[po:po + KB, 0:CH], mb[:, 2 * KB:3 * KB], Yim,
                         start=False, stop=True, tile_position=tp)
        nc.tensor.matmul(ps[po:po + KB, CH:2 * CH], mb[:, KB:2 * KB], Yre,
                         start=True, stop=False, tile_position=tp)
        m = nc.tensor.matmul(ps[po:po + KB, CH:2 * CH], mb[:, 0:KB], Yim,
                             start=False, stop=True, tile_position=tp)
        c.mark(m, Q + 'mm')
        g = k1 // PACKE
        c.wait(V, Q + 'mm')
        V.tensor_copy(X[po:po + KB, g * 2 * CH:(g + 1) * 2 * CH],
                      ps[po:po + KB, 0:2 * CH])
        c.mark(V.drain(), Q + 'vv')
    c.wait(nc.tensor, Q + 'vv')


def b1_conv(s, par):
    c, nc = s.c, s.nc
    V, S = nc.vector, nc.scalar
    A = s.arena
    SEQ, T, DC, D = s.SEQ, s.T, s.DC, s.D
    QNZ = SEQ // 128
    KB, PSTR, PACKE = s.KB, s.PSTR, s.PACKE
    NGRP = s.NGRP
    CP = s.CPASS
    NPASS = s.NPASS
    Q = 'c_'

    stage = A.f32(0, SEQ)
    xn1 = A.bf16(8192, SEQ)
    rT = A.bf16(0, 128 * CP)
    Ybuf = A.bf16(8192, CP * 256)
    Xbuf = A.bf16(0, NGRP * 2 * CP)
    kfull = A.bf16(8192, NGRP * 2 * CP)
    scr1 = A.bf16(16384, NGRP * CP)
    scr2 = A.bf16(20480, NGRP * CP)
    Gbuf = A.bf16(8192, 128 * 2 * 128)
    GTre = A.bf16(0, CP * 128)
    GTim = A.bf16(24576, CP * 128)
    hown = A.bf16(8192, T)
    hpeer = A.bf16(8192 + T // 2, T)
    lnA = A.f32(12288, 512, 1)
    lnB = A.f32(12800, 512, 1)
    iab = s._cv_iab
    sny = s.sny_sb
    snyb = s.snyb_sb
    pnyb = s.pnyb_sb

    psG = [s.pb[4], s.pb[5]]
    psH = s.pb[6]
    psB2 = s.pb[7]
    psT = s.pb[3]

    for pp in range(NPASS):
        ch0 = par * (D // 2) + pp * CP
        s.barrier_lite(Q + 'bar')
        c.dma(nc.sync, stage[:, 0:T], s.x1sh[0, ch0:ch0 + 128, :], q=Q + 'ld')
        c.dma(nc.sync, stage[:, T:SEQ], s.x1sh[1, ch0:ch0 + 128, :], q=Q + 'ld')
        NTT = T // 512
        for jb in range(SEQ // 512):
            half, jo = jb // NTT, jb % NTT
            c.wait(nc.sync, Q + 'mm')
            c.dma(nc.sync, lnA, s.statsh[half, 0, jo * 512:(jo + 1) * 512][None, :],
                  q=Q + 'ld')
            c.dma(nc.sync, lnB, s.statsh[half, 1, jo * 512:(jo + 1) * 512][None, :],
                  q=Q + 'ld')
            sl = slice(jb * 512, jb * 512 + 512)
            c.wait(nc.tensor, Q + 'vv')
            c.wait(nc.tensor, Q + 'ld')
            m = nc.tensor.matmul(psB2[0:128, 0:512], s.onr(), lnA,
                                 start=True, stop=True)
            c.mark(m, Q + 'mm')
            c.wait(V, Q + 'mm')
            c.wait(V, Q + 'ld')
            V.tensor_mul(xn1[:, sl], stage[:, sl], psB2[0:128, 0:512])
            c.mark(V.drain(), Q + 'vv')
            c.wait(nc.tensor, Q + 'vv')
            m = nc.tensor.matmul(psB2[0:128, 0:512], s.onr(), lnB,
                                 start=True, stop=True)
            c.mark(m, Q + 'mm')
            c.wait(V, Q + 'mm')
            V.tensor_add(xn1[:, sl], xn1[:, sl], psB2[0:128, 0:512])
            c.mark(V.drain(), Q + 'vv')
            c.wait(nc.tensor, Q + 'vv')
        for p8 in range(16):
            for pi in range(8):
                p = p8 * 8 + pi
                c.mark(nc.tensor.transpose(
                    psT[0:QNZ, 0:512].bitcast(BF16)[0:QNZ, pi * 128:(pi + 1) * 128],
                    xn1[:, p::128], s.id_b[:, :]), Q + 'mm')
            c.wait(V, Q + 'mm')
            V.tensor_copy(rT[0:QNZ, p8 * 1024:(p8 + 1) * 1024],
                          psT[0:QNZ, 0:512].bitcast(BF16)[0:QNZ, 0:1024])
            c.mark(V.drain(), Q + 'vv')
            c.wait(nc.tensor, Q + 'vv')
        emit_fft_fwd(s, Q, rT, CP, Ybuf, Xbuf, sny[0:1, pp * CP:(pp + 1) * CP])
        c.wait(V, Q + 'vv')
        V.tensor_copy(snyb[0:1, pp * CP:(pp + 1) * CP],
                      sny[0:1, pp * CP:(pp + 1) * CP])
        c.mark(V.drain(), Q + 'vv')

        bands = ([(0, 128)] if KB == PSTR else
                 [(po, po + KB) for po in (0, PSTR)])
        c.wait(nc.sync, Q + 'vv')
        for a, b_ in bands:
            c.dma(nc.sync,
                  kfull[a:b_, :].rearrange("p (g c h) -> p g c h", g=NGRP, c=2),
                  s.kf_dram[a:b_, :, :, pp * CP:(pp + 1) * CP], q=Q + 'ld')
        c.wait(V, Q + 'ld')
        def gch(ap):
            return ap.rearrange("p (g c h) -> p g c h", g=NGRP, c=2)
        for a, b_ in bands:
            Xr = gch(Xbuf[:, :])[a:b_, :, 0, :]
            Xi = gch(Xbuf[:, :])[a:b_, :, 1, :]
            Kr = gch(kfull[:, :])[a:b_, :, 0, :]
            Ki = gch(kfull[:, :])[a:b_, :, 1, :]
            t1 = scr1[:, :].rearrange("p (g h) -> p g h", g=NGRP)[a:b_]
            t2 = scr2[:, :].rearrange("p (g h) -> p g h", g=NGRP)[a:b_]
            V.tensor_mul(t1, Xr, Ki)
            V.drain()
            V.tensor_mul(t2, Xi, Kr)
            V.drain()
            V.tensor_mul(Xr, Xr, Kr)
            V.drain()
            V.tensor_mul(Kr, Xi, Ki)
            V.drain()
            V.tensor_sub(Xr, Xr, Kr)
            V.drain()
            V.tensor_add(Xi, t1, t2)
            V.drain()
        V.tensor_mul(pnyb[0:1, pp * CP:(pp + 1) * CP],
                     snyb[0:1, pp * CP:(pp + 1) * CP],
                     s.kfny_rowb[0:1, pp * CP:(pp + 1) * CP])
        c.mark(V.drain(), Q + 'vv')

        c.wait(nc.tensor, Q + 'vv')
        c.wait(nc.tensor, Q + 'ss')
        PA = [(s.pb[4], s.pb[5]), (s.pb[0], s.pb[1])]
        for k1 in range(128):
            ia = iab[k1 % 2]
            c.wait_lag(nc.sync, Q + 'mm', 1)
            c.dma(nc.sync, ia[:, :], s.cst['i_all'][k1], q=Q + 'ld')
            c.wait(nc.tensor, Q + 'ld')
            c.wait_lag(nc.tensor, Q + 'vv', 1)
            c.wait_lag(nc.tensor, Q + 'ss', 1)
            pg0, pg1 = PA[k1 % 2]
            po = (k1 % PACKE) * PSTR
            g = k1 // PACKE
            tpr = (po, 0)
            Pre = Xbuf[po:po + KB, (g * 2 + 0) * CP:(g * 2 + 1) * CP]
            Pim = Xbuf[po:po + KB, (g * 2 + 1) * CP:(g * 2 + 2) * CP]
            nc.tensor.matmul(pg0[0:CP, 0:128], Pre, ia[po:po + KB, 0:128],
                             start=True, stop=False, tile_position=tpr)
            nc.tensor.matmul(pg1[0:CP, 0:128], Pre, ia[po:po + KB, 128:256],
                             start=True, stop=False, tile_position=tpr)
            stop0 = (k1 != 0)
            nc.tensor.matmul(pg0[0:CP, 0:128], Pim, ia[po:po + KB, 256:384],
                             start=False, stop=stop0, tile_position=tpr)
            m = nc.tensor.matmul(pg1[0:CP, 0:128], Pim, ia[po:po + KB, 0:128],
                                 start=False, stop=True, tile_position=tpr)
            if k1 == 0:
                m2 = nc.tensor.matmul(pg0[0:CP, 0:128],
                                      pnyb[0:1, pp * CP:(pp + 1) * CP],
                                      s.sgnrow_sb[:, :], start=False, stop=True)
            c.mark(m if k1 != 0 else m2, Q + 'mm')
            c.wait(V, Q + 'mm')
            V.tensor_copy(Gbuf[0:CP, (k1 * 2 + 0) * 128:(k1 * 2 + 1) * 128],
                          pg0[0:CP, 0:128])
            c.mark(V.drain(), Q + 'vv')
            c.wait(S, Q + 'mm')
            S.copy(Gbuf[0:CP, (k1 * 2 + 1) * 128:(k1 * 2 + 2) * 128],
                   pg1[0:CP, 0:128])
            c.mark(S.drain(), Q + 'ss')
        c.wait(nc.tensor, Q + 'vv')
        c.wait(nc.tensor, Q + 'ss')

        for comp in range(2):
            GT = GTre if comp == 0 else GTim
            for p4 in range(32):
                for pi in range(4):
                    p = p4 * 4 + pi
                    c.mark(nc.tensor.transpose(
                        psT[0:128, 0:512].bitcast(BF16)[0:128, pi * 128:(pi + 1) * 128],
                        Gbuf[0:CP, (comp * 128 + p)::256], s.id_b[0:CP, 0:CP]),
                        Q + 'mm')
                c.wait(V, Q + 'mm')
                V.tensor_copy(GT[:, p4 * 4 * CP:(p4 * 4 + 4) * CP],
                              psT[0:128, 0:512].bitcast(BF16)[0:128, 0:512])
                c.mark(V.drain(), Q + 'vv')
                c.wait(nc.tensor, Q + 'vv')

        qo = par * (QNZ // 2)
        qp = (1 - par) * (QNZ // 2)
        psHr = [s.pb[6], s.pb[7]]
        for p in range(128):
            ph = psHr[p % 2]
            c.wait_lag(nc.tensor, Q + 'vv', 1)
            c.wait_lag(nc.tensor, Q + 'ss', 1)
            nc.tensor.matmul(ph[0:128, 0:QNZ], GTre[:, p * CP:(p + 1) * CP],
                             s.fftb_re_sb[:, 0:QNZ], start=True, stop=False)
            m = nc.tensor.matmul(ph[0:128, 0:QNZ], GTim[:, p * CP:(p + 1) * CP],
                                 s.fftb_im_sb[:, 0:QNZ], start=False, stop=True)
            c.mark(m, Q + 'mm')
            c.wait(V, Q + 'mm')
            V.tensor_copy(hown[:, p::128], ph[0:128, qo:qo + QNZ // 2])
            c.mark(V.drain(), Q + 'vv')
            c.wait(S, Q + 'mm')
            S.copy(hpeer[:, p::128], ph[0:128, qp:qp + QNZ // 2])
            c.mark(S.drain(), Q + 'ss')
        c.wait(nc.sync, Q + 'vv')
        c.wait(nc.sync, Q + 'ss')
        c.dma(nc.sync, s.hfeat_dram[pp], hown[:, :], q=Q + 'ld')
        c.dma(nc.sync, s.hsh[par, pp], hpeer[:, :], q=Q + 'ld')
        c.wait(nc.sync, Q + 'ld')
    s.barrier_lite(Q + 'bar')


def b1_final(s, par):
    c, nc = s.c, s.nc
    V, S = nc.vector, nc.scalar
    T, D, DC, NT = s.T, s.D, s.DC, s.NT
    A = s.arena
    Q = 'f_'

    xq = A.bf16(0, DC * 512)
    htile = A.bf16(2048, 512)
    y1 = A.f32(2304, 512)
    gt = A.f32(2816, 512)
    outt = A.f32(3328, 512)
    vt = A.f32(3840, 512)
    wqcol = A.bf16(4352, DC * 128)
    x1t = [A.f32(4864, 512), A.f32(5376, 512)]
    bD = A.f32(5888, 512)
    bq = A.f32(6400, 512)
    mdt = A.f32(6912, 512)
    stq = A.f32(7424, 512, 1)
    stnmq = A.f32(7936, 512, 1)
    stDs = A.f32(8448, 512, 1)
    stnDms = A.f32(8960, 512, 1)
    stxsc = A.f32(9472, 512, 1)

    psF = s.pb[0]
    psB = s.pb[1]
    psO = s.pb[2]
    l1b_c = s.col("fn_l1b", 8)
    c.dma(nc.sync, l1b_c, s.l1b.rearrange("(c p) -> p c", p=128), q=Q + 'ld',
          slow=True)

    def bcast(dst, strip):
        c.wait(nc.tensor, Q + 'vv')
        m = nc.tensor.matmul(psB[0:128, 0:512], s.onr(), strip,
                             start=True, stop=True)
        c.mark(m, Q + 'mm')
        c.wait(V, Q + 'mm')
        V.tensor_copy(dst[:, :], psB[0:128, 0:512])
        c.mark(V.drain(), Q + 'vv')

    for nt in range(NT):
        tsl = slice(nt * 512, (nt + 1) * 512)
        c.wait(nc.sync, Q + 'mm')
        c.dma(nc.sync, stq, s.rowd[R_Q, tsl][None, :], q=Q + 'ld')
        c.dma(nc.sync, stnmq, s.rowd[R_NMQ, tsl][None, :], q=Q + 'ld')
        c.dma(nc.sync, stDs, s.rowd[R_DS, tsl][None, :], q=Q + 'ld')
        c.dma(nc.sync, stnDms, s.rowd[R_NDMS, tsl][None, :], q=Q + 'ld')
        c.dma(nc.sync, stxsc, s.rowd[R_XSC, tsl][None, :], q=Q + 'ld')
        c.wait(nc.tensor, Q + 'ld')
        bcast(bq, stq)
        bcast(mdt, stnmq)
        for kf in range(DC):
            b = x1t[kf % 2]
            c.wait(nc.sync, Q + 'vv')
            c.wait(nc.sync, Q + 'mm')
            c.dma(nc.sync, b[:, :], s.x1sh[par, kf * 128:(kf + 1) * 128, tsl],
                  q=Q + 'ld')
            c.wait(V, Q + 'ld')
            c.wait(V, Q + 'st')
            V.tensor_mul(vt[:, :], b[:, :], bq[:, :])
            V.drain()
            V.tensor_add(vt[:, :], vt[:, :], mdt[:, :])
            V.drain()
            ivf = y1.bitcast(mybir.dt.int32)
            V.tensor_copy(ivf, vt[:, :])
            V.drain()
            V.tensor_copy(gt[:, :], ivf)
            V.drain()
            V.tensor_sub(vt[:, :], vt[:, :], gt[:, :])
            V.drain()
            V.tensor_single_scalar(y1[:, :], vt[:, :], 0.5, op=AL.is_ge)
            V.drain()
            V.tensor_add(gt[:, :], gt[:, :], y1[:, :])
            V.drain()
            V.tensor_single_scalar(y1[:, :], vt[:, :], -0.5, op=AL.is_lt)
            V.drain()
            V.tensor_sub(xq[:, kf * 512:(kf + 1) * 512], gt[:, :], y1[:, :])
            c.mark(V.drain(), Q + 'vv')
        bcast(bD, stDs)
        bcast(mdt, stnDms)
        bcast(bq, stxsc)

        for dc in range(DC):
            my_half = (dc // (DC // 2)) == par
            ppx = dc % (DC // 2)
            hsrc = s.hfeat_dram[ppx] if my_half else s.hsh[1 - par, ppx]
            c.wait(nc.sync, Q + 'vv')
            c.dma(nc.sync, htile[:, :], hsrc[:, tsl], q=Q + 'ld')
            b = x1t[dc % 2]
            c.wait(nc.sync, Q + 'mm')
            c.dma(nc.sync, b[:, :], s.x1sh[par, dc * 128:(dc + 1) * 128, tsl],
                  q=Q + 'ld')
            c.wait(V, Q + 'ld')
            V.tensor_mul(y1[:, :], b[:, :], bD[:, :])
            V.drain()
            V.tensor_add(y1[:, :], y1[:, :], mdt[:, :])
            V.drain()
            V.tensor_add(y1[:, :], y1[:, :], htile[:, :])
            c.mark(V.drain(), Q + 'vv')
            c.wait(S, Q + 'vv')
            S.activation(gt[:, :], y1[:, :], AF.Gelu)
            c.mark(S.drain(), Q + 'ss')
            c.wait(nc.sync, Q + 'mm')
            c.dma(nc.sync,
                  wqcol[:, :].rearrange("p (k o) -> p k o", o=128),
                  s.wqt_dram[:, dc * 128:(dc + 1) * 128].rearrange(
                      "(k p) o -> p k o", p=128), q=Q + 'wld')
            c.wait(nc.tensor, Q + 'vv')
            c.wait(nc.tensor, Q + 'wld')
            for kf in range(DC):
                mm = nc.tensor.matmul(psF[0:128, 0:512],
                                      wqcol[:, kf * 128:(kf + 1) * 128],
                                      xq[:, kf * 512:(kf + 1) * 512],
                                      start=(kf == 0), stop=(kf == DC - 1))
            c.mark(mm, Q + 'mm')
            c.wait(V, Q + 'mm')
            c.wait(V, Q + 'ss')
            V.tensor_scalar(outt[:, :], psF[0:128, 0:512],
                            s.wqs_col[:, dc:dc + 1], None, op0=AL.mult)
            V.drain()
            V.tensor_mul(outt[:, :], outt[:, :], bq[:, :])
            V.drain()
            V.scalar_tensor_tensor(outt[:, :], outt[:, :],
                                   l1b_c[:, dc:dc + 1], gt[:, :],
                                   op0=AL.add, op1=AL.add)
            V.drain()
            V.tensor_add(outt[:, :], outt[:, :], b[:, :])
            c.mark(V.drain(), Q + 'vv')
            c.wait(nc.tensor, Q + 'vv')
            for j in range(4):
                c.mark(nc.tensor.transpose(psO[0:128, j * 128:(j + 1) * 128],
                                           outt[:, j * 128:(j + 1) * 128],
                                           s.id_f[:, :]), Q + 'mm')
            c.wait(V, Q + 'mm')
            c.wait(V, Q + 'st')
            V.tensor_copy(vt[:, :], psO[0:128, 0:512])
            c.mark(V.drain(), Q + 'vv')
            c.wait(nc.sync, Q + 'vv')
            c.dma(nc.sync,
                  s.y_out[nt * 512:(nt + 1) * 512, dc * 128:(dc + 1) * 128]
                      .rearrange("(j p) d -> p j d", p=128),
                  vt[:, :].rearrange("p (j d) -> p j d", j=4), q=Q + 'st')
    c.wait(nc.sync, Q + 'st')
    s.barrier_lite(Q + 'bar')


def b1_s4_chain(s, par):
    c, nc = s.c, s.nc
    V, S, G = nc.vector, nc.scalar, nc.gpsimd
    A = s.arena
    SEQ, D = s.SEQ, s.D
    QDs = SEQ // 128
    KB, PSTR, PACKE = s.KB, s.PSTR, s.PACKE
    NGRP = s.NGRP
    NCH = SEQ // 512
    nw = SEQ // 64
    APC = 512 // nw
    Q = 'q_'
    LB = 32768

    cre = A.f32(0, SEQ, 64)
    cim = A.f32(SEQ, SEQ, 64)
    gmre = A.bf16(2 * SEQ, SEQ, 64)
    gmim = A.bf16(2 * SEQ + SEQ // 2, SEQ, 64)
    if SEQ >= 4096:
        O_GTm, O_Yg, O_gbuf, O_Ymode, O_Xmode, O_gfT = 0, 8192, 16384, 0, 16384, 24576
    else:
        O_GTm, O_Yg, O_gbuf = 4 * SEQ, 4 * SEQ + 8192, 4 * SEQ + 16384
        O_Ymode, O_Xmode, O_gfT = 0, 16384, 0
    GTmre = A.bf16(O_GTm, 128 * 64)
    GTmim = A.bf16(O_GTm + 4096, 128 * 64)
    Ygbuf = A.bf16(O_Yg, 64 * 256)
    gbuf = A.bf16(O_gbuf, 128 * 2 * 64, QDs)
    Ymode = A.bf16(O_Ymode, 128 * 256)
    Xmode = A.bf16(O_Xmode, NGRP * 2 * 128)
    gfT = A.bf16(O_gfT, 128 * 2 * KB)

    assert nw <= 128
    s4t = [A.f32(LB + i * 512, 512, 64) for i in range(5)]
    s4g = [A.bf16(LB + 2560, 3 * QDs),
           A.bf16(LB + 2560 + (3 * QDs + 1) // 2, 3 * QDs)]
    cl = A.f32(LB + 2944, 64)
    ct_sb = A.bf16(LB + 3008, D // 2)
    kfs = A.bf16(LB + 3264, 512, 64)
    ctt = A.f32(LB + 3520, 128, 64)
    k4 = A.f32(LB + 3648, 4 * nw, 64)
    tstg = A.f32(LB + 4160, 512, 1)
    i1r = A.f32(LB + 4672, 512, 1)
    i1i = A.f32(LB + 5184, 512, 1)
    wr = A.f32(LB + 5696, 512, 1)
    wi = A.f32(LB + 6208, 512, 1)
    dsk2 = A.f32(LB + 6720, 2, 1)
    assert LB + 7008 <= s.AW

    c.dma(nc.sync, s.fft_s1_sb[:, :], s.cst['f_s1'][:, :], q=Q + 'ld')
    c.dma(nc.sync, s.s4_s1a_sb[:, :], s.cst['s4_s1a'][:, :], q=Q + 'ld')
    c.dma(nc.sync, s.s4_s1b_sb[:, :], s.cst['s4_s1b'][:, :], q=Q + 'ld')
    c.dma(nc.sync, s.sgncol_sb[:, :], s.cst['sign_col'][:, :], q=Q + 'ld')
    c.dma(nc.sync, s.sgnrow_sb[:, :], s.cst['sign_row_n'][:, :], q=Q + 'ld')
    c.dma(nc.sync, s.fftb_re_sb[:, :], s.cst['b_re'][:, :], q=Q + 'ld')
    c.dma(nc.sync, s.fftb_im_sb[:, :], s.cst['b_im'][:, :], q=Q + 'ld')

    vc = s.col("s4_vc", 24)
    for i, t in enumerate(["B_re", "B_im", "P_re", "P_im", "Q_re", "Q_im",
                           "diag_re", "diag_im"]):
        c.dma(nc.sync, vc[0:64, i:i + 1], getattr(s, "s4_" + t)[:, None], q=Q + 'ld')
    Bre, Bim, Pre, Pim, Qre, Qim, dre, dim = (vc[0:64, i:i + 1] for i in range(8))
    QBre, QBim, QPre, QPim = (vc[0:64, i:i + 1] for i in range(8, 12))
    nQBim, nQPim = (vc[0:64, i:i + 1] for i in range(12, 14))
    acol, a2col, i2dc = (vc[0:64, i:i + 1] for i in range(14, 17))
    c.dma(nc.sync, dsk2[0:1, 0:1], s.s4_step[:, None], q=Q + 'ld')
    c.wait(V, Q + 'ld')
    V.tensor_scalar(dsk2[0:1, 0:1], dsk2[0:1, 0:1], 1e-6, 0.5, op0=AL.max, op1=AL.mult)
    V.drain()
    s.recip_nr(V, dsk2[0:1, 1:2], dsk2[0:1, 0:1], dsk2[0:1, 0:1])
    c.mark(V.drain(), Q + 'vv')
    c.wait(nc.tensor, Q + 'vv')
    c.mark(nc.tensor.matmul(s.pb[7][0:64, 0:1], s.onr(n=64),
                            dsk2[0:1, 1:2], start=True, stop=True), Q + 'mm')
    c.wait(V, Q + 'mm')
    V.tensor_copy(i2dc, s.pb[7][0:64, 0:1])
    c.mark(V.drain(), Q + 'gg')
    tA, tB = (vc[0:64, i:i + 1] for i in range(19, 21))
    def cmul(outre, outim, ar, ai, br, bi):
        V.tensor_mul(tA, ar, br); V.drain()
        V.tensor_mul(tB, ai, bi); V.drain()
        V.tensor_sub(outre, tA, tB); V.drain()
        V.tensor_mul(tA, ar, bi); V.drain()
        V.tensor_mul(tB, ai, br); V.drain()
        V.tensor_add(outim, tA, tB); V.drain()
    cmul(QBre, QBim, Qre, Qim, Bre, Bim)
    cmul(QPre, QPim, Qre, Qim, Pre, Pim)
    V.tensor_scalar(nQBim, QBim, -1.0, None, op0=AL.mult); V.drain()
    V.tensor_scalar(nQPim, QPim, -1.0, None, op0=AL.mult); V.drain()
    V.tensor_scalar(acol, dre, -1.0, None, op0=AL.mult); V.drain()
    V.tensor_mul(a2col, dre, dre)
    c.mark(V.drain(), Q + 'vv')

    bt, dt_, rc, t4, t5 = s4t
    psB = s.pb[7]
    psK = s.pb[6]
    c.wait(G, Q + 'ld')
    for lc in range(NCH):
        sl = slice(lc * 512, (lc + 1) * 512)
        c.wait(nc.sync, Q + 'mm')
        c.dma(nc.sync, tstg, s.cst['s4_timag'][0:1, sl], q=Q + 'ld')
        c.wait(nc.tensor, Q + 'vv')
        c.wait(nc.tensor, Q + 'gg')
        c.wait(nc.tensor, Q + 'ld')
        m = nc.tensor.matmul(psB[0:64, 0:512], s.onr(n=64), tstg,
                             start=True, stop=True)
        c.mark(m, Q + 'mm')
        c.wait(V, Q + 'mm')
        V.tensor_scalar(bt[:, :], psB[0:64, 0:512], i2dc, dim,
                        op0=AL.mult, op1=AL.subtract)
        c.mark(V.drain(), Q + 'vv')
        c.wait(S, Q + 'vv')
        S.activation(dt_[:, :], bt[:, :], AF.Square)
        c.mark(S.drain(), Q + 'ss')
        c.wait(V, Q + 'ss')
        V.tensor_scalar(dt_[:, :], dt_[:, :], a2col, None, op0=AL.add)
        V.drain()
        s.recip_nr(V, rc[:, :], dt_[:, :], t4[:, :])
        V.tensor_scalar(cre[:, sl], rc[:, :], acol, None, op0=AL.mult)
        V.drain()
        V.tensor_mul(t4[:, :], bt[:, :], rc[:, :])
        V.drain()
        V.tensor_scalar(cim[:, sl], t4[:, :], -1.0, None, op0=AL.mult)
        c.mark(V.drain(), Q + 'vv')
        for ki, (ca, cb, aa, bb) in enumerate([
            (QBre, nQBim, cre, cim), (QBre, QBim, cim, cre),
            (QPre, nQPim, cre, cim), (QPre, QPim, cim, cre),
        ]):
            c.wait(nc.tensor, Q + 'vv')
            nc.tensor.matmul(psK[0:1, 0:512], ca, aa[:, sl], start=True, stop=False)
            m = nc.tensor.matmul(psK[0:1, 0:512], cb, bb[:, sl], start=False, stop=True)
            c.mark(m, Q + 'mm')
            c.wait(V, Q + 'mm')
            c.wait(V, Q + 'ld')
            V.tensor_copy(wr, psK[0:1, 0:512])
            c.mark(V.drain(), Q + 'vv')
            c.wait(nc.sync, Q + 'vv')
            c.dma(nc.sync, s.rowtmp[0][None, :], wr, q=Q + 'ld')
            c.chain(Q + 'ld')
            c.dma(nc.sync, k4[lc * APC:(lc + 1) * APC, ki * nw:(ki + 1) * nw],
                  s.rowtmp[0].rearrange("(a b) -> a b", b=nw), q=Q + 'ld',
                  slow=True)

    ka = k4[0:64, 0:nw]
    kb = k4[0:64, nw:2 * nw]
    kc = k4[0:64, 2 * nw:3 * nw]
    kd = k4[0:64, 3 * nw:4 * nw]
    tden = ctt[0:64, 0:nw]
    tden2 = t4[0:64, 0:nw]
    trec = rc[0:64, 0:nw]
    c.wait(V, Q + 'ld')
    V.tensor_scalar_add(kc, kc, 1.0)
    V.drain()
    V.tensor_mul(tden, kc, kc)
    V.drain()
    V.tensor_mul(tden2, kd, kd)
    V.drain()
    V.tensor_add(tden, tden, tden2)
    V.drain()
    s.recip_nr(V, trec, tden, tden2)
    V.tensor_mul(tden, ka, kc)
    V.drain()
    V.tensor_mul(tden2, kb, kd)
    V.drain()
    V.tensor_add(tden, tden, tden2)
    V.drain()
    V.tensor_mul(tden2, kb, kc)
    V.drain()
    V.tensor_mul(kb, ka, kd)
    V.drain()
    V.tensor_sub(tden2, tden2, kb)
    V.drain()
    V.tensor_mul(ka, tden, trec)
    V.drain()
    V.tensor_mul(kb, tden2, trec)
    c.mark(V.drain(), Q + 'vv')

    for lc in range(NCH):
        sl = slice(lc * 512, (lc + 1) * 512)
        c.wait(nc.sync, Q + 'vv')
        c.wait(nc.sync, Q + 'mm')
        c.dma(nc.sync, s.rowtmp[0].rearrange("(a b) -> a b", b=nw),
              ka[lc * APC:(lc + 1) * APC, :], q=Q + 'ld', slow=True)
        c.dma(nc.sync, s.rowtmp[1].rearrange("(a b) -> a b", b=nw),
              kb[lc * APC:(lc + 1) * APC, :], q=Q + 'ld', slow=True)
        c.chain(Q + 'ld')
        c.dma(nc.sync, wr, s.rowtmp[0][None, :], q=Q + 'ld')
        c.dma(nc.sync, wi, s.rowtmp[1][None, :], q=Q + 'ld')
        c.dma(nc.sync, i1r, s.cst['s4_i1z_re'][0:1, sl], q=Q + 'ld')
        c.dma(nc.sync, i1i, s.cst['s4_i1z_im'][0:1, sl], q=Q + 'ld')
        c.wait(nc.tensor, Q + 'vv')
        c.wait(nc.tensor, Q + 'ld')
        m = nc.tensor.matmul(psB[0:64, 0:512], s.onr(n=64), wr,
                             start=True, stop=True)
        c.mark(m, Q + 'mm')
        c.wait(V, Q + 'mm')
        V.tensor_copy(bt[:, :], psB[0:64, 0:512])
        c.mark(V.drain(), Q + 'vv')
        c.wait(nc.tensor, Q + 'vv')
        m = nc.tensor.matmul(psB[0:64, 0:512], s.onr(n=64), wi,
                             start=True, stop=True)
        c.mark(m, Q + 'mm')
        c.wait(V, Q + 'mm')
        V.tensor_copy(dt_[:, :], psB[0:64, 0:512])
        c.mark(V.drain(), Q + 'vv')
        V.tensor_scalar(rc[:, :], bt[:, :], Pre, None, op0=AL.mult)
        V.drain()
        V.tensor_scalar(t4[:, :], dt_[:, :], Pim, None, op0=AL.mult)
        V.drain()
        V.tensor_sub(rc[:, :], rc[:, :], t4[:, :])
        V.drain()
        V.tensor_scalar(rc[:, :], rc[:, :], -1.0, Bre, op0=AL.mult, op1=AL.add)
        V.drain()
        V.tensor_scalar(t4[:, :], dt_[:, :], Pre, None, op0=AL.mult)
        V.drain()
        V.tensor_scalar(bt[:, :], bt[:, :], Pim, None, op0=AL.mult)
        V.drain()
        V.tensor_add(t4[:, :], t4[:, :], bt[:, :])
        V.drain()
        V.tensor_scalar(t4[:, :], t4[:, :], -1.0, Bim, op0=AL.mult, op1=AL.add)
        V.drain()
        V.tensor_mul(bt[:, :], cre[:, sl], rc[:, :])
        V.drain()
        V.tensor_mul(dt_[:, :], cim[:, sl], t4[:, :])
        V.drain()
        V.tensor_sub(bt[:, :], bt[:, :], dt_[:, :])
        V.drain()
        V.tensor_mul(dt_[:, :], cre[:, sl], t4[:, :])
        V.drain()
        V.tensor_mul(rc[:, :], cim[:, sl], rc[:, :])
        V.drain()
        V.tensor_add(dt_[:, :], dt_[:, :], rc[:, :])
        c.mark(V.drain(), Q + 'vv')
        c.wait(nc.tensor, Q + 'vv')
        m = nc.tensor.matmul(psB[0:64, 0:512], s.onr(n=64), i1r,
                             start=True, stop=True)
        c.mark(m, Q + 'mm')
        c.wait(V, Q + 'mm')
        V.tensor_mul(rc[:, :], psB[0:64, 0:512], bt[:, :])
        V.drain()
        V.tensor_mul(t4[:, :], psB[0:64, 0:512], dt_[:, :])
        c.mark(V.drain(), Q + 'vv')
        c.wait(nc.tensor, Q + 'vv')
        m = nc.tensor.matmul(psB[0:64, 0:512], s.onr(n=64), i1i,
                             start=True, stop=True)
        c.mark(m, Q + 'mm')
        c.wait(V, Q + 'mm')
        V.tensor_mul(t5[:, :], psB[0:64, 0:512], dt_[:, :])
        V.drain()
        V.tensor_mul(dt_[:, :], psB[0:64, 0:512], bt[:, :])
        V.drain()
        V.tensor_sub(gmre[:, sl], rc[:, :], t5[:, :])
        V.drain()
        V.tensor_add(gmim[:, sl], t4[:, :], dt_[:, :])
        c.mark(V.drain(), Q + 'vv')

    psTr = [s.pb[3], s.pb[0]]
    c.wait(nc.tensor, Q + 'vv')
    for comp, (gmc, GTm) in enumerate(((gmre, GTmre), (gmim, GTmim))):
        for p8 in range(16):
            pt = psTr[p8 % 2]
            ev = V if p8 % 2 == 0 else S
            evq = Q + ('vv' if p8 % 2 == 0 else 'ss')
            c.wait(nc.tensor, evq)
            for pi in range(8):
                lp = p8 * 8 + pi
                c.mark(nc.tensor.transpose(
                    pt[0:QDs, 0:512].bitcast(BF16)[0:QDs, pi * 64:(pi + 1) * 64],
                    gmc[0:64, lp::128], s.id_b[0:64, 0:64]), Q + 'mm')
            c.wait(ev, Q + 'mm')
            if ev is V:
                V.tensor_copy(GTm[0:QDs, p8 * 512:(p8 + 1) * 512],
                              pt[0:QDs, 0:512].bitcast(BF16)[0:QDs, 0:512])
            else:
                S.copy(GTm[0:QDs, p8 * 512:(p8 + 1) * 512],
                       pt[0:QDs, 0:512].bitcast(BF16)[0:QDs, 0:512])
            c.mark(ev.drain(), evq)
    c.wait(nc.tensor, Q + 'vv')
    c.wait(nc.tensor, Q + 'ss')

    psYr = [s.pb[0], s.pb[3]]
    for n in range(64):
        ps = psYr[n % 2]
        ev = V if n % 2 == 0 else S
        evq = Q + ('vv' if n % 2 == 0 else 'ss')
        c.wait(nc.tensor, evq)
        nc.tensor.matmul(ps[0:128, 0:256], GTmre[0:QDs, n::64],
                         s.s4_s1a_sb[0:QDs, :], start=True, stop=False)
        m = nc.tensor.matmul(ps[0:128, 0:256], GTmim[0:QDs, n::64],
                             s.s4_s1b_sb[0:QDs, :], start=False, stop=True)
        c.mark(m, Q + 'mm')
        c.wait(ev, Q + 'mm')
        if ev is V:
            V.tensor_copy(Ygbuf[:, n * 256:(n + 1) * 256], ps[0:128, 0:256])
        else:
            S.copy(Ygbuf[:, n * 256:(n + 1) * 256], ps[0:128, 0:256])
        c.mark(ev.drain(), evq)
    c.wait(nc.tensor, Q + 'vv')
    c.wait(nc.tensor, Q + 'ss')

    psG2r = [s.pb[1], s.pb[2]]
    for t1 in range(128):
        gb = s4g[t1 % 2]
        c.wait_lag(nc.sync, Q + 'mm', 1)
        c.dma(nc.sync, gb[:, :], s.cst['s4_gall'][t1], q=Q + 'ld')
        c.wait(nc.tensor, Q + 'ld')
        ps2 = psG2r[t1 % 2]
        ev = V if t1 % 2 == 0 else S
        evq = Q + ('vv' if t1 % 2 == 0 else 'ss')
        c.wait(nc.tensor, evq)
        Ygre = Ygbuf[:, t1::256]
        Ygim = Ygbuf[:, 128 + t1::256]
        nc.tensor.matmul(ps2[0:QDs, 0:64], gb[:, 0:QDs], Ygre,
                         start=True, stop=False)
        nc.tensor.matmul(ps2[0:QDs, 0:64], gb[:, 2 * QDs:3 * QDs], Ygim,
                         start=False, stop=True)
        nc.tensor.matmul(ps2[0:QDs, 64:128], gb[:, QDs:2 * QDs], Ygre,
                         start=True, stop=False)
        m = nc.tensor.matmul(ps2[0:QDs, 64:128], gb[:, 0:QDs], Ygim,
                             start=False, stop=True)
        c.mark(m, Q + 'mm')
        c.wait(ev, Q + 'mm')
        if ev is V:
            V.tensor_copy(gbuf[0:QDs, t1 * 128:(t1 + 1) * 128], ps2[0:QDs, 0:128])
        else:
            S.copy(gbuf[0:QDs, t1 * 128:(t1 + 1) * 128], ps2[0:QDs, 0:128])
        c.mark(ev.drain(), evq)
    c.wait(nc.tensor, Q + 'vv')
    c.wait(nc.tensor, Q + 'ss')

    gfny = tstg
    emit_fft_fwd(s, Q, gbuf, 128, Ymode, Xmode, gfny[0:1, 0:128])

    psC = s.pb[2]
    ch0 = par * (D // 2)
    for cc in range(4):
        rows = slice(ch0 + cc * 128, ch0 + (cc + 1) * 128)
        c.wait(nc.sync, Q + 'mm')
        c.dma(nc.sync, cl[:, :], s.s4_C_re[rows, :], q=Q + 'ld')
        c.wait(nc.tensor, Q + 'ld')
        c.wait(nc.tensor, Q + 'vv')
        c.mark(nc.tensor.transpose(psC[0:64, 0:128], cl[:, :], s.id_f[:, :]),
               Q + 'mm')
        c.wait(V, Q + 'mm')
        V.tensor_copy(ct_sb[0:64, cc * 128:(cc + 1) * 128], psC[0:64, 0:128])
        c.mark(V.drain(), Q + 'vv')
        c.wait(nc.sync, Q + 'mm')
        c.dma(nc.sync, cl[:, :], s.s4_C_im[rows, :], q=Q + 'ld')
        c.wait(nc.tensor, Q + 'ld')
        c.wait(nc.tensor, Q + 'vv')
        c.mark(nc.tensor.transpose(psC[0:64, 0:128], cl[:, :], s.id_f[:, :]),
               Q + 'mm')
        c.wait(V, Q + 'mm')
        V.tensor_scalar(kfs[0:64, 0:128], psC[0:64, 0:128], -1.0, None,
                        op0=AL.mult)
        c.mark(V.drain(), Q + 'vv')
        c.wait(nc.sync, Q + 'vv')
        c.dma(nc.sync, ct_sb[64:128, cc * 128:(cc + 1) * 128],
              kfs[0:64, 0:128], q=Q + 'ld')
    c.wait(nc.tensor, Q + 'ld')

    psCr = [s.pb[2], s.pb[0]]
    for k1 in range(128):
        po = (k1 % PACKE) * PSTR
        g = k1 // PACKE
        for comp in range(2):
            pc2 = psCr[comp]
            ev = V if comp == 0 else S
            evq = Q + ('vv' if comp == 0 else 'ss')
            c.wait(nc.tensor, evq)
            c.mark(nc.tensor.transpose(
                pc2[0:128, 0:128].bitcast(BF16)[0:128, 0:KB],
                Xmode[po:po + KB, (g * 2 + comp) * 128:(g * 2 + comp + 1) * 128],
                s.id_b[po:po + KB, po:po + KB]), Q + 'mm')
            c.wait(ev, Q + 'mm')
            if ev is V:
                V.tensor_copy(gfT[:, (k1 * 2 + comp) * KB:(k1 * 2 + comp + 1) * KB],
                              pc2[0:128, 0:128].bitcast(BF16)[0:128, 0:KB])
            else:
                S.copy(gfT[:, (k1 * 2 + comp) * KB:(k1 * 2 + comp + 1) * KB],
                       pc2[0:128, 0:128].bitcast(BF16)[0:128, 0:KB])
            c.mark(ev.drain(), evq)
    c.wait(nc.tensor, Q + 'vv')
    c.wait(nc.tensor, Q + 'ss')
    kfs2 = A.bf16(LB + 6752, 512, 64)
    kfsr = [kfs, kfs2]
    for k1 in range(128):
        po = (k1 % PACKE) * PSTR
        g = k1 // PACKE
        for comp in range(2):
            pc2 = psCr[comp]
            ev = V if comp == 0 else S
            evq = Q + ('vv' if comp == 0 else 'ss')
            kf_b = kfsr[comp]
            c.wait(nc.tensor, evq)
            m = nc.tensor.matmul(pc2[0:KB, 0:D // 2],
                                 gfT[:, (k1 * 2 + comp) * KB:(k1 * 2 + comp + 1) * KB],
                                 ct_sb[:, :], start=True, stop=True)
            c.mark(m, Q + 'mm')
            c.wait(ev, Q + 'mm')
            c.wait(ev, Q + 'ld')
            if ev is V:
                V.tensor_copy(kf_b[0:KB, 0:D // 2], pc2[0:KB, 0:D // 2])
            else:
                S.copy(kf_b[0:KB, 0:D // 2], pc2[0:KB, 0:D // 2])
            c.mark(ev.drain(), evq)
            c.wait(nc.sync, evq)
            c.dma(nc.sync, s.kf_dram[po:po + KB, g, comp, :], kf_b[0:KB, 0:D // 2],
                  q=Q + 'ld')
    gfnyc_f = cl[:, 0:1]
    gfnyc = s.gfnyc_sb
    c.wait(nc.sync, Q + 'ld')
    c.dma(nc.sync, s.rowtmp[0][None, 0:128], gfny[0:1, 0:128], q=Q + 'ld')
    c.chain(Q + 'ld')
    c.dma(nc.sync, gfnyc_f,
          s.rowtmp[0, 0:128].rearrange("(p x) -> p x", x=1),
          q=Q + 'ld', slow=True)
    c.wait(V, Q + 'ld')
    V.tensor_copy(gfnyc[:, :], gfnyc_f)
    c.mark(V.drain(), Q + 'vv')
    c.wait(nc.tensor, Q + 'vv')
    m = nc.tensor.matmul(psC[0:1, 0:D // 2], gfnyc[:, :], ct_sb[:, :],
                         start=True, stop=True)
    c.mark(m, Q + 'mm')
    c.wait(V, Q + 'mm')
    V.tensor_copy(s.kfny_rowb[0:1, :], psC[0:1, 0:D // 2])
    c.mark(V.drain(), Q + 'vv')

_BUILD_CACHE = None


def _build():
    global _BUILD_CACHE
    if _BUILD_CACHE is None:
        mk = MKBuild(8192, 1024, NCORES=8)
        nc, consts = mk.build(tail=emit_block1)
        _BUILD_CACHE = (mk, nc, consts)
    return _BUILD_CACHE


_IN_MAP = {
    "W_rkv": "L0_attn_W_rkv", "b_rkv": "L0_attn_b_rkv",
    "W_o": "L0_attn_W_o", "b_o": "L0_attn_b_o",
    "nu_log": "L0_attn_nu_log", "gamma_log": "L0_attn_gamma_log",
    "Wg": "L0_ffn_Wg", "ln_scale": "L0_ffn_ln_scale",
    "ln_bias": "L0_ffn_ln_bias", "W1": "L0_ffn_W1", "b1": "L0_ffn_b1",
    "W2": "L0_ffn_W2", "b2": "L0_ffn_b2",
    "l1w": "L1_ffn_weight", "l1b": "L1_ffn_bias",
    "Dskip": "L1_attn_D", "s4_step": "L1_attn_step",
    "s4_C_re": "L1_attn_C_re", "s4_C_im": "L1_attn_C_im",
    "s4_B_re": "L1_attn_B_re", "s4_B_im": "L1_attn_B_im",
    "s4_P_re": "L1_attn_P_re", "s4_P_im": "L1_attn_P_im",
    "s4_Q_re": "L1_attn_Q_re", "s4_Q_im": "L1_attn_Q_im",
    "s4_diag_re": "L1_attn_diag_re", "s4_diag_im": "L1_attn_diag_im",
}


def kernel(**inputs):
    from concourse.bass_utils import run_bass_kernel_spmd
    mk, nc, consts = _build()
    x = np.ascontiguousarray(np.asarray(inputs["L0_attn_inputs"], np.float32))
    B, SEQ, D = x.shape
    T = SEQ // 2
    common = {k: np.ascontiguousarray(np.asarray(inputs[v], np.float32))
              for k, v in _IN_MAP.items()}
    common.update(consts)
    in_maps = []
    for c in range(8):
        b, half = c // 2, c % 2
        m = dict(common)
        m["x_in"] = np.ascontiguousarray(x[b, half * T:(half + 1) * T, :])
        in_maps.append(m)
    res = run_bass_kernel_spmd(nc, in_maps, core_ids=list(range(8)))
    out = np.empty((B, SEQ, D), np.float32)
    for c in range(8):
        b, half = c // 2, c % 2
        out[b, half * T:(half + 1) * T, :] = res.results[c]["y_out"]
    return out


# version 22
# speedup vs baseline: 1.0973x; 1.0101x over previous
import numpy as np
import ml_dtypes
import concourse.bass as bass
import concourse.mybir as mybir

F32 = mybir.dt.float32
BF16 = mybir.dt.bfloat16
AL = mybir.AluOpType
AF = mybir.ActivationFunctionType
AX = mybir.AxisListType
BF = ml_dtypes.bfloat16


class Ctx:

    def __init__(self, nc, stack):
        self.nc = nc
        self.stack = stack
        self.sems = {}
        self.ctr = {}
        self.last_wait = {}
        self.dma_queues = set()
        self.consts = {}
        self._cap = {}

    def sem(self, name):
        if name not in self.sems:
            self.sems[name] = self.stack.enter_context(self.nc.semaphore(name))
            self.ctr[name] = 0
        return self.sems[name]

    def dma(self, eng, out, in_, q="ld", slow=False):
        s = self.sem(q)
        self.dma_queues.add(q)
        kw = {"allow_slow_non_contiguous": True} if slow else {}
        inst = eng.dma_start(out=out, in_=in_, **kw)
        lw = self.last_wait.get(q, 0)
        if lw > 0:
            inst._wait_ge(s, lw)
        inst.then_inc(s, 16)
        self.ctr[q] += 16
        return self.ctr[q]

    def wait_lag(self, eng, q, lag):
        self.sem(q)
        v = max(0, self.ctr[q] - lag)
        self.last_wait[q] = max(self.last_wait.get(q, 0), v)
        eng.wait_ge(self.sem(q), v)

    def chain(self, q):
        self.last_wait[q] = max(self.last_wait.get(q, 0), self.ctr.get(q, 0))

    def mark(self, inst, q):
        s = self.sem(q)
        inst.then_inc(s, 1)
        self.ctr[q] += 1
        return self.ctr[q]

    def wait(self, eng, q, val=None):
        self.sem(q)
        v = self.ctr[q] if val is None else val
        self.last_wait[q] = max(self.last_wait.get(q, 0), v)
        eng.wait_ge(self.sem(q), v)

    def const(self, name, arr):
        if name in self.consts:
            return self._cap[name]
        arr = np.ascontiguousarray(arr)
        dt = {np.dtype(np.float32): F32, np.dtype(BF): BF16}[arr.dtype]
        ap = self.nc.dram_tensor("c_" + name, list(arr.shape), dt,
                                 kind="ExternalInput").ap()
        self.consts["c_" + name] = arr
        self._cap[name] = ap
        return ap

    def sbuf(self, name, shape, dtype=F32):
        if not hasattr(self, "_sbufs"):
            self._sbufs = {}
        if name not in self._sbufs:
            self._sbufs[name] = self.stack.enter_context(
                self.nc.sbuf_tensor(name, shape, dtype))
        return self._sbufs[name]

    def psum(self, name, banks=1):
        return self.stack.enter_context(
            self.nc.psum_tensor(name, [128, 512 * banks], F32))

    def barrier(self):
        self.nc.all_engine_barrier()


class Arena:

    def __init__(self, c, words, name="arena"):
        self.words = words
        self.t = c.sbuf(name, [128, words], F32)

    def f32(self, off, n, partitions=128):
        assert off + n <= self.words, (off, n, self.words)
        return self.t[0:partitions, off:off + n]

    def bf16(self, off, n_elem, partitions=128):
        assert n_elem % 2 == 0
        nw = n_elem // 2
        assert off + nw <= self.words, (off, n_elem, self.words)
        return self.t[0:partitions, off:off + nw].bitcast(BF16)

"""Megakernel: prologue + block 0 (LN -> RWKV-scan attn + MoE -> residual).

Parameterized by SEQ so small-shape sim tests and full-shape device runs share
code. mk1.py adds block 1 (S4 conv + quantized FFN).

Memory plan (per-partition words, SBUF ~53k words usable):
  arena CORE [0, AW): phase-scoped overlays
    block0: xn [0,XN_W) | xatt [XN_W,2*XN_W) | R [2*XN_W, +R_W) phase locals
    block1: chain regions [0, 32768) + locals [32768, +B1L_W)
  statics: identities/ones/cval/cols/rowscr/mask8/b2_b/... (small)
  big per-token rows live in DRAM (rowd/rowd_b/statsh) and are staged as
  [1 x 512] strips on demand.
"""
import numpy as np
import ml_dtypes
from contextlib import ExitStack

import concourse.bass as bass
import concourse.mybir as mybir
import concourse.bass_isa as bass_isa
from concourse import dve_ops
from concourse import library_config


LN_EPS = 1e-5

R_S, R_NEGM, R_S2, R_Q, R_NMQ, R_XSC, R_DS, R_NDMS = range(8)
B_NEGMS, B_NEGMSS2 = 0, 1


class MKBuild:
    def __init__(self, SEQ, D=1024, NCORES=8):
        self.SEQ, self.D, self.NC = SEQ, D, NCORES
        self.T = SEQ // 2
        self.DC = D // 128
        self.NJ = self.T // 128
        self.NT = self.T // 512
        self.SPAN = min(1024, self.T)
        self.NSPAN = self.T // self.SPAN
        self.E, self.H = 8, 256
        self.NFFT = 2 * SEQ
        self.QNZ = SEQ // 128
        self.QD = self.NFFT // 128
        self.KB = self.QD // 2
        self.PSTR = max(self.KB, 64)
        self.PACKE = 128 // self.PSTR
        self.NGRP = 128 // self.PACKE
        self.CPASS = 128
        self.NPASS = (D // 2) // self.CPASS

    def barrier_lite(self, tag='bar'):
        c, nc = self.c, self.nc
        for q in sorted(c.dma_queues - {'cw_ld', 'cr_ld'}):
            c.wait(nc.sync, q)
        sem = c.sem(tag)
        for e in self.engines:
            e.drain().then_inc(sem, 1)
        c.ctr[tag] += len(self.engines)
        for e in self.engines:
            e.wait_ge(sem, c.ctr[tag])

    def col(self, name, n=1):
        if name not in self.col_ix:
            o = self.col_used
            self.col_used += n
            assert self.col_used <= self.cols.shape[1]
            self.col_ix[name] = (o, n)
        o, n_ = self.col_ix[name]
        return self.cols[:, o:o + n_]

    def colb(self, name, n=1):
        if name not in self.colb_ix:
            o = self.colb_used
            self.colb_used += n
            assert self.colb_used <= self.colsb.shape[1]
            self.colb_ix[name] = (o, n)
        o, n_ = self.colb_ix[name]
        return self.colsb[:, o:o + n_]

    def xnc(self, ch, sl=None):
        a = self.xn_bf[:, ch * self.T:(ch + 1) * self.T]
        return a if sl is None else a[:, sl]

    def xac(self, ch, sl=None):
        a = self.xatt_bf[:, ch * self.T:(ch + 1) * self.T]
        return a if sl is None else a[:, sl]

    def cols_to_row(self, colap, rowap, ncols, pfx='', bf=False):
        c, nc = self.c, self.nc
        ps = self.pb[7]
        c.wait(nc.tensor, pfx + 'vv')
        c.mark(nc.tensor.transpose(ps[0:ncols, 0:128], colap, self.id_f[:, :]),
               pfx + 'mm')
        c.wait(nc.vector, pfx + 'mm')
        c.wait(nc.vector, pfx + 'ld')
        src = self.rowscr[0:ncols, 0:128]
        nc.vector.tensor_copy(src, ps[0:ncols, 0:128])
        if bf:
            nc.vector.drain()
            nc.vector.tensor_copy(self.rowscr_b[0:ncols, 0:128], src)
            src = self.rowscr_b[0:ncols, 0:128]
        c.mark(nc.vector.drain(), pfx + 'vv')
        c.wait(nc.sync, pfx + 'vv')
        c.dma(nc.sync, rowap, src, q=pfx + 'ld')
        c.wait(nc.sync, pfx + 'ld')

    def recip_nr(self, eng, out, in_, scratch):
        eng.reciprocal(out=out, in_=in_)
        eng.drain()

    def build(self, tail='block0'):
        s = self
        nc = bass.Bass("TRN2", target_bir_lowering=False, num_devices=s.NC)
        s.nc = nc
        s.stack = ExitStack()
        s.stack.__enter__()
        c = Ctx(nc, s.stack)
        s.c = c
        s.engines = [nc.sync, nc.vector, nc.scalar, nc.tensor, nc.gpsimd]
        T, D, DC = s.T, s.D, s.DC

        s.col_ix, s.col_used = {}, 0
        s.colb_ix, s.colb_used = {}, 0

        din = lambda n, sh: nc.dram_tensor(n, list(sh), F32, kind="ExternalInput").ap()
        s.x_in = din("x_in", [T, D])
        s.W_rkv = din("W_rkv", [D, 3 * D]); s.b_rkv = din("b_rkv", [3 * D])
        s.W_o = din("W_o", [D, D]); s.b_o = din("b_o", [D])
        s.nu_log = din("nu_log", [D]); s.gamma_log = din("gamma_log", [D])
        s.Wg = din("Wg", [8, D])
        s.ln_scale = din("ln_scale", [8, D]); s.ln_bias = din("ln_bias", [8, D])
        s.W1 = din("W1", [8, D, 256]); s.b1 = din("b1", [8, 256])
        s.W2 = din("W2", [8, 256, D]); s.b2 = din("b2", [8, D])
        s.l1w = din("l1w", [D, D]); s.l1b = din("l1b", [D])
        s.Dskip = din("Dskip", [1]); s.s4_step = din("s4_step", [1])
        s.s4_C_re = din("s4_C_re", [D, 64]); s.s4_C_im = din("s4_C_im", [D, 64])
        for t in ["B_re", "B_im", "P_re", "P_im", "Q_re", "Q_im",
                  "diag_re", "diag_im"]:
            setattr(s, "s4_" + t, din("s4_" + t, [64]))

        dram = lambda n, sh, dt=F32: nc.dram_tensor(n, sh, dt, kind="Internal").ap()
        shared = lambda n, sh, dt=F32: nc.dram_tensor(
            n, sh, dt, kind="Internal", addr_space="Shared").ap()
        s.wrkv_bf = dram("wrkv_bf", [D, 3 * D], BF16)
        s.wo_bf = dram("wo_bf", [D, D], BF16)
        s.w2_bf = dram("w2_bf", [8, 256, D], BF16)
        s.w1p = dram("w1p", [8, D, 256], BF16)
        s.xfeat = dram("xfeat", [D, T])
        s.rowd = dram("rowd", [8, T])
        s.rowd_b = dram("rowd_b", [2, T], BF16)
        s.rowtmp = dram("rowtmp", [2, 512])
        s.csrow_dram = dram("csrow_dram", [1, 3 * D], BF16)
        s.csw1_dram = dram("csw1_dram", [1, 8 * 256], BF16)
        s.wqt_dram = dram("wqt_dram", [D, D], BF16)
        s.kf_dram = dram("kf_dram", [128, s.NGRP, 2, D // 2], BF16)
        s.hfeat_dram = dram("hfeat_dram", [s.NPASS, 128, T], BF16)
        s.x1sh = shared("x1sh", [2, D, T])
        s.carrysh = shared("carrysh", [2, 128, 16])
        s.statsh = shared("statsh", [2, 2, T])
        s.hsh = shared("hsh", [2, s.NPASS, 128, T], BF16)
        s.y_out = nc.dram_tensor("y_out", [T, D], F32, kind="ExternalOutput").ap()

        s.ident_f = c.const("ident_f", np.eye(128, dtype=np.float32))
        s.id_f = c.sbuf("id_f", [128, 128])
        s.ident_b = c.const("ident_b", np.eye(128, dtype=BF))
        s.id_b = c.sbuf("id_b", [128, 128], BF16)
        s.ones_f = c.const("ones_f", np.ones((128, 128), np.float32))
        s.on_f = c.sbuf("on_f", [128, 128])
        s.ones_b = c.const("ones_b", np.ones((128, 128), dtype=BF))
        s.on_b = c.sbuf("on_b", [128, 128], BF16)
        oh = np.zeros((8, 8 * 128), dtype=BF)
        for e in range(8):
            oh[e, e * 128:(e + 1) * 128] = 1
        s.onehot_c = c.const("onehot", oh)
        s.onehot_sb = c.sbuf("onehot_sb", [8, 8 * 128], BF16)
        s.cvals = [1e-5, 1e-6, 0.0, 1.0, -1.0, 0.5]
        s.cval_const = c.const("cvals", np.tile(
            np.array(s.cvals, np.float32)[None, :], (128, 1)))
        s.cval_sb = c.sbuf("cval_sb", [128, len(s.cvals)])
        for _i, _v in enumerate(s.cvals):
            nc.const_aps.register(F32, _v, s.cval_sb[:, _i:_i + 1])

        def onr(p=0, n=128):
            return s.on_f[p:p + 1, 0:n]
        def onc(n=128):
            return s.on_f[0:n, 0:1]
        s.onr, s.onc = onr, onc
        def onrb(p=0, n=128):
            return s.on_b[p:p + 1, 0:n]
        def oncb(n=128):
            return s.on_b[0:n, 0:1]
        s.onrb, s.oncb = onrb, oncb

        s.XN_W = DC * T // 2
        s.R_W = 11264
        s.B1L_W = 7168
        s.AW = max(2 * s.XN_W + s.R_W, 32768 + s.B1L_W)
        s.arena = Arena(c, s.AW)
        A = s.arena
        s.O_R = 2 * s.XN_W
        s.xn_bf = A.bf16(0, 2 * s.XN_W)
        s.xatt_bf = A.bf16(s.XN_W, 2 * s.XN_W)

        def rf(off, n, partitions=128):
            assert off + n <= s.R_W, (off, n)
            return A.f32(s.O_R + off, n, partitions)
        def rb(off, n_elem, partitions=128):
            assert off + n_elem // 2 <= s.R_W, (off, n_elem)
            return A.bf16(s.O_R + off, n_elem, partitions)
        s.rf, s.rb = rf, rb

        s.rowscr = c.sbuf("rowscr", [128, 128])
        s.rowscr_b = c.sbuf("rowscr_b", [128, 128], BF16)
        s.stg_b = c.sbuf("stg_b", [1, 512], BF16)
        s.cols = c.sbuf("cols", [128, 768])
        s.colsb = c.sbuf("colsb", [128, 64], BF16)
        s.mask8 = c.sbuf("mask8", [8, T], BF16)
        s.cswg = c.sbuf("cswg", [1, 8])
        s.wgt_sb = c.sbuf("wgt_sb", [128, 8 * DC])
        s.b2_b = c.sbuf("b2_b", [8, D], BF16)
        s.pb = [c.psum(f"pb{i}") for i in range(8)]

        s.emit(tail)
        return nc, dict(c.consts)

    def emit(self, tail):
        s, c, nc = self, self.c, self.nc
        with nc.named_scope("p0_prologue"):
            self.emit_prologue()
            self.barrier_lite()
        with nc.named_scope("p1_stats"):
            self.emit_stats()
            self.barrier_lite()
        with nc.named_scope("p2_rowmath"):
            self.emit_rowmath()
            self.barrier_lite()
        with nc.named_scope("p3_xn"):
            self.emit_xn_pass()
            self.barrier_lite()

        engines = self.engines
        idxs = [e.compute_val(e.partition_id() % 2) for e in engines]
        base = dict(c.ctr)
        base_lw = dict(c.last_wait)
        deltas = {}

        def run_arm(fn, tag):
            for k in list(c.ctr):
                c.ctr[k] = base.get(k, 0)
            c.last_wait.clear()
            c.last_wait.update(base_lw)
            fn()
            deltas[tag] = {k: v - base.get(k, 0) for k, v in c.ctr.items()
                           if v != base.get(k, 0)}
            for k in list(c.ctr):
                c.ctr[k] = base.get(k, 0)
            c.last_wait.clear()
            c.last_wait.update(base_lw)

        with nc.named_scope("p4_sw1"):
            for arm in nc.Switch(engines=engines, index=idxs, n=2):
                run_arm((lambda: self.emit_attn(0, True)) if arm == 0
                        else (lambda: self.emit_moe(1, True)),
                        'attn0' if arm == 0 else 'moe1')
            nc.all_core_barrier()
        with nc.named_scope("p5_sw2"):
            for arm in nc.Switch(engines=engines, index=idxs, n=2):
                run_arm((lambda: self.emit_moe(0, False)) if arm == 0
                        else (lambda: self.emit_attn(1, False)),
                        'moe0' if arm == 0 else 'attn1')

        skip = {'cw_ld', 'cr_ld'}
        da0 = {k: v for k, v in deltas['attn0'].items() if k not in skip}
        da1 = {k: v for k, v in deltas['attn1'].items() if k not in skip}
        assert da0 == da1, (da0, da1)
        assert deltas['moe0'] == deltas['moe1'], (deltas['moe0'], deltas['moe1'])
        allk = set(deltas['attn0']) | set(deltas['moe0']) | set(base)
        for k in allk:
            c.ctr[k] = base.get(k, 0) + deltas['attn0'].get(k, 0) + deltas['moe0'].get(k, 0)
            c.last_wait[k] = c.ctr[k]
        nc.all_core_barrier()

        if tail == 'block0':
            self.emit_debug_tail()
        else:
            tail(self)

    def emit_debug_tail(self):
        s, c, nc = self, self.c, self.nc
        dbg = nc.dram_tensor("dbg_x1", [s.D, s.T], F32, kind="ExternalOutput").ap()
        buf = s.rf(0, s.T)
        idxs = [nc.sync.compute_val(nc.sync.partition_id() % 2)]
        base = c.ctr.get('dbg', 0)
        delta = 0
        for arm in nc.Switch(engines=[nc.sync], index=idxs, n=2):
            c.ctr['dbg'] = base
            c.last_wait['dbg'] = base
            for ch in range(s.DC):
                c.dma(nc.sync, buf[:, :], s.x1sh[arm, ch * 128:(ch + 1) * 128, :], q='dbg')
                c.wait(nc.sync, 'dbg')
                c.dma(nc.sync, dbg[ch * 128:(ch + 1) * 128, :], buf[:, :], q='dbg')
                c.wait(nc.sync, 'dbg')
            delta = c.ctr['dbg'] - base
        c.ctr['dbg'] = base + delta
        c.last_wait['dbg'] = base + delta

    def emit_prologue(self):
        s, c, nc = self, self.c, self.nc
        D, T, DC = s.D, s.T, s.DC

        c.dma(nc.sync, s.id_f[:, :], s.ident_f[:, :])
        c.dma(nc.sync, s.id_b[:, :], s.ident_b[:, :])
        c.dma(nc.sync, s.on_f[:, :], s.ones_f[:, :])
        c.dma(nc.sync, s.on_b[:, :], s.ones_b[:, :])
        c.dma(nc.sync, s.onehot_sb[:, :], s.onehot_c[:, :])
        c.dma(nc.sync, s.cval_sb[:, :], s.cval_const[:, :])

        c.dma(nc.gpsimd, s.wrkv_bf[:, :], s.W_rkv[:, :], q='gd')
        c.dma(nc.gpsimd, s.wo_bf[:, :], s.W_o[:, :], q='gd')
        c.dma(nc.gpsimd, s.w2_bf[:, :, :], s.W2[:, :, :], q='gd')
        c.dma(nc.gpsimd, s.b2_b[:, :], s.b2[:, :], q='gd')

        nu_c = self.col("nu", 8); galog_c = self.col("galog", 8)
        s.lam_c = self.col("lam", 8); s.gam_c = self.col("gam", 8)
        nuv_c = self.col("nuv", 8)
        c.dma(nc.sync, nu_c, s.nu_log.rearrange("(c p) -> p c", p=128), slow=True)
        c.dma(nc.sync, galog_c, s.gamma_log.rearrange("(c p) -> p c", p=128), slow=True)
        s.brkv_c = self.col("brkv", 24)
        c.dma(nc.sync, s.brkv_c, s.b_rkv.rearrange("(c p) -> p c", p=128), slow=True)
        s.bo_c = self.col("bo", 8)
        c.dma(nc.sync, s.bo_c, s.b_o.rearrange("(c p) -> p c", p=128), slow=True)
        lnsc_c = self.col("lnsc", 64)
        lnbi_c = self.col("lnbi", 64)
        c.dma(nc.sync, lnsc_c, s.ln_scale.rearrange("e (c p) -> p (e c)", p=128), slow=True)
        c.dma(nc.sync, lnbi_c, s.ln_bias.rearrange("e (c p) -> p (e c)", p=128), slow=True)
        b1_c = self.col("b1c", 16)
        c.dma(nc.sync, b1_c, s.b1.rearrange("e (c p) -> p (e c)", p=128), slow=True)

        c.wait(nc.scalar, 'ld')
        nc.scalar.activation(nuv_c, nu_c, AF.Exp)
        nc.scalar.drain()
        nc.scalar.activation(s.lam_c, nuv_c, AF.Exp, scale=-1.0)
        c.mark(nc.scalar.drain(), 'ss')
        c.wait(nc.vector, 'ld')
        nc.vector.tensor_add(galog_c, galog_c, nu_c)
        c.mark(nc.vector.drain(), 'vv')
        c.wait(nc.scalar, 'vv')
        nc.scalar.activation(galog_c, galog_c, AF.Exp)
        c.mark(nc.scalar.drain(), 'ss')
        c.wait(nc.vector, 'ss')
        nc.vector.tensor_scalar_add(s.gam_c, galog_c, -1.0)
        nc.vector.drain()

        wg_sb = s.rf(0, D, 8)
        c.dma(nc.sync, wg_sb, s.Wg[:, :])
        c.wait(nc.tensor, 'ld')
        ps = s.pb[6]
        for kf in range(DC):
            c.mark(nc.tensor.transpose(ps[0:128, kf * 8:(kf + 1) * 8],
                                       wg_sb[0:8, kf * 128:(kf + 1) * 128],
                                       s.id_f[0:8, 0:8]), 'mm')
        c.wait(nc.vector, 'mm')
        nc.vector.tensor_copy(s.wgt_sb[:, :], ps[0:128, 0:8 * DC])
        c.mark(nc.vector.drain(), 'vv')
        ps2 = s.pb[5]
        c.wait(nc.tensor, 'vv')
        for kf in range(DC):
            m = nc.tensor.matmul(ps2[0:1, 0:8], s.onc(),
                                 s.wgt_sb[:, kf * 8:(kf + 1) * 8],
                                 start=(kf == 0), stop=(kf == DC - 1))
        c.mark(m, 'mm')
        c.wait(nc.vector, 'mm')
        nc.vector.tensor_copy(s.cswg[0:1, :], ps2[0:1, 0:8])
        c.mark(nc.vector.drain(), 'vv')

        self.barrier_lite()

        wbuf = [s.rb(0, 512), s.rb(256, 512)]
        ps3 = s.pb[4]
        for ob in range(3 * D // 512):
            c.wait(nc.tensor, 'vv')
            for kf in range(DC):
                b = wbuf[kf % 2]
                c.wait(nc.sync, 'mm')
                c.dma(nc.sync, b[:, :], s.wrkv_bf[kf * 128:(kf + 1) * 128,
                                                  ob * 512:(ob + 1) * 512])
                c.wait(nc.tensor, 'ld')
                m = nc.tensor.matmul(ps3[0:1, 0:512], s.oncb(), b[:, :],
                                     start=(kf == 0), stop=(kf == DC - 1))
                c.mark(m, 'mm')
            c.wait(nc.vector, 'mm')
            c.wait(nc.vector, 'ld')
            nc.vector.tensor_copy(s.stg_b[0:1, :], ps3[0:1, 0:512])
            c.mark(nc.vector.drain(), 'vv')
            c.wait(nc.sync, 'vv')
            c.dma(nc.sync, s.csrow_dram[0:1, ob * 512:(ob + 1) * 512],
                  s.stg_b[0:1, :])
        self.barrier_lite()

        w1buf2 = [s.rf(512, 256), s.rf(896, 256)]
        w1bufb2 = [s.rb(768, 256), s.rb(1152, 256)]
        s.beta_c = self.col("beta", 16)
        bia_cb = self.colb("lnbi_b", 64)
        nc.vector.tensor_copy(bia_cb, lnbi_c)
        c.mark(nc.vector.drain(), 'vv')
        c.wait(nc.tensor, 'vv')
        psb = [s.pb[3], s.pb[1]]
        psc = s.pb[2]
        for e in range(8):
            c.wait(nc.tensor, 'vv')
            for kf in range(DC):
                i = kf % 2
                c.wait_lag(nc.sync, 'vv', 1)
                c.dma(nc.sync, w1buf2[i][:, :], s.W1[e, kf * 128:(kf + 1) * 128, :])
                c.wait(nc.vector, 'ld')
                c.wait_lag(nc.vector, 'mm', 1)
                nc.vector.tensor_scalar(w1bufb2[i][:, :], w1buf2[i][:, :],
                                        lnsc_c[:, e * 8 + kf:e * 8 + kf + 1], None,
                                        op0=AL.mult)
                c.mark(nc.vector.drain(), 'vv')
                c.wait(nc.sync, 'vv')
                c.dma(nc.sync, s.w1p[e, kf * 128:(kf + 1) * 128, :], w1bufb2[i][:, :])
                c.wait(nc.tensor, 'vv')
                for hc in range(2):
                    nc.tensor.matmul(psb[hc][0:128, 0:1],
                                     w1bufb2[i][:, hc * 128:(hc + 1) * 128],
                                     bia_cb[:, e * 8 + kf:e * 8 + kf + 1],
                                     start=(kf == 0), stop=(kf == DC - 1))
                m = nc.tensor.matmul(psc[0:1, 0:256],
                                     s.oncb(), w1bufb2[i][:, :],
                                     start=(kf == 0), stop=(kf == DC - 1))
                c.mark(m, 'mm')
            c.wait(nc.vector, 'mm')
            c.wait(nc.vector, 'ld')
            nc.vector.tensor_copy(s.stg_b[0:1, 0:256], psc[0:1, 0:256])
            c.mark(nc.vector.drain(), 'vv')
            c.wait(nc.sync, 'vv')
            c.dma(nc.sync, s.csw1_dram[0:1, e * 256:(e + 1) * 256],
                  s.stg_b[0:1, 0:256])
            for hc in range(2):
                nc.vector.tensor_add(s.beta_c[:, e * 2 + hc:e * 2 + hc + 1],
                                     psb[hc][0:128, 0:1],
                                     b1_c[:, e * 2 + hc:e * 2 + hc + 1])
            c.mark(nc.vector.drain(), 'vv')
            c.wait(nc.sync, 'vv')
            c.wait(nc.tensor, 'vv')

    def emit_stats(self):
        s, c, nc = self, self.c, self.nc
        NJ, D = s.NJ, s.D
        s.mc = self.col("mc", NJ)
        s.qc = self.col("qc", NJ)
        xt = [s.rf(0, 1024), s.rf(1024, 1024)]
        sq = s.rf(2048, 1024)
        for j in range(NJ):
            b = xt[j % 2]
            if j >= 2:
                c.wait(nc.sync, 'ss', c.ctr['ss'])
                c.wait(nc.sync, 'vv', c.ctr['vv'])
            c.dma(nc.sync, b[:, :], s.x_in[j * 128:(j + 1) * 128, :])
            c.wait(nc.scalar, 'ld')
            nc.scalar.activation(sq[:, :], b[:, :], AF.Square,
                                 accum_out=s.qc[:, j:j + 1])
            c.mark(nc.scalar.drain(), 'ss')
            c.wait(nc.vector, 'ld')
            nc.vector.reduce_sum(s.mc[:, j:j + 1], b[:, :], axis=AX.X)
            c.mark(nc.vector.drain(), 'vv')

    def emit_rowmath(self):
        s, c, nc = self, self.c, self.nc
        NJ, D, T = s.NJ, s.D, s.T
        col = self.col
        m = col("m", NJ); v = col("v", NJ); sc = col("sc", NJ)
        s2c = col("s2c", NJ); t0 = col("t0", NJ); t1 = col("t1", NJ)
        V, S = nc.vector, nc.scalar
        V.tensor_scalar(m, s.mc, 1.0 / D, None, op0=AL.mult)
        V.drain()
        V.tensor_scalar(t0, s.qc, 1.0 / D, None, op0=AL.mult)
        V.drain()
        V.tensor_mul(t1, m, m)
        V.drain()
        V.tensor_sub(v, t0, t1)
        c.mark(V.drain(), 'vv')
        c.wait(S, 'vv')
        S.activation(t0, v, AF.Sqrt, bias=float(LN_EPS))
        c.mark(S.drain(), 'ss')
        c.wait(V, 'ss')
        self.recip_nr(V, sc, t0, t1)
        V.tensor_mul(t0, sc, sc)
        V.drain()
        V.tensor_mul(t0, t0, v)
        c.mark(V.drain(), 'vv')
        c.wait(S, 'vv')
        S.activation(t0, t0, AF.Sqrt, bias=float(LN_EPS))
        c.mark(S.drain(), 'ss')
        c.wait(V, 'ss')
        self.recip_nr(V, s2c, t0, t1)
        nms = col("nms", NJ); nmss2 = col("nmss2", NJ); nm = col("nm", NJ)
        ss2 = col("ss2", NJ)
        V.tensor_mul(nms, m, sc)
        V.drain()
        V.tensor_scalar(nms, nms, -1.0, None, op0=AL.mult)
        V.drain()
        V.tensor_mul(ss2, sc, s2c)
        V.drain()
        V.tensor_mul(nmss2, m, ss2)
        V.drain()
        V.tensor_scalar(nmss2, nmss2, -1.0, None, op0=AL.mult)
        V.drain()
        V.tensor_scalar(nm, m, -1.0, None, op0=AL.mult)
        c.mark(V.drain(), 'vv')
        self.barrier_lite()
        rr = lambda i: s.rowd[i].rearrange("(j p) -> j p", p=128)
        rrb = lambda i: s.rowd_b[i].rearrange("(j p) -> j p", p=128)
        self.cols_to_row(sc, rr(R_S), NJ)
        self.cols_to_row(s2c, rr(R_S2), NJ)
        self.cols_to_row(nm, rr(R_NEGM), NJ)
        self.cols_to_row(nms, rrb(B_NEGMS), NJ, bf=True)
        self.cols_to_row(nmss2, rrb(B_NEGMSS2), NJ, bf=True)

    def emit_xn_pass(self):
        s, c, nc = self, self.c, self.nc
        NJ, D, DC, T = s.NJ, s.D, s.DC, s.T
        V, G = nc.vector, nc.gpsimd
        xt = [s.rf(0, 1024), s.rf(1024, 1024)]
        sB = s.rf(2048, 128)
        xf = [s.rf(2176 + i * 128, 128) for i in range(4)]
        sseg = s.rf(2688, 128, 1)
        nseg = s.rf(2816, 128, 1)
        lg8 = s.rf(2944, 128, 8)
        t8 = s.rf(3072, 8)
        maxc = s.rf(3080, 1)
        psT = [s.pb[i] for i in range(4)]
        psS = s.pb[4]
        psL = s.pb[5]
        c.sem('gg')
        for j in range(NJ):
            b = xt[j % 2]
            c.dma(nc.sync, b[:, :], s.x_in[j * 128:(j + 1) * 128, :])
            c.dma(nc.sync, sseg, s.rowd[R_S, j * 128:(j + 1) * 128][None, :])
            c.dma(nc.sync, nseg, s.rowd[R_NEGM, j * 128:(j + 1) * 128][None, :])
            c.wait(nc.tensor, 'vv')
            c.wait(nc.tensor, 'ld')
            c.mark(nc.tensor.matmul(psS[0:128, 0:128], s.onr(), sseg,
                                    start=True, stop=True), 'mm')
            c.wait(nc.vector, 'mm')
            nc.vector.tensor_copy(sB[:, :], psS[0:128, 0:128])
            c.mark(nc.vector.drain(), 'vv')
            for ch in range(DC):
                pt = psT[ch % 4]
                c.wait(nc.tensor, 'vv')
                c.wait(nc.tensor, 'ss')
                c.mark(nc.tensor.transpose(pt[0:128, 0:128],
                                           b[:, ch * 128:(ch + 1) * 128],
                                           s.id_f[:, :]), 'mm')
                c.wait(nc.vector, 'mm')
                nc.vector.tensor_mul(s.xnc(ch)[:, j * 128:(j + 1) * 128],
                                     pt[0:128, 0:128], sB[:, :])
                c.mark(nc.vector.drain(), 'vv')
                c.wait(nc.scalar, 'mm')
                c.wait(nc.scalar, 'ld')
                xfb = xf[ch % 4]
                nc.scalar.copy(xfb[:, :], pt[0:128, 0:128])
                c.mark(nc.scalar.drain(), 'ss')
                c.wait(nc.sync, 'ss')
                c.dma(nc.sync, s.xfeat[ch * 128:(ch + 1) * 128,
                                       j * 128:(j + 1) * 128], xfb[:, :])
                c.wait(nc.tensor, 'ss')
                mmi = nc.tensor.matmul(psL[0:8, 0:128], s.wgt_sb[:, ch * 8:(ch + 1) * 8],
                                       xfb[:, :], start=(ch == 0), stop=False)
            mmi = nc.tensor.matmul(psL[0:8, 0:128], s.cswg[0:1, :],
                                   nseg, start=False, stop=True)
            c.mark(mmi, 'mm')
            c.wait(V, 'mm')
            V.tensor_copy(lg8[:, :], psL[0:8, 0:128])
            c.mark(V.drain(), 'vv')
            c.wait(nc.tensor, 'vv')
            c.mark(nc.tensor.transpose(psS[0:128, 0:8], lg8[:, :],
                                       s.id_f[0:8, 0:8]), 'mm')
            c.wait(V, 'mm')
            V.tensor_reduce(maxc, psS[0:128, 0:8], axis=AX.X, op=AL.max)
            V.drain()
            V.tensor_scalar(t8, psS[0:128, 0:8], maxc, None, op0=AL.is_ge)
            c.mark(V.drain(), 'vv')
            c.wait(nc.tensor, 'vv')
            c.mark(nc.tensor.transpose(psS[0:8, 0:128], t8,
                                       s.id_f[:, :]), 'mm')
            c.wait(V, 'mm')
            V.tensor_copy(s.mask8[:, j * 128:(j + 1) * 128], psS[0:8, 0:128])
            c.mark(V.drain(), 'vv')
            c.wait(nc.sync, 'vv')

    def emit_attn(self, parity, first):
        s, c, nc = self, self.c, self.nc
        T, D, DC, NT, SPAN, NSPAN = s.T, s.D, s.DC, s.NT, s.SPAN, s.NSPAN
        V, S = nc.vector, nc.scalar
        NTS = SPAN // 512 if SPAN >= 512 else 1
        SP2 = SPAN // 2

        o = 0
        def nx(n):
            nonlocal o
            r = o; o += n; return r
        wbuf = s.rb(nx(1536), 3072)
        csr = s.rb(nx(1536), 3 * D, 1)
        negms = s.rb(nx(T // 2), T, 1)
        kbf = s.rb(nx(SP2), SPAN)
        kvbf = s.rb(nx(SP2), SPAN)
        u1bf = s.rb(nx(SP2), SPAN)
        rbf = s.rb(nx(SP2), SPAN)
        u2f = s.rf(nx(SPAN), SPAN)
        rf1 = s.rf(nx(SPAN), SPAN)
        wob = s.rb(nx(512), 1024)
        xftile = s.rf(nx(512), 512)
        x1t = s.rf(nx(512), 512)
        cgat = s.rf(nx(16), 16)
        ctile = s.rf(nx(16), 16)
        cc = s.rf(nx(2), 2)
        assert o <= s.R_W, o

        c.dma(nc.sync, csr, s.csrow_dram[0:1, :], q='a_ld')
        c.dma(nc.sync, negms, s.rowd_b[B_NEGMS:B_NEGMS + 1, :], q='a_ld')
        if parity == 1:
            c.dma(nc.sync, ctile[:, :], s.carrysh[0, :, :], q='cr_ld')
            c.wait(nc.vector, 'cr_ld', c.ctr['cr_ld'])

        psRsets = [[s.pb[0], s.pb[1], s.pb[2]], [s.pb[3], s.pb[4], s.pb[5]]]

        til = 0
        for ch in range(DC):
            for oi, oc in enumerate([ch, 8 + ch, 16 + ch]):
                c.dma(nc.sync,
                      wbuf[:, oi * 8 * 128:(oi + 1) * 8 * 128].rearrange(
                          "p (kf o) -> p kf o", o=128),
                      s.wrkv_bf[:, oc * 128:(oc + 1) * 128].rearrange(
                          "(kf p) o -> p kf o", p=128), q='a_ld')
            for half in range(NSPAN):
                for ntl in range(NTS):
                    t0 = half * SPAN + ntl * 512
                    sl_ps = slice(0, 512)
                    sl_sp = slice(ntl * 512, ntl * 512 + 512)
                    psR = psRsets[til % 2]
                    til += 1
                    c.wait(nc.tensor, 'a_ld')
                    c.wait_lag(nc.tensor, 'a_vv', 1)
                    for oi in range(3):
                        for kf in range(DC):
                            mmi = nc.tensor.matmul(
                                psR[oi][:, sl_ps],
                                wbuf[:, (oi * 8 + kf) * 128:(oi * 8 + kf + 1) * 128],
                                s.xnc(kf)[:, t0:t0 + 512],
                                start=(kf == 0), stop=False)
                        mmi = nc.tensor.matmul(
                            psR[oi][:, sl_ps],
                            csr[0:1, (oi * 8 + ch) * 128:(oi * 8 + ch + 1) * 128],
                            negms[0:1, t0:t0 + 512],
                            start=False, stop=True)
                    c.mark(mmi, 'a_mm')
                    c.wait(S, 'a_mm')
                    c.wait(S, 'a_vv')
                    S.activation(rbf[:, sl_sp], psR[0][:, sl_ps], AF.Sigmoid,
                                 bias=s.brkv_c[:, ch:ch + 1])
                    S.drain()
                    S.activation(kbf[:, sl_sp], psR[1][:, sl_ps], AF.Exp,
                                 bias=s.brkv_c[:, 8 + ch:8 + ch + 1])
                    c.mark(S.drain(), 'a_ss')
                    c.wait(V, 'a_ss')
                    V.scalar_tensor_tensor(kvbf[:, sl_sp], psR[2][:, sl_ps],
                                           s.brkv_c[:, 16 + ch:16 + ch + 1],
                                           kbf[:, sl_sp], op0=AL.add, op1=AL.mult)
                    c.mark(V.drain(), 'a_vv')
                if half == 0:
                    if parity == 0:
                        init_kv, init_k = 0.0, 0.0
                    else:
                        init_kv = ctile[:, ch:ch + 1]
                        init_k = ctile[:, 8 + ch:8 + ch + 1]
                else:
                    init_kv = cc[:, 0:1]
                    init_k = cc[:, 1:2]
                lamb = s.lam_c[:, ch:ch + 1].broadcast_to([128, SPAN])
                V.tensor_tensor_scan(u1bf[:, :], lamb, kvbf[:, :], init_kv,
                                     op0=AL.mult, op1=AL.add)
                V.drain()
                V.tensor_tensor_scan(u2f[:, :], lamb, kbf[:, :], init_k,
                                     op0=AL.mult, op1=AL.add)
                V.drain()
                if half < NSPAN - 1:
                    V.tensor_copy(cc[:, 0:1], u1bf[:, SPAN - 1:SPAN])
                    V.tensor_copy(cc[:, 1:2], u2f[:, SPAN - 1:SPAN])
                    V.drain()
                if parity == 0 and half == NSPAN - 1:
                    V.tensor_copy(cgat[:, ch:ch + 1], u1bf[:, SPAN - 1:SPAN])
                    V.tensor_copy(cgat[:, 8 + ch:8 + ch + 1], u2f[:, SPAN - 1:SPAN])
                    V.drain()
                V.scalar_tensor_tensor(u1bf[:, :], kvbf[:, :],
                                       s.gam_c[:, ch:ch + 1], u1bf[:, :],
                                       op0=AL.mult, op1=AL.add)
                V.drain()
                V.scalar_tensor_tensor(u2f[:, :], kbf[:, :],
                                       s.gam_c[:, ch:ch + 1], u2f[:, :],
                                       op0=AL.mult, op1=AL.add)
                V.drain()
                V.tensor_scalar_add(u2f[:, :], u2f[:, :], 1e-6)
                V.drain()
                V.reciprocal(out=rf1[:, :], in_=u2f[:, :])
                V.drain()
                V.tensor_mul(u1bf[:, :], u1bf[:, :], rf1[:, :])
                V.drain()
                sl = slice(half * SPAN, (half + 1) * SPAN)
                V.tensor_mul(s.xac(ch)[:, sl], u1bf[:, :], rbf[:, :])
                c.mark(V.drain(), 'a_vv')
            c.wait(nc.sync, 'a_mm')

        if parity == 0:
            c.wait(nc.sync, 'a_vv')
            c.dma(nc.sync, s.carrysh[0, :, :], cgat[:, :], q='cw_ld')
            c.wait(nc.sync, 'cw_ld', c.ctr['cw_ld'])

        self.barrier_lite('a_bar')

        psW = s.pb[3]
        for dc in range(DC):
            c.dma(nc.sync, wob[:, :].rearrange("p (ku o) -> p ku o", o=128),
                  s.wo_bf[:, dc * 128:(dc + 1) * 128].rearrange(
                      "(ku p) o -> p ku o", p=128), q='a_ld')
            for nt in range(NT):
                src_ap = (s.xfeat[dc * 128:(dc + 1) * 128, nt * 512:(nt + 1) * 512]
                          if first else
                          s.x1sh[parity, dc * 128:(dc + 1) * 128, nt * 512:(nt + 1) * 512])
                c.dma(nc.sync, xftile[:, :], src_ap, q='a_ld')
                c.wait(nc.tensor, 'a_ld')
                c.wait(nc.tensor, 'a_vv')
                for ku in range(DC):
                    mmi = nc.tensor.matmul(psW[:, 0:512],
                                           wob[:, ku * 128:(ku + 1) * 128],
                                           s.xac(ku)[:, nt * 512:(nt + 1) * 512],
                                           start=(ku == 0), stop=(ku == DC - 1))
                c.mark(mmi, 'a_mm')
                c.wait(V, 'a_mm')
                c.wait(V, 'a_ld')
                V.scalar_tensor_tensor(x1t[:, :], psW[:, 0:512],
                                       s.bo_c[:, dc:dc + 1], xftile[:, :],
                                       op0=AL.add, op1=AL.add)
                c.mark(V.drain(), 'a_vv')
                c.wait(nc.sync, 'a_vv')
                c.dma(nc.sync, s.x1sh[parity, dc * 128:(dc + 1) * 128,
                                      nt * 512:(nt + 1) * 512], x1t[:, :], q='a_ld')
        self.barrier_lite('a_bar')

    def emit_moe(self, parity, first):
        s, c, nc = self, self.c, self.nc
        T, D, DC, NT = s.T, s.D, s.DC, s.NT
        V, S = nc.vector, nc.scalar

        o = 0
        def nx(n):
            nonlocal o
            r = o; o += n; return r
        xhat = s.rb(nx(DC * 256), DC * 512)
        w1b = s.rb(nx(1024), 2048)
        w2b = s.rb(nx(1024), 2048)
        mh = s.rb(nx(4096), 8192)
        s2B = s.rf(nx(512), 512)
        mskB = s.rb(nx(256), 512)
        x1t = s.rf(nx(512), 512)
        csw1 = s.rb(nx(1024), 2048, 1)
        s2seg = s.rf(nx(512), 512, 1)
        ns2seg = s.rb(nx(256), 512, 1)
        assert o <= s.R_W

        c.dma(nc.sync, csw1, s.csw1_dram[0:1, :], q='m_ld')

        psB = s.pb[0]
        psH = s.pb[1]
        psH2 = s.pb[2]
        psO = s.pb[3]

        for nt in range(NT):
            tsl = slice(nt * 512, (nt + 1) * 512)
            c.wait(nc.tensor, 'm_ld')
            c.dma(nc.sync, s2seg, s.rowd[R_S2, tsl][None, :], q='m_ld')
            c.dma(nc.sync, ns2seg, s.rowd_b[B_NEGMSS2, tsl][None, :], q='m_ld')
            c.wait(nc.tensor, 'm_vv')
            c.wait(nc.tensor, 'm_ld')
            c.mark(nc.tensor.matmul(psB[:, 0:512], s.onr(), s2seg,
                                    start=True, stop=True), 'm_mm')
            c.wait(V, 'm_mm')
            V.tensor_copy(s2B[:, :], psB[:, 0:512])
            c.mark(V.drain(), 'm_vv')
            c.wait(V, 'm_vv')
            for ch in range(DC):
                V.tensor_mul(xhat[:, ch * 512:(ch + 1) * 512],
                             s.xnc(ch)[:, tsl], s2B[:, :])
            c.mark(V.drain(), 'm_vv')
            def w1_load(e, hh):
                c.dma(nc.sync,
                      w1b[:, hh * 1024:(hh + 1) * 1024].rearrange(
                          "p (kf h) -> p kf h", h=256),
                      s.w1p[e, hh * 512:(hh + 1) * 512].rearrange(
                          "(kf p) h -> p kf h", p=128), q='m_ld')
            c.wait(nc.sync, 'm_mm')
            w1_load(0, 0)
            for e in range(8):
                c.wait(nc.tensor, 'm_vv')
                c.mark(nc.tensor.matmul(psB[:, 0:512],
                                        s.onehot_sb[:, e * 128:(e + 1) * 128],
                                        s.mask8[:, tsl],
                                        start=True, stop=True), 'm_mm')
                c.wait(V, 'm_mm')
                V.tensor_copy(mskB[:, :], psB[:, 0:512])
                c.mark(V.drain(), 'm_vv')
                for hh in range(2):
                    c.wait(nc.tensor, 'm_ld')
                    c.wait(nc.sync, 'm_mm')
                    if hh == 0:
                        w1_load(e, 1)
                    elif e < 7:
                        w1_load(e + 1, 0)
                    last = None
                    for kf in range(hh * 4, hh * 4 + 4):
                        for hc, psh in ((0, psH), (1, psH2)):
                            last = nc.tensor.matmul(
                                psh[:, 0:512],
                                w1b[:, (kf * 2 + hc) * 128:(kf * 2 + hc + 1) * 128],
                                xhat[:, kf * 512:(kf + 1) * 512],
                                start=(kf == 0), stop=False)
                    c.mark(last, 'm_mm')
                for hc, psh in ((0, psH), (1, psH2)):
                    mmi = nc.tensor.matmul(
                        psh[:, 0:512],
                        csw1[0:1, e * 256 + hc * 128:e * 256 + hc * 128 + 128],
                        ns2seg, start=False, stop=True)
                    c.mark(mmi, 'm_mm')
                    c.wait(S, 'm_mm')
                    S.activation(mh[:, (e * 2 + hc) * 512:(e * 2 + hc + 1) * 512],
                                 psh[:, 0:512], AF.Gelu,
                                 bias=s.beta_c[:, e * 2 + hc:e * 2 + hc + 1])
                    c.mark(S.drain(), 'm_ss')
                c.wait(V, 'm_ss')
                for hc in range(2):
                    sl = slice((e * 2 + hc) * 512, (e * 2 + hc + 1) * 512)
                    V.tensor_mul(mh[:, sl], mh[:, sl], mskB[:, :])
                c.mark(V.drain(), 'm_vv')
                c.wait(nc.tensor, 'm_vv')
            def w2_load(dc_, hh):
                c.dma(nc.sync,
                      w2b[:, hh * 1024:(hh + 1) * 1024].rearrange(
                          "p (e hc o) -> p e hc o", e=4, hc=2),
                      s.w2_bf[hh * 4:(hh + 1) * 4, :,
                              dc_ * 128:(dc_ + 1) * 128].rearrange(
                          "e (hc p) o -> p e hc o", p=128), q='m_ld')
            c.wait(nc.sync, 'm_mm')
            w2_load(0, 0)
            for dc in range(DC):
                rsrc = (s.xfeat[dc * 128:(dc + 1) * 128, tsl] if first else
                        s.x1sh[parity, dc * 128:(dc + 1) * 128, tsl])
                c.wait(nc.sync, 'm_ld')
                c.dma(nc.sync, x1t[:, :], rsrc, q='m_ld')
                c.wait(nc.tensor, 'm_vv')
                for hh in range(2):
                    c.wait(nc.tensor, 'm_ld')
                    c.wait(nc.sync, 'm_mm')
                    if hh == 0:
                        w2_load(dc, 1)
                    elif dc < DC - 1:
                        w2_load(dc + 1, 0)
                    last = None
                    for ei in range(hh * 8, hh * 8 + 8):
                        last = nc.tensor.matmul(psO[:, 0:512],
                                                w2b[:, ei * 128:(ei + 1) * 128],
                                                mh[:, ei * 512:(ei + 1) * 512],
                                                start=(ei == 0), stop=False)
                    c.mark(last, 'm_mm')
                mmi = nc.tensor.matmul(psO[:, 0:512],
                                       s.b2_b[:, dc * 128:(dc + 1) * 128],
                                       s.mask8[:, tsl], start=False, stop=True)
                c.mark(mmi, 'm_mm')
                c.wait(V, 'm_mm')
                c.wait(V, 'm_ld')
                V.tensor_add(x1t[:, :], psO[:, 0:512], x1t[:, :])
                c.mark(V.drain(), 'm_vv')
                c.wait(nc.sync, 'm_vv')
                c.dma(nc.sync, s.x1sh[parity, dc * 128:(dc + 1) * 128, tsl],
                      x1t[:, :], q='m_ld')
                c.wait(nc.tensor, 'm_vv')
        self.barrier_lite('m_bar')

"""Block 1 emitters: L1 stats, w_q prep, S4 kernel chain, FFT conv, epilogue.

Emitted as the tail of mk0.MKBuild. All big buffers are overlaid views into
s.arena CORE [0, 32768) plus a locals region [32768, +B1L_W); per-token rows
live in DRAM (rowd / statsh) and are staged as [1 x 512] strips.

Conventions (full-size values in brackets):
  QNZ = SEQ/128 [64]   slow time digit (nonzero range), also inverse q range
  KB  = NFFT/256 [64]  k2 range; k = k1 + 128*k2, k1 in [0,128), k2 in [0,KB)
  PACKE = 128//PSTR [2] k1 values packed per 128 partitions, PSTR=max(KB,32)
  CPASS [128]          conv channels per pass; NPASS = (D/2)/CPASS [4]
X layout: [128 x (NGRP x 2comp x CPASS)] bf16, k1 at partition offset
  (k1%PACKE)*PSTR, column group g=k1//PACKE, comp-major then channel.
Y layout: [128p x (ch x 256)] bf16 (per ch: k1re 0:128 | k1im 128:256).
G layout: [128ch x (k1 x 2comp x 128p)] bf16 -> GT[k1 x (p x ch)] via transpose.
"""
import numpy as np
import ml_dtypes
import concourse.bass as bass
import concourse.mybir as mybir
import concourse.bass_isa as bass_isa
from concourse import dve_ops


LN_EPS = 1e-5
R_S, R_NEGM, R_S2, R_Q, R_NMQ, R_XSC, R_DS, R_NDMS = range(8)


def mk1_constants(s, c):
    SEQ, NFFT = s.SEQ, 2 * s.SEQ
    QNZ = SEQ // 128
    QD = NFFT // 128
    KB = QD // 2
    QDs = SEQ // 128
    k1v = np.arange(128)
    pv = np.arange(128)
    qv = np.arange(QNZ)
    k2v = np.arange(KB)

    w1 = np.exp(-2j * np.pi * np.outer(qv, k1v % QD) / QD)
    cst = {}
    cst['f_s1'] = np.concatenate([w1.real, w1.imag], axis=1).astype(BF)

    m_all = (np.exp(-2j * np.pi * np.outer(pv, k1v) / NFFT)[:, :, None]
             * np.exp(-2j * np.pi * np.outer(pv, k2v) / QD)[:, None, :])
    _mre = np.transpose(m_all.real, (1, 0, 2))
    _mim = np.transpose(m_all.imag, (1, 0, 2))
    cst['f_mall'] = np.concatenate([_mre, _mim, -_mim], axis=2).astype(BF)

    ia = (np.exp(2j * np.pi * np.outer(k2v, pv) / QD)[None, :, :]
          * np.exp(2j * np.pi * k1v[:, None, None] * pv[None, None, :] / NFFT))
    ia = ia * (2.0 / NFFT)
    ia[0, 0, :] *= 0.5
    PSTR = max(KB, 64)
    PACKE = 128 // PSTR

    def dup(a):
        out = np.zeros((a.shape[0], 128, a.shape[2]), np.float64)
        for g in range(PACKE):
            out[:, g * PSTR:g * PSTR + KB, :] = a
        return out
    cst['i_all'] = np.concatenate(
        [dup(ia.real), dup(ia.imag), dup(-ia.imag)], axis=2).astype(BF)

    wb = np.exp(2j * np.pi * np.outer(k1v, np.arange(QNZ)) / QD)
    cst['b_re'] = wb.real.astype(BF)
    cst['b_im'] = (-wb.imag).astype(BF)
    cst['sign_col'] = (((-1.0) ** pv)[:, None]).astype(BF)
    cst['sign_row_n'] = ((((-1.0) ** pv) / NFFT)[None, :]).astype(BF)

    lv = np.arange(SEQ)
    ang = (-2 * np.pi * lv / SEQ).astype(np.float32)
    zl = (np.cos(ang) + 1j * np.sin(ang)).astype(np.complex64)
    cst['s4_timag'] = ((1 - zl) / (1 + zl)).imag.astype(np.float32)[None, :]
    cst['s4_i1z_re'] = (2.0 / (1 + zl)).real.astype(np.float32)[None, :]
    cst['s4_i1z_im'] = (2.0 / (1 + zl)).imag.astype(np.float32)[None, :]

    lqv = np.arange(QDs)
    t1v = np.arange(128)
    s4w1 = np.exp(2j * np.pi * np.outer(lqv, t1v % QDs) / QDs)
    cst['s4_s1a'] = np.concatenate([s4w1.real, s4w1.imag], axis=1).astype(BF)
    cst['s4_s1b'] = np.concatenate([-s4w1.imag, s4w1.real], axis=1).astype(BF)
    t2v = np.arange(QDs)
    m2 = (np.exp(2j * np.pi * np.outer(pv, t1v) / SEQ)[:, :, None]
          * np.exp(2j * np.pi * np.outer(pv, t2v) / QDs)[:, None, :]) / SEQ
    _gre = np.transpose(m2.real, (1, 0, 2))
    _gim = np.transpose(m2.imag, (1, 0, 2))
    cst['s4_gall'] = np.concatenate([_gre, _gim, -_gim], axis=2).astype(BF)

    aps = {}
    for k, v in cst.items():
        aps[k] = c.const(k, v)
    return aps


def mk1_statics(s):
    c = s.c
    QDs = s.SEQ // 128
    s.fft_s1_sb = c.sbuf("fft_s1_sb", [QDs, 256], BF16)
    s.s4_s1a_sb = c.sbuf("s4_s1a_sb", [QDs, 256], BF16)
    s.s4_s1b_sb = c.sbuf("s4_s1b_sb", [QDs, 256], BF16)
    s.sgncol_sb = c.sbuf("sgncol_sb", [128, 1], BF16)
    s.sgnrow_sb = c.sbuf("sgnrow_sb", [1, 128], BF16)
    s.fftb_re_sb = c.sbuf("fftb_re_sb", [128, QDs], BF16)
    s.fftb_im_sb = c.sbuf("fftb_im_sb", [128, QDs], BF16)
    s.kfny_rowb = c.sbuf("kfny_rowb", [1, s.D // 2], BF16)
    s._fwd_mbuf = [c.sbuf(f"fwd_mbuf{i}", [128, 3 * s.KB], BF16) for i in range(2)]
    s._cv_iab = [c.sbuf(f"cv_ia{i}", [128, 3 * 128], BF16) for i in range(2)]
    s.sny_sb = c.sbuf("sny_sb", [1, s.D // 2])
    s.snyb_sb = c.sbuf("snyb_sb", [1, s.D // 2], BF16)
    s.pnyb_sb = c.sbuf("pnyb_sb", [1, s.D // 2], BF16)
    s.gfnyc_sb = c.sbuf("gfnyc_sb", [128, 1], BF16)


def emit_block1(s):
    c, nc = s.c, s.nc
    s.cst = mk1_constants(s, c)
    mk1_statics(s)

    engines = s.engines
    idxs = [e.compute_val(e.partition_id() % 2) for e in engines]

    def switched(fn_by_parity, tags):
        base = dict(c.ctr)
        base_lw = dict(c.last_wait)
        deltas = {}
        for arm in nc.Switch(engines=engines, index=idxs, n=2):
            for k in list(c.ctr):
                c.ctr[k] = base.get(k, 0)
            c.last_wait.clear(); c.last_wait.update(base_lw)
            fn_by_parity[arm]()
            deltas[arm] = {k: v - base.get(k, 0) for k, v in c.ctr.items()
                           if v != base.get(k, 0)}
            for k in list(c.ctr):
                c.ctr[k] = base.get(k, 0)
            c.last_wait.clear(); c.last_wait.update(base_lw)
        assert deltas.get(0, {}) == deltas.get(1, {}), (tags, deltas)
        for k in set(deltas.get(0, {})) | set(base):
            c.ctr[k] = base.get(k, 0) + deltas.get(0, {}).get(k, 0)
            c.last_wait[k] = c.ctr[k]

    with nc.named_scope("q_stats"):
        switched({0: lambda: b1_stats(s, 0), 1: lambda: b1_stats(s, 1)}, 'stats')
        nc.all_core_barrier()
    with nc.named_scope("c_conv"):
        switched({0: lambda: b1_conv(s, 0), 1: lambda: b1_conv(s, 1)}, 'conv')
        nc.all_core_barrier()
    with nc.named_scope("f_final"):
        switched({0: lambda: b1_final(s, 0), 1: lambda: b1_final(s, 1)}, 'final')


def b1_stats(s, par):
    c, nc = s.c, s.nc
    T, D, DC, NT, NJ, SEQ = s.T, s.D, s.DC, s.NT, s.NJ, s.SEQ
    V, S, G = nc.vector, nc.scalar, nc.gpsimd
    A = s.arena
    Q = 'q_'

    xt = [A.f32(0, 512), A.f32(512, 512)]
    sq = A.f32(1024, 512)
    rmx = A.f32(1536, 512)
    rmn = A.f32(2048, 512)
    neg = A.f32(2560, 512)
    strip = [A.f32(3072, 512, 1), A.f32(3584, 512, 1)]
    coltile = A.f32(4096, 16 * NJ)
    W0 = 4096 + 16 * NJ
    wt = A.f32(W0, 1024)
    wq16 = A.bf16(W0 + 1024, 1024)
    wqt_t = A.bf16(W0 + 1536, 128)
    sqw = A.f32(W0 + 1600, 1024)
    rndf = A.f32(W0 + 2624, 1024)
    dskq = A.f32(W0 + 3648, 32, 1)

    def cq(i):
        return coltile[:, i * NJ:(i + 1) * NJ]
    SX, QX, MX, MN, M1, S1, T0, T1, QQ, NMQ, XSC, DS, NDMS, NMS1, DSKC = range(15)

    psA = s.pb[0]
    psB = s.pb[1]
    for nt in range(NT):
        tsl = slice(nt * 512, (nt + 1) * 512)
        for ch in range(DC):
            b = xt[ch % 2]
            c.wait(nc.sync, Q + 'ss')
            c.wait(nc.sync, Q + 'vv')
            c.dma(nc.sync, b[:, :], s.x1sh[par, ch * 128:(ch + 1) * 128, tsl],
                  q=Q + 'ld')
            c.wait(S, Q + 'ld')
            c.wait(S, Q + 'mm')
            S.activation(sq[:, :], b[:, :], AF.Square)
            c.mark(S.drain(), Q + 'ss')
            c.wait(nc.tensor, Q + 'ss')
            nc.tensor.matmul(psA[0:1, 0:512], s.onc(), b[:, :],
                             start=(ch == 0), stop=(ch == DC - 1))
            m = nc.tensor.matmul(psB[0:1, 0:512], s.onc(), sq[:, :],
                                 start=(ch == 0), stop=(ch == DC - 1))
            c.mark(m, Q + 'mm')
            c.wait(V, Q + 'ld')
            if ch == 0:
                V.tensor_copy(rmx[:, :], b[:, :])
                V.drain()
                V.tensor_scalar(rmn[:, :], b[:, :], -1.0, None, op0=AL.mult)
            else:
                V.tensor_tensor(rmx[:, :], rmx[:, :], b[:, :], op=AL.max)
                V.drain()
                V.tensor_scalar(neg[:, :], b[:, :], -1.0, None, op0=AL.mult)
                V.drain()
                V.tensor_tensor(rmn[:, :], rmn[:, :], neg[:, :], op=AL.max)
            c.mark(V.drain(), Q + 'vv')
        c.wait(V, Q + 'mm')
        c.wait(V, Q + 'ld')
        V.tensor_copy(strip[0], psA[0:1, 0:512])
        V.drain()
        V.tensor_copy(strip[1], psB[0:1, 0:512])
        c.mark(V.drain(), Q + 'vv')
        c.wait(nc.tensor, Q + 'vv')
        JB = 512 // 128

        def row_to_cols(st, dst, slot):
            c.wait(nc.sync, Q + 'vv')
            c.dma(nc.sync, s.rowtmp[slot][None, :], st, q=Q + 'ld')
            c.chain(Q + 'ld')
            c.dma(nc.sync, dst,
                  s.rowtmp[slot].rearrange("(j p) -> p j", p=128), q=Q + 'ld',
                  slow=True)
        row_to_cols(strip[0], cq(SX)[:, nt * JB:(nt + 1) * JB], 0)
        row_to_cols(strip[1], cq(QX)[:, nt * JB:(nt + 1) * JB], 1)
        psD = s.pb[2]
        for k, t in ((MX, rmx), (MN, rmn)):
            for jj in range(JB):
                c.wait(nc.tensor, Q + 'vv')
                c.mark(nc.tensor.transpose(psD[0:128, 0:128],
                                           t[:, jj * 128:(jj + 1) * 128],
                                           s.id_f[:, :]), Q + 'mm')
                c.wait(V, Q + 'mm')
                V.tensor_reduce(cq(k)[:, nt * JB + jj:nt * JB + jj + 1],
                                psD[0:128, 0:128], axis=AX.X, op=AL.max)
                c.mark(V.drain(), Q + 'vv')
        c.wait(nc.tensor, Q + 'vv')

    c.dma(nc.sync, dskq[0:1, 0:1], s.Dskip[None, :], q=Q + 'ld')
    c.wait(nc.tensor, Q + 'ld')
    psD = s.pb[2]
    c.mark(nc.tensor.matmul(psD[0:128, 0:1], s.onr(), dskq[0:1, 0:1],
                            start=True, stop=True), Q + 'mm')
    c.wait(V, Q + 'mm')
    V.tensor_copy(cq(DSKC)[:, 0:1], psD[0:128, 0:1])
    c.mark(V.drain(), Q + 'vv')
    V.tensor_scalar(cq(M1), cq(SX), 1.0 / D, None, op0=AL.mult)
    V.drain()
    V.tensor_scalar(cq(T0), cq(QX), 1.0 / D, None, op0=AL.mult)
    V.drain()
    V.tensor_mul(cq(T1), cq(M1), cq(M1))
    V.drain()
    V.tensor_sub(cq(T0), cq(T0), cq(T1))
    c.mark(V.drain(), Q + 'vv')
    c.wait(S, Q + 'vv')
    S.activation(cq(T0), cq(T0), AF.Sqrt, bias=float(LN_EPS))
    c.mark(S.drain(), Q + 'ss')
    c.wait(V, Q + 'ss')
    s.recip_nr(V, cq(S1), cq(T0), cq(T1))
    V.tensor_sub(cq(T0), cq(MX), cq(M1))
    V.drain()
    V.tensor_add(cq(T1), cq(MN), cq(M1))
    V.drain()
    V.tensor_tensor(cq(T0), cq(T0), cq(T1), op=AL.max)
    V.drain()
    V.tensor_mul(cq(T0), cq(T0), cq(S1))
    V.drain()
    V.tensor_scalar(cq(T0), cq(T0), 1e-8, None, op0=AL.max)
    V.drain()
    V.tensor_scalar(cq(XSC), cq(T0), 1.0 / 127.0, None, op0=AL.mult)
    V.drain()
    s.recip_nr(V, cq(T1), cq(XSC), cq(T0))
    V.tensor_mul(cq(QQ), cq(S1), cq(T1))
    V.drain()
    V.tensor_mul(cq(NMQ), cq(M1), cq(QQ))
    V.drain()
    V.tensor_scalar(cq(NMQ), cq(NMQ), -1.0, None, op0=AL.mult)
    V.drain()
    V.tensor_mul(cq(NMS1), cq(M1), cq(S1))
    V.drain()
    V.tensor_scalar(cq(NMS1), cq(NMS1), -1.0, None, op0=AL.mult)
    V.drain()
    V.tensor_scalar(cq(DS), cq(S1), cq(DSKC)[:, 0:1], None, op0=AL.mult)
    V.drain()
    V.tensor_mul(cq(NDMS), cq(DS), cq(M1))
    V.drain()
    V.tensor_scalar(cq(NDMS), cq(NDMS), -1.0, None, op0=AL.mult)
    c.mark(V.drain(), Q + 'vv')

    rr = lambda i: s.rowd[i].rearrange("(j p) -> j p", p=128)
    sh = lambda i: s.statsh[par, i].rearrange("(j p) -> j p", p=128)
    s.cols_to_row(cq(S1), sh(0), NJ, pfx=Q)
    s.cols_to_row(cq(NMS1), sh(1), NJ, pfx=Q)
    s.cols_to_row(cq(QQ), rr(R_Q), NJ, pfx=Q)
    s.cols_to_row(cq(NMQ), rr(R_NMQ), NJ, pfx=Q)
    s.cols_to_row(cq(XSC), rr(R_XSC), NJ, pfx=Q)
    s.cols_to_row(cq(DS), rr(R_DS), NJ, pfx=Q)
    s.cols_to_row(cq(NDMS), rr(R_NDMS), NJ, pfx=Q)

    wqs = s.col("b1_wsc", 8)
    wcol = s.col("b1_wcol", 1)
    wcol2 = s.col("b1_wcol2", 1)
    psT = s.pb[3]
    for oc in range(DC):
        c.wait(nc.sync, Q + 'mm')
        c.wait(nc.sync, Q + 'vv')
        c.dma(nc.sync, wt[:, :], s.l1w[oc * 128:(oc + 1) * 128, :], q=Q + 'ld')
        c.wait(V, Q + 'ld')
        V.tensor_reduce(wcol, wt[:, :], axis=AX.X, op=AL.max,
                        apply_absolute_value=True)
        V.drain()
        V.tensor_scalar(wcol, wcol, 1e-8, None, op0=AL.max)
        V.drain()
        V.tensor_scalar(wqs[:, oc:oc + 1], wcol, 1.0 / 127.0, None, op0=AL.mult)
        V.drain()
        s.recip_nr(V, wcol2, wqs[:, oc:oc + 1], wcol)
        V.tensor_scalar(wt[:, :], wt[:, :], wcol2, None, op0=AL.mult)
        V.drain()
        iv = sqw.bitcast(mybir.dt.int32)
        V.tensor_copy(iv, wt[:, :])
        V.drain()
        V.tensor_copy(rndf[:, :], iv)
        V.drain()
        V.tensor_sub(wt[:, :], wt[:, :], rndf[:, :])
        V.drain()
        V.tensor_single_scalar(sqw[:, :], wt[:, :], 0.5, op=AL.is_ge)
        V.drain()
        V.tensor_add(rndf[:, :], rndf[:, :], sqw[:, :])
        V.drain()
        V.tensor_single_scalar(sqw[:, :], wt[:, :], -0.5, op=AL.is_lt)
        V.drain()
        V.tensor_sub(wq16[:, :], rndf[:, :], sqw[:, :])
        c.mark(V.drain(), Q + 'vv')
        for kf in range(DC):
            c.wait(nc.tensor, Q + 'vv')
            c.mark(nc.tensor.transpose(psT[0:128, 0:128].bitcast(BF16)[:, 0:128],
                                       wq16[:, kf * 128:(kf + 1) * 128],
                                       s.id_b[:, :]), Q + 'mm')
            c.wait(V, Q + 'mm')
            c.wait(V, Q + 'ld')
            V.tensor_copy(wqt_t[:, :], psT[0:128, 0:128].bitcast(BF16)[:, 0:128])
            c.mark(V.drain(), Q + 'vv')
            c.wait(nc.sync, Q + 'vv')
            c.dma(nc.sync, s.wqt_dram[kf * 128:(kf + 1) * 128, oc * 128:(oc + 1) * 128],
                  wqt_t[:, :], q=Q + 'ld')
    s.wqs_col = wqs

    b1_s4_chain(s, par)
    s.barrier_lite(Q + 'bar')


def emit_fft_fwd(s, Q, rT, CH, Y, X, nyrow, kst=None):
    c, nc = s.c, s.nc
    V, S = nc.vector, nc.scalar
    QNZ = s.SEQ // 128
    KB = s.KB
    PSTR, PACKE = s.PSTR, s.PACKE
    s1c = s.fft_s1_sb
    psY = [s.pb[0], s.pb[3]]
    psX = [s.pb[1], s.pb[2]]
    mall = kst if kst is not None else s.cst['f_mall']
    mbuf = s._fwd_mbuf

    for chp in range(0, CH, 2):
        ps = psY[(chp // 2) % 2]
        c.wait_lag(nc.tensor, Q + 'ss', 1)
        for ci in range(2):
            m = nc.tensor.matmul(ps[0:128, ci * 256:(ci + 1) * 256],
                                 rT[0:QNZ, (chp + ci)::CH],
                                 s1c[0:QNZ, :], start=True, stop=True)
        c.mark(m, Q + 'mm')
        c.wait(S, Q + 'mm')
        S.copy(Y[:, chp * 256:(chp + 2) * 256], ps[0:128, 0:512])
        c.mark(S.drain(), Q + 'ss')
    c.wait(nc.tensor, Q + 'ss')
    m = nc.tensor.matmul(psX[0][0:1, 0:CH], s.sgncol_sb[:, :], Y[:, 0::256],
                         start=True, stop=True)
    c.mark(m, Q + 'mm')
    c.wait(V, Q + 'mm')
    V.tensor_copy(nyrow[0:1, 0:CH], psX[0][0:1, 0:CH])
    c.mark(V.drain(), Q + 'vv')
    c.wait(nc.tensor, Q + 'vv')

    for k1 in range(128):
        mb = mbuf[k1 % 2]
        c.wait_lag(nc.sync, Q + 'mm', 1)
        c.dma(nc.sync, mb[:, :], mall[k1], q=Q + 'ld')
        c.wait(nc.tensor, Q + 'ld')
        c.wait_lag(nc.tensor, Q + 'vv', 1)
        po = (k1 % PACKE) * PSTR
        ps = psX[k1 % 2]
        tp = (0, po)
        Yre = Y[:, k1::256]
        Yim = Y[:, 128 + k1::256]
        nc.tensor.matmul(ps[po:po + KB, 0:CH], mb[:, 0:KB], Yre,
                         start=True, stop=False, tile_position=tp)
        nc.tensor.matmul(ps[po:po + KB, 0:CH], mb[:, 2 * KB:3 * KB], Yim,
                         start=False, stop=True, tile_position=tp)
        nc.tensor.matmul(ps[po:po + KB, CH:2 * CH], mb[:, KB:2 * KB], Yre,
                         start=True, stop=False, tile_position=tp)
        m = nc.tensor.matmul(ps[po:po + KB, CH:2 * CH], mb[:, 0:KB], Yim,
                             start=False, stop=True, tile_position=tp)
        c.mark(m, Q + 'mm')
        g = k1 // PACKE
        c.wait(V, Q + 'mm')
        V.tensor_copy(X[po:po + KB, g * 2 * CH:(g + 1) * 2 * CH],
                      ps[po:po + KB, 0:2 * CH])
        c.mark(V.drain(), Q + 'vv')
    c.wait(nc.tensor, Q + 'vv')


def b1_conv(s, par):
    c, nc = s.c, s.nc
    V, S = nc.vector, nc.scalar
    A = s.arena
    SEQ, T, DC, D = s.SEQ, s.T, s.DC, s.D
    QNZ = SEQ // 128
    KB, PSTR, PACKE = s.KB, s.PSTR, s.PACKE
    NGRP = s.NGRP
    CP = s.CPASS
    NPASS = s.NPASS
    Q = 'c_'

    stage = A.f32(0, SEQ)
    xn1 = A.bf16(8192, SEQ)
    rT = A.bf16(0, 128 * CP)
    Ybuf = A.bf16(8192, CP * 256)
    Xbuf = A.bf16(0, NGRP * 2 * CP)
    kfull = A.bf16(8192, NGRP * 2 * CP)
    scr1 = A.bf16(16384, NGRP * CP)
    scr2 = A.bf16(20480, NGRP * CP)
    Gbuf = A.bf16(8192, 128 * 2 * 128)
    GTre = A.bf16(0, CP * 128)
    GTim = A.bf16(24576, CP * 128)
    hown = A.bf16(8192, T)
    hpeer = A.bf16(8192 + T // 2, T)
    lnA = A.f32(12288, 512, 1)
    lnB = A.f32(12800, 512, 1)
    iab = s._cv_iab
    sny = s.sny_sb
    snyb = s.snyb_sb
    pnyb = s.pnyb_sb

    psG = [s.pb[4], s.pb[5]]
    psH = s.pb[6]
    psB2 = s.pb[7]
    psT = s.pb[3]

    for pp in range(NPASS):
        ch0 = par * (D // 2) + pp * CP
        s.barrier_lite(Q + 'bar')
        c.dma(nc.sync, stage[:, 0:T], s.x1sh[0, ch0:ch0 + 128, :], q=Q + 'ld')
        c.dma(nc.sync, stage[:, T:SEQ], s.x1sh[1, ch0:ch0 + 128, :], q=Q + 'ld')
        NTT = T // 512
        for jb in range(SEQ // 512):
            half, jo = jb // NTT, jb % NTT
            c.wait(nc.sync, Q + 'mm')
            c.dma(nc.sync, lnA, s.statsh[half, 0, jo * 512:(jo + 1) * 512][None, :],
                  q=Q + 'ld')
            c.dma(nc.sync, lnB, s.statsh[half, 1, jo * 512:(jo + 1) * 512][None, :],
                  q=Q + 'ld')
            sl = slice(jb * 512, jb * 512 + 512)
            c.wait(nc.tensor, Q + 'vv')
            c.wait(nc.tensor, Q + 'ld')
            m = nc.tensor.matmul(psB2[0:128, 0:512], s.onr(), lnA,
                                 start=True, stop=True)
            c.mark(m, Q + 'mm')
            c.wait(V, Q + 'mm')
            c.wait(V, Q + 'ld')
            V.tensor_mul(xn1[:, sl], stage[:, sl], psB2[0:128, 0:512])
            c.mark(V.drain(), Q + 'vv')
            c.wait(nc.tensor, Q + 'vv')
            m = nc.tensor.matmul(psB2[0:128, 0:512], s.onr(), lnB,
                                 start=True, stop=True)
            c.mark(m, Q + 'mm')
            c.wait(V, Q + 'mm')
            V.tensor_add(xn1[:, sl], xn1[:, sl], psB2[0:128, 0:512])
            c.mark(V.drain(), Q + 'vv')
            c.wait(nc.tensor, Q + 'vv')
        psTr = [s.pb[3], s.pb[0]]
        for p8 in range(16):
            pt = psTr[p8 % 2]
            ev = V if p8 % 2 == 0 else S
            evq = Q + ('vv' if p8 % 2 == 0 else 'ss')
            c.wait(nc.tensor, evq)
            for pi in range(8):
                p = p8 * 8 + pi
                c.mark(nc.tensor.transpose(
                    pt[0:QNZ, 0:512].bitcast(BF16)[0:QNZ, pi * 128:(pi + 1) * 128],
                    xn1[:, p::128], s.id_b[:, :]), Q + 'mm')
            c.wait(ev, Q + 'mm')
            if ev is V:
                V.tensor_copy(rT[0:QNZ, p8 * 1024:(p8 + 1) * 1024],
                              pt[0:QNZ, 0:512].bitcast(BF16)[0:QNZ, 0:1024])
            else:
                S.copy(rT[0:QNZ, p8 * 1024:(p8 + 1) * 1024],
                       pt[0:QNZ, 0:512].bitcast(BF16)[0:QNZ, 0:1024])
            c.mark(ev.drain(), evq)
        c.wait(nc.tensor, Q + 'vv')
        c.wait(nc.tensor, Q + 'ss')
        emit_fft_fwd(s, Q, rT, CP, Ybuf, Xbuf, sny[0:1, pp * CP:(pp + 1) * CP])
        c.wait(V, Q + 'vv')
        V.tensor_copy(snyb[0:1, pp * CP:(pp + 1) * CP],
                      sny[0:1, pp * CP:(pp + 1) * CP])
        c.mark(V.drain(), Q + 'vv')

        bands = ([(0, 128)] if KB == PSTR else
                 [(po, po + KB) for po in (0, PSTR)])
        c.wait(nc.sync, Q + 'vv')
        for a, b_ in bands:
            c.dma(nc.sync,
                  kfull[a:b_, :].rearrange("p (g c h) -> p g c h", g=NGRP, c=2),
                  s.kf_dram[a:b_, :, :, pp * CP:(pp + 1) * CP], q=Q + 'ld')
        c.wait(V, Q + 'ld')
        def gch(ap):
            return ap.rearrange("p (g c h) -> p g c h", g=NGRP, c=2)
        for a, b_ in bands:
            Xr = gch(Xbuf[:, :])[a:b_, :, 0, :]
            Xi = gch(Xbuf[:, :])[a:b_, :, 1, :]
            Kr = gch(kfull[:, :])[a:b_, :, 0, :]
            Ki = gch(kfull[:, :])[a:b_, :, 1, :]
            t1 = scr1[:, :].rearrange("p (g h) -> p g h", g=NGRP)[a:b_]
            t2 = scr2[:, :].rearrange("p (g h) -> p g h", g=NGRP)[a:b_]
            V.tensor_mul(t1, Xr, Ki)
            V.drain()
            V.tensor_mul(t2, Xi, Kr)
            V.drain()
            V.tensor_mul(Xr, Xr, Kr)
            V.drain()
            V.tensor_mul(Kr, Xi, Ki)
            V.drain()
            V.tensor_sub(Xr, Xr, Kr)
            V.drain()
            V.tensor_add(Xi, t1, t2)
            V.drain()
        V.tensor_mul(pnyb[0:1, pp * CP:(pp + 1) * CP],
                     snyb[0:1, pp * CP:(pp + 1) * CP],
                     s.kfny_rowb[0:1, pp * CP:(pp + 1) * CP])
        c.mark(V.drain(), Q + 'vv')

        c.wait(nc.tensor, Q + 'vv')
        c.wait(nc.tensor, Q + 'ss')
        PA = [(s.pb[4], s.pb[5]), (s.pb[0], s.pb[1])]
        for k1 in range(128):
            ia = iab[k1 % 2]
            c.wait_lag(nc.sync, Q + 'mm', 1)
            c.dma(nc.sync, ia[:, :], s.cst['i_all'][k1], q=Q + 'ld')
            c.wait(nc.tensor, Q + 'ld')
            c.wait_lag(nc.tensor, Q + 'vv', 1)
            c.wait_lag(nc.tensor, Q + 'ss', 1)
            pg0, pg1 = PA[k1 % 2]
            po = (k1 % PACKE) * PSTR
            g = k1 // PACKE
            tpr = (po, 0)
            Pre = Xbuf[po:po + KB, (g * 2 + 0) * CP:(g * 2 + 1) * CP]
            Pim = Xbuf[po:po + KB, (g * 2 + 1) * CP:(g * 2 + 2) * CP]
            nc.tensor.matmul(pg0[0:CP, 0:128], Pre, ia[po:po + KB, 0:128],
                             start=True, stop=False, tile_position=tpr)
            nc.tensor.matmul(pg1[0:CP, 0:128], Pre, ia[po:po + KB, 128:256],
                             start=True, stop=False, tile_position=tpr)
            stop0 = (k1 != 0)
            nc.tensor.matmul(pg0[0:CP, 0:128], Pim, ia[po:po + KB, 256:384],
                             start=False, stop=stop0, tile_position=tpr)
            m = nc.tensor.matmul(pg1[0:CP, 0:128], Pim, ia[po:po + KB, 0:128],
                                 start=False, stop=True, tile_position=tpr)
            if k1 == 0:
                m2 = nc.tensor.matmul(pg0[0:CP, 0:128],
                                      pnyb[0:1, pp * CP:(pp + 1) * CP],
                                      s.sgnrow_sb[:, :], start=False, stop=True)
            c.mark(m if k1 != 0 else m2, Q + 'mm')
            c.wait(V, Q + 'mm')
            V.tensor_copy(Gbuf[0:CP, (k1 * 2 + 0) * 128:(k1 * 2 + 1) * 128],
                          pg0[0:CP, 0:128])
            c.mark(V.drain(), Q + 'vv')
            c.wait(S, Q + 'mm')
            S.copy(Gbuf[0:CP, (k1 * 2 + 1) * 128:(k1 * 2 + 2) * 128],
                   pg1[0:CP, 0:128])
            c.mark(S.drain(), Q + 'ss')
        c.wait(nc.tensor, Q + 'vv')
        c.wait(nc.tensor, Q + 'ss')

        for comp in range(2):
            GT = GTre if comp == 0 else GTim
            for p4 in range(32):
                for pi in range(4):
                    p = p4 * 4 + pi
                    c.mark(nc.tensor.transpose(
                        psT[0:128, 0:512].bitcast(BF16)[0:128, pi * 128:(pi + 1) * 128],
                        Gbuf[0:CP, (comp * 128 + p)::256], s.id_b[0:CP, 0:CP]),
                        Q + 'mm')
                c.wait(V, Q + 'mm')
                V.tensor_copy(GT[:, p4 * 4 * CP:(p4 * 4 + 4) * CP],
                              psT[0:128, 0:512].bitcast(BF16)[0:128, 0:512])
                c.mark(V.drain(), Q + 'vv')
                c.wait(nc.tensor, Q + 'vv')

        qo = par * (QNZ // 2)
        qp = (1 - par) * (QNZ // 2)
        psHr = [s.pb[6], s.pb[7]]
        c.wait(nc.tensor, Q + 'vv')
        c.wait(nc.tensor, Q + 'ss')
        for p in range(128):
            ph = psHr[p % 2]
            c.wait_lag(nc.tensor, Q + 'vv', 1)
            c.wait_lag(nc.tensor, Q + 'ss', 1)
            nc.tensor.matmul(ph[0:128, 0:QNZ], GTre[:, p * CP:(p + 1) * CP],
                             s.fftb_re_sb[:, 0:QNZ], start=True, stop=False)
            m = nc.tensor.matmul(ph[0:128, 0:QNZ], GTim[:, p * CP:(p + 1) * CP],
                                 s.fftb_im_sb[:, 0:QNZ], start=False, stop=True)
            c.mark(m, Q + 'mm')
            c.wait(V, Q + 'mm')
            V.tensor_copy(hown[:, p::128], ph[0:128, qo:qo + QNZ // 2])
            c.mark(V.drain(), Q + 'vv')
            c.wait(S, Q + 'mm')
            S.copy(hpeer[:, p::128], ph[0:128, qp:qp + QNZ // 2])
            c.mark(S.drain(), Q + 'ss')
        c.wait(nc.sync, Q + 'vv')
        c.wait(nc.sync, Q + 'ss')
        c.dma(nc.sync, s.hfeat_dram[pp], hown[:, :], q=Q + 'ld')
        c.dma(nc.sync, s.hsh[par, pp], hpeer[:, :], q=Q + 'ld')
        c.wait(nc.sync, Q + 'ld')
    s.barrier_lite(Q + 'bar')


def b1_final(s, par):
    c, nc = s.c, s.nc
    V, S = nc.vector, nc.scalar
    T, D, DC, NT = s.T, s.D, s.DC, s.NT
    A = s.arena
    Q = 'f_'

    xq = A.bf16(0, DC * 512)
    htile = A.bf16(2048, 512)
    y1 = A.f32(2304, 512)
    gt = A.f32(2816, 512)
    outt = A.f32(3328, 512)
    vt = A.f32(3840, 512)
    wqall = A.bf16(9984, DC * D)
    x1t = [A.f32(4864, 512), A.f32(5376, 512)]
    bD = A.f32(5888, 512)
    bq = A.f32(6400, 512)
    mdt = A.f32(6912, 512)
    stq = A.f32(7424, 512, 1)
    stnmq = A.f32(7936, 512, 1)
    stDs = A.f32(8448, 512, 1)
    stnDms = A.f32(8960, 512, 1)
    stxsc = A.f32(9472, 512, 1)

    psFr = [s.pb[0], s.pb[3]]
    psB = s.pb[1]
    psO = s.pb[2]
    l1b_c = s.col("fn_l1b", 8)
    c.dma(nc.sync, l1b_c, s.l1b.rearrange("(c p) -> p c", p=128), q=Q + 'ld',
          slow=True)
    c.dma(nc.sync, wqall[:, :].rearrange("p (k m o) -> p k m o", m=DC, o=128),
          s.wqt_dram.rearrange("(k p) (m o) -> p k m o", p=128, o=128),
          q=Q + 'ld')

    def bcast(dst, strip):
        c.wait(nc.tensor, Q + 'vv')
        m = nc.tensor.matmul(psB[0:128, 0:512], s.onr(), strip,
                             start=True, stop=True)
        c.mark(m, Q + 'mm')
        c.wait(V, Q + 'mm')
        V.tensor_copy(dst[:, :], psB[0:128, 0:512])
        c.mark(V.drain(), Q + 'vv')

    for nt in range(NT):
        tsl = slice(nt * 512, (nt + 1) * 512)
        c.wait(nc.sync, Q + 'mm')
        c.dma(nc.sync, stq, s.rowd[R_Q, tsl][None, :], q=Q + 'ld')
        c.dma(nc.sync, stnmq, s.rowd[R_NMQ, tsl][None, :], q=Q + 'ld')
        c.dma(nc.sync, stDs, s.rowd[R_DS, tsl][None, :], q=Q + 'ld')
        c.dma(nc.sync, stnDms, s.rowd[R_NDMS, tsl][None, :], q=Q + 'ld')
        c.dma(nc.sync, stxsc, s.rowd[R_XSC, tsl][None, :], q=Q + 'ld')
        c.wait(nc.tensor, Q + 'ld')
        bcast(bq, stq)
        bcast(mdt, stnmq)
        for kf in range(DC):
            b = x1t[kf % 2]
            c.wait(nc.sync, Q + 'vv')
            c.wait(nc.sync, Q + 'mm')
            c.dma(nc.sync, b[:, :], s.x1sh[par, kf * 128:(kf + 1) * 128, tsl],
                  q=Q + 'ld')
            c.wait(V, Q + 'ld')
            c.wait(V, Q + 'st')
            V.tensor_mul(vt[:, :], b[:, :], bq[:, :])
            V.drain()
            V.tensor_add(vt[:, :], vt[:, :], mdt[:, :])
            V.drain()
            ivf = y1.bitcast(mybir.dt.int32)
            V.tensor_copy(ivf, vt[:, :])
            V.drain()
            V.tensor_copy(gt[:, :], ivf)
            V.drain()
            V.tensor_sub(vt[:, :], vt[:, :], gt[:, :])
            V.drain()
            V.tensor_single_scalar(y1[:, :], vt[:, :], 0.5, op=AL.is_ge)
            V.drain()
            V.tensor_add(gt[:, :], gt[:, :], y1[:, :])
            V.drain()
            V.tensor_single_scalar(y1[:, :], vt[:, :], -0.5, op=AL.is_lt)
            V.drain()
            V.tensor_sub(xq[:, kf * 512:(kf + 1) * 512], gt[:, :], y1[:, :])
            c.mark(V.drain(), Q + 'vv')
        bcast(bD, stDs)
        bcast(mdt, stnDms)
        bcast(bq, stxsc)

        for dc in range(DC):
            my_half = (dc // (DC // 2)) == par
            ppx = dc % (DC // 2)
            hsrc = s.hfeat_dram[ppx] if my_half else s.hsh[1 - par, ppx]
            c.wait(nc.sync, Q + 'vv')
            c.dma(nc.sync, htile[:, :], hsrc[:, tsl], q=Q + 'ld')
            b = x1t[dc % 2]
            c.wait(nc.sync, Q + 'mm')
            c.dma(nc.sync, b[:, :], s.x1sh[par, dc * 128:(dc + 1) * 128, tsl],
                  q=Q + 'ld')
            c.wait(V, Q + 'ld')
            V.tensor_mul(y1[:, :], b[:, :], bD[:, :])
            V.drain()
            V.tensor_add(y1[:, :], y1[:, :], mdt[:, :])
            V.drain()
            V.tensor_add(y1[:, :], y1[:, :], htile[:, :])
            c.mark(V.drain(), Q + 'vv')
            c.wait(S, Q + 'vv')
            S.activation(gt[:, :], y1[:, :], AF.Gelu)
            c.mark(S.drain(), Q + 'ss')
            psF = psFr[dc % 2]
            c.wait(nc.tensor, Q + 'vv')
            c.wait_lag(nc.tensor, Q + 'pf', 1)
            c.wait(nc.tensor, Q + 'ld')
            for kf in range(DC):
                mm = nc.tensor.matmul(psF[0:128, 0:512],
                                      wqall[:, (kf * DC + dc) * 128:(kf * DC + dc + 1) * 128],
                                      xq[:, kf * 512:(kf + 1) * 512],
                                      start=(kf == 0), stop=(kf == DC - 1))
            c.mark(mm, Q + 'mm')
            c.wait(V, Q + 'mm')
            c.wait(V, Q + 'ss')
            V.tensor_scalar(outt[:, :], psF[0:128, 0:512],
                            s.wqs_col[:, dc:dc + 1], None, op0=AL.mult)
            c.mark(V.drain(), Q + 'pf')
            V.tensor_mul(outt[:, :], outt[:, :], bq[:, :])
            V.drain()
            V.scalar_tensor_tensor(outt[:, :], outt[:, :],
                                   l1b_c[:, dc:dc + 1], gt[:, :],
                                   op0=AL.add, op1=AL.add)
            V.drain()
            V.tensor_add(outt[:, :], outt[:, :], b[:, :])
            c.mark(V.drain(), Q + 'vv')
            c.wait(nc.tensor, Q + 'vv')
            for j in range(4):
                c.mark(nc.tensor.transpose(psO[0:128, j * 128:(j + 1) * 128],
                                           outt[:, j * 128:(j + 1) * 128],
                                           s.id_f[:, :]), Q + 'mm')
            c.wait(S, Q + 'mm')
            c.wait(S, Q + 'st')
            S.copy(vt[:, :], psO[0:128, 0:512])
            c.mark(S.drain(), Q + 'ss')
            c.wait(nc.sync, Q + 'ss')
            c.dma(nc.sync,
                  s.y_out[nt * 512:(nt + 1) * 512, dc * 128:(dc + 1) * 128]
                      .rearrange("(j p) d -> p j d", p=128),
                  vt[:, :].rearrange("p (j d) -> p j d", j=4), q=Q + 'st')
    c.wait(nc.sync, Q + 'st')
    s.barrier_lite(Q + 'bar')


def b1_s4_chain(s, par):
    c, nc = s.c, s.nc
    V, S, G = nc.vector, nc.scalar, nc.gpsimd
    A = s.arena
    SEQ, D = s.SEQ, s.D
    QDs = SEQ // 128
    KB, PSTR, PACKE = s.KB, s.PSTR, s.PACKE
    NGRP = s.NGRP
    NCH = SEQ // 512
    nw = SEQ // 64
    APC = 512 // nw
    Q = 'q_'
    LB = 32768

    cre = A.f32(0, SEQ, 64)
    cim = A.f32(SEQ, SEQ, 64)
    gmre = A.bf16(2 * SEQ, SEQ, 64)
    gmim = A.bf16(2 * SEQ + SEQ // 2, SEQ, 64)
    if SEQ >= 4096:
        O_GTm, O_Yg, O_gbuf, O_Ymode, O_Xmode, O_gfT = 0, 8192, 16384, 0, 16384, 24576
    else:
        O_GTm, O_Yg, O_gbuf = 4 * SEQ, 4 * SEQ + 8192, 4 * SEQ + 16384
        O_Ymode, O_Xmode, O_gfT = 0, 16384, 0
    GTmre = A.bf16(O_GTm, 128 * 64)
    GTmim = A.bf16(O_GTm + 4096, 128 * 64)
    Ygbuf = A.bf16(O_Yg, 64 * 256)
    gbuf = A.bf16(O_gbuf, 128 * 2 * 64, QDs)
    Ymode = A.bf16(O_Ymode, 128 * 256)
    Xmode = A.bf16(O_Xmode, NGRP * 2 * 128)
    gfT = A.bf16(O_gfT, 128 * 2 * KB)

    assert nw <= 128
    s4t = [A.f32(LB + i * 512, 512, 64) for i in range(5)]
    s4g = [A.bf16(LB + 2560, 3 * QDs),
           A.bf16(LB + 2560 + (3 * QDs + 1) // 2, 3 * QDs)]
    cl = A.f32(LB + 2944, 64)
    ct_sb = A.bf16(LB + 3008, D // 2)
    kfs = A.bf16(LB + 3264, 512, 64)
    ctt = A.f32(LB + 3520, 128, 64)
    k4 = A.f32(LB + 3648, 4 * nw, 64)
    tstg = A.f32(LB + 4160, 512, 1)
    i1r = A.f32(LB + 4672, 512, 1)
    i1i = A.f32(LB + 5184, 512, 1)
    wr = A.f32(LB + 5696, 512, 1)
    wi = A.f32(LB + 6208, 512, 1)
    dsk2 = A.f32(LB + 6720, 2, 1)
    assert LB + 7008 <= s.AW

    c.dma(nc.sync, s.fft_s1_sb[:, :], s.cst['f_s1'][:, :], q=Q + 'ld')
    c.dma(nc.sync, s.s4_s1a_sb[:, :], s.cst['s4_s1a'][:, :], q=Q + 'ld')
    c.dma(nc.sync, s.s4_s1b_sb[:, :], s.cst['s4_s1b'][:, :], q=Q + 'ld')
    c.dma(nc.sync, s.sgncol_sb[:, :], s.cst['sign_col'][:, :], q=Q + 'ld')
    c.dma(nc.sync, s.sgnrow_sb[:, :], s.cst['sign_row_n'][:, :], q=Q + 'ld')
    c.dma(nc.sync, s.fftb_re_sb[:, :], s.cst['b_re'][:, :], q=Q + 'ld')
    c.dma(nc.sync, s.fftb_im_sb[:, :], s.cst['b_im'][:, :], q=Q + 'ld')

    vc = s.col("s4_vc", 24)
    for i, t in enumerate(["B_re", "B_im", "P_re", "P_im", "Q_re", "Q_im",
                           "diag_re", "diag_im"]):
        c.dma(nc.sync, vc[0:64, i:i + 1], getattr(s, "s4_" + t)[:, None], q=Q + 'ld')
    Bre, Bim, Pre, Pim, Qre, Qim, dre, dim = (vc[0:64, i:i + 1] for i in range(8))
    QBre, QBim, QPre, QPim = (vc[0:64, i:i + 1] for i in range(8, 12))
    nQBim, nQPim = (vc[0:64, i:i + 1] for i in range(12, 14))
    acol, a2col, i2dc = (vc[0:64, i:i + 1] for i in range(14, 17))
    c.dma(nc.sync, dsk2[0:1, 0:1], s.s4_step[:, None], q=Q + 'ld')
    c.wait(V, Q + 'ld')
    V.tensor_scalar(dsk2[0:1, 0:1], dsk2[0:1, 0:1], 1e-6, 0.5, op0=AL.max, op1=AL.mult)
    V.drain()
    s.recip_nr(V, dsk2[0:1, 1:2], dsk2[0:1, 0:1], dsk2[0:1, 0:1])
    c.mark(V.drain(), Q + 'vv')
    c.wait(nc.tensor, Q + 'vv')
    c.mark(nc.tensor.matmul(s.pb[7][0:64, 0:1], s.onr(n=64),
                            dsk2[0:1, 1:2], start=True, stop=True), Q + 'mm')
    c.wait(V, Q + 'mm')
    V.tensor_copy(i2dc, s.pb[7][0:64, 0:1])
    c.mark(V.drain(), Q + 'gg')
    tA, tB = (vc[0:64, i:i + 1] for i in range(19, 21))
    def cmul(outre, outim, ar, ai, br, bi):
        V.tensor_mul(tA, ar, br); V.drain()
        V.tensor_mul(tB, ai, bi); V.drain()
        V.tensor_sub(outre, tA, tB); V.drain()
        V.tensor_mul(tA, ar, bi); V.drain()
        V.tensor_mul(tB, ai, br); V.drain()
        V.tensor_add(outim, tA, tB); V.drain()
    cmul(QBre, QBim, Qre, Qim, Bre, Bim)
    cmul(QPre, QPim, Qre, Qim, Pre, Pim)
    V.tensor_scalar(nQBim, QBim, -1.0, None, op0=AL.mult); V.drain()
    V.tensor_scalar(nQPim, QPim, -1.0, None, op0=AL.mult); V.drain()
    V.tensor_scalar(acol, dre, -1.0, None, op0=AL.mult); V.drain()
    V.tensor_mul(a2col, dre, dre)
    c.mark(V.drain(), Q + 'vv')

    bt, dt_, rc, t4, t5 = s4t
    psB = s.pb[7]
    psK = s.pb[6]
    c.wait(G, Q + 'ld')
    for lc in range(NCH):
        sl = slice(lc * 512, (lc + 1) * 512)
        c.wait(nc.sync, Q + 'mm')
        c.dma(nc.sync, tstg, s.cst['s4_timag'][0:1, sl], q=Q + 'ld')
        c.wait(nc.tensor, Q + 'vv')
        c.wait(nc.tensor, Q + 'gg')
        c.wait(nc.tensor, Q + 'ld')
        m = nc.tensor.matmul(psB[0:64, 0:512], s.onr(n=64), tstg,
                             start=True, stop=True)
        c.mark(m, Q + 'mm')
        c.wait(V, Q + 'mm')
        V.tensor_scalar(bt[:, :], psB[0:64, 0:512], i2dc, dim,
                        op0=AL.mult, op1=AL.subtract)
        c.mark(V.drain(), Q + 'vv')
        c.wait(S, Q + 'vv')
        S.activation(dt_[:, :], bt[:, :], AF.Square)
        c.mark(S.drain(), Q + 'ss')
        c.wait(V, Q + 'ss')
        V.tensor_scalar(dt_[:, :], dt_[:, :], a2col, None, op0=AL.add)
        V.drain()
        s.recip_nr(V, rc[:, :], dt_[:, :], t4[:, :])
        V.tensor_scalar(cre[:, sl], rc[:, :], acol, None, op0=AL.mult)
        V.drain()
        V.tensor_mul(t4[:, :], bt[:, :], rc[:, :])
        V.drain()
        V.tensor_scalar(cim[:, sl], t4[:, :], -1.0, None, op0=AL.mult)
        c.mark(V.drain(), Q + 'vv')
        for ki, (ca, cb, aa, bb) in enumerate([
            (QBre, nQBim, cre, cim), (QBre, QBim, cim, cre),
            (QPre, nQPim, cre, cim), (QPre, QPim, cim, cre),
        ]):
            c.wait(nc.tensor, Q + 'vv')
            nc.tensor.matmul(psK[0:1, 0:512], ca, aa[:, sl], start=True, stop=False)
            m = nc.tensor.matmul(psK[0:1, 0:512], cb, bb[:, sl], start=False, stop=True)
            c.mark(m, Q + 'mm')
            c.wait(V, Q + 'mm')
            c.wait(V, Q + 'ld')
            V.tensor_copy(wr, psK[0:1, 0:512])
            c.mark(V.drain(), Q + 'vv')
            c.wait(nc.sync, Q + 'vv')
            c.dma(nc.sync, s.rowtmp[0][None, :], wr, q=Q + 'ld')
            c.chain(Q + 'ld')
            c.dma(nc.sync, k4[lc * APC:(lc + 1) * APC, ki * nw:(ki + 1) * nw],
                  s.rowtmp[0].rearrange("(a b) -> a b", b=nw), q=Q + 'ld',
                  slow=True)

    ka = k4[0:64, 0:nw]
    kb = k4[0:64, nw:2 * nw]
    kc = k4[0:64, 2 * nw:3 * nw]
    kd = k4[0:64, 3 * nw:4 * nw]
    tden = ctt[0:64, 0:nw]
    tden2 = t4[0:64, 0:nw]
    trec = rc[0:64, 0:nw]
    c.wait(V, Q + 'ld')
    V.tensor_scalar_add(kc, kc, 1.0)
    V.drain()
    V.tensor_mul(tden, kc, kc)
    V.drain()
    V.tensor_mul(tden2, kd, kd)
    V.drain()
    V.tensor_add(tden, tden, tden2)
    V.drain()
    s.recip_nr(V, trec, tden, tden2)
    V.tensor_mul(tden, ka, kc)
    V.drain()
    V.tensor_mul(tden2, kb, kd)
    V.drain()
    V.tensor_add(tden, tden, tden2)
    V.drain()
    V.tensor_mul(tden2, kb, kc)
    V.drain()
    V.tensor_mul(kb, ka, kd)
    V.drain()
    V.tensor_sub(tden2, tden2, kb)
    V.drain()
    V.tensor_mul(ka, tden, trec)
    V.drain()
    V.tensor_mul(kb, tden2, trec)
    c.mark(V.drain(), Q + 'vv')

    for lc in range(NCH):
        sl = slice(lc * 512, (lc + 1) * 512)
        c.wait(nc.sync, Q + 'vv')
        c.wait(nc.sync, Q + 'mm')
        c.dma(nc.sync, s.rowtmp[0].rearrange("(a b) -> a b", b=nw),
              ka[lc * APC:(lc + 1) * APC, :], q=Q + 'ld', slow=True)
        c.dma(nc.sync, s.rowtmp[1].rearrange("(a b) -> a b", b=nw),
              kb[lc * APC:(lc + 1) * APC, :], q=Q + 'ld', slow=True)
        c.chain(Q + 'ld')
        c.dma(nc.sync, wr, s.rowtmp[0][None, :], q=Q + 'ld')
        c.dma(nc.sync, wi, s.rowtmp[1][None, :], q=Q + 'ld')
        c.dma(nc.sync, i1r, s.cst['s4_i1z_re'][0:1, sl], q=Q + 'ld')
        c.dma(nc.sync, i1i, s.cst['s4_i1z_im'][0:1, sl], q=Q + 'ld')
        c.wait(nc.tensor, Q + 'vv')
        c.wait(nc.tensor, Q + 'ld')
        m = nc.tensor.matmul(psB[0:64, 0:512], s.onr(n=64), wr,
                             start=True, stop=True)
        c.mark(m, Q + 'mm')
        c.wait(V, Q + 'mm')
        V.tensor_copy(bt[:, :], psB[0:64, 0:512])
        c.mark(V.drain(), Q + 'vv')
        c.wait(nc.tensor, Q + 'vv')
        m = nc.tensor.matmul(psB[0:64, 0:512], s.onr(n=64), wi,
                             start=True, stop=True)
        c.mark(m, Q + 'mm')
        c.wait(V, Q + 'mm')
        V.tensor_copy(dt_[:, :], psB[0:64, 0:512])
        c.mark(V.drain(), Q + 'vv')
        V.tensor_scalar(rc[:, :], bt[:, :], Pre, None, op0=AL.mult)
        V.drain()
        V.tensor_scalar(t4[:, :], dt_[:, :], Pim, None, op0=AL.mult)
        V.drain()
        V.tensor_sub(rc[:, :], rc[:, :], t4[:, :])
        V.drain()
        V.tensor_scalar(rc[:, :], rc[:, :], -1.0, Bre, op0=AL.mult, op1=AL.add)
        V.drain()
        V.tensor_scalar(t4[:, :], dt_[:, :], Pre, None, op0=AL.mult)
        V.drain()
        V.tensor_scalar(bt[:, :], bt[:, :], Pim, None, op0=AL.mult)
        V.drain()
        V.tensor_add(t4[:, :], t4[:, :], bt[:, :])
        V.drain()
        V.tensor_scalar(t4[:, :], t4[:, :], -1.0, Bim, op0=AL.mult, op1=AL.add)
        V.drain()
        V.tensor_mul(bt[:, :], cre[:, sl], rc[:, :])
        V.drain()
        V.tensor_mul(dt_[:, :], cim[:, sl], t4[:, :])
        V.drain()
        V.tensor_sub(bt[:, :], bt[:, :], dt_[:, :])
        V.drain()
        V.tensor_mul(dt_[:, :], cre[:, sl], t4[:, :])
        V.drain()
        V.tensor_mul(rc[:, :], cim[:, sl], rc[:, :])
        V.drain()
        V.tensor_add(dt_[:, :], dt_[:, :], rc[:, :])
        c.mark(V.drain(), Q + 'vv')
        c.wait(nc.tensor, Q + 'vv')
        m = nc.tensor.matmul(psB[0:64, 0:512], s.onr(n=64), i1r,
                             start=True, stop=True)
        c.mark(m, Q + 'mm')
        c.wait(V, Q + 'mm')
        V.tensor_mul(rc[:, :], psB[0:64, 0:512], bt[:, :])
        V.drain()
        V.tensor_mul(t4[:, :], psB[0:64, 0:512], dt_[:, :])
        c.mark(V.drain(), Q + 'vv')
        c.wait(nc.tensor, Q + 'vv')
        m = nc.tensor.matmul(psB[0:64, 0:512], s.onr(n=64), i1i,
                             start=True, stop=True)
        c.mark(m, Q + 'mm')
        c.wait(V, Q + 'mm')
        V.tensor_mul(t5[:, :], psB[0:64, 0:512], dt_[:, :])
        V.drain()
        V.tensor_mul(dt_[:, :], psB[0:64, 0:512], bt[:, :])
        V.drain()
        V.tensor_sub(gmre[:, sl], rc[:, :], t5[:, :])
        V.drain()
        V.tensor_add(gmim[:, sl], t4[:, :], dt_[:, :])
        c.mark(V.drain(), Q + 'vv')

    psTr = [s.pb[3], s.pb[0]]
    c.wait(nc.tensor, Q + 'vv')
    for comp, (gmc, GTm) in enumerate(((gmre, GTmre), (gmim, GTmim))):
        for p8 in range(16):
            pt = psTr[p8 % 2]
            ev = V if p8 % 2 == 0 else S
            evq = Q + ('vv' if p8 % 2 == 0 else 'ss')
            c.wait(nc.tensor, evq)
            for pi in range(8):
                lp = p8 * 8 + pi
                c.mark(nc.tensor.transpose(
                    pt[0:QDs, 0:512].bitcast(BF16)[0:QDs, pi * 64:(pi + 1) * 64],
                    gmc[0:64, lp::128], s.id_b[0:64, 0:64]), Q + 'mm')
            c.wait(ev, Q + 'mm')
            if ev is V:
                V.tensor_copy(GTm[0:QDs, p8 * 512:(p8 + 1) * 512],
                              pt[0:QDs, 0:512].bitcast(BF16)[0:QDs, 0:512])
            else:
                S.copy(GTm[0:QDs, p8 * 512:(p8 + 1) * 512],
                       pt[0:QDs, 0:512].bitcast(BF16)[0:QDs, 0:512])
            c.mark(ev.drain(), evq)
    c.wait(nc.tensor, Q + 'vv')
    c.wait(nc.tensor, Q + 'ss')

    psYr = [s.pb[0], s.pb[3]]
    for n in range(64):
        ps = psYr[n % 2]
        ev = V if n % 2 == 0 else S
        evq = Q + ('vv' if n % 2 == 0 else 'ss')
        c.wait(nc.tensor, evq)
        nc.tensor.matmul(ps[0:128, 0:256], GTmre[0:QDs, n::64],
                         s.s4_s1a_sb[0:QDs, :], start=True, stop=False)
        m = nc.tensor.matmul(ps[0:128, 0:256], GTmim[0:QDs, n::64],
                             s.s4_s1b_sb[0:QDs, :], start=False, stop=True)
        c.mark(m, Q + 'mm')
        c.wait(ev, Q + 'mm')
        if ev is V:
            V.tensor_copy(Ygbuf[:, n * 256:(n + 1) * 256], ps[0:128, 0:256])
        else:
            S.copy(Ygbuf[:, n * 256:(n + 1) * 256], ps[0:128, 0:256])
        c.mark(ev.drain(), evq)
    c.wait(nc.tensor, Q + 'vv')
    c.wait(nc.tensor, Q + 'ss')

    psG2r = [s.pb[1], s.pb[2]]
    for t1 in range(128):
        gb = s4g[t1 % 2]
        c.wait_lag(nc.sync, Q + 'mm', 1)
        c.dma(nc.sync, gb[:, :], s.cst['s4_gall'][t1], q=Q + 'ld')
        c.wait(nc.tensor, Q + 'ld')
        ps2 = psG2r[t1 % 2]
        ev = V if t1 % 2 == 0 else S
        evq = Q + ('vv' if t1 % 2 == 0 else 'ss')
        c.wait(nc.tensor, evq)
        Ygre = Ygbuf[:, t1::256]
        Ygim = Ygbuf[:, 128 + t1::256]
        nc.tensor.matmul(ps2[0:QDs, 0:64], gb[:, 0:QDs], Ygre,
                         start=True, stop=False)
        nc.tensor.matmul(ps2[0:QDs, 0:64], gb[:, 2 * QDs:3 * QDs], Ygim,
                         start=False, stop=True)
        nc.tensor.matmul(ps2[0:QDs, 64:128], gb[:, QDs:2 * QDs], Ygre,
                         start=True, stop=False)
        m = nc.tensor.matmul(ps2[0:QDs, 64:128], gb[:, 0:QDs], Ygim,
                             start=False, stop=True)
        c.mark(m, Q + 'mm')
        c.wait(ev, Q + 'mm')
        if ev is V:
            V.tensor_copy(gbuf[0:QDs, t1 * 128:(t1 + 1) * 128], ps2[0:QDs, 0:128])
        else:
            S.copy(gbuf[0:QDs, t1 * 128:(t1 + 1) * 128], ps2[0:QDs, 0:128])
        c.mark(ev.drain(), evq)
    c.wait(nc.tensor, Q + 'vv')
    c.wait(nc.tensor, Q + 'ss')

    gfny = tstg
    emit_fft_fwd(s, Q, gbuf, 128, Ymode, Xmode, gfny[0:1, 0:128])

    psC = s.pb[2]
    ch0 = par * (D // 2)
    for cc in range(4):
        rows = slice(ch0 + cc * 128, ch0 + (cc + 1) * 128)
        c.wait(nc.sync, Q + 'mm')
        c.dma(nc.sync, cl[:, :], s.s4_C_re[rows, :], q=Q + 'ld')
        c.wait(nc.tensor, Q + 'ld')
        c.wait(nc.tensor, Q + 'vv')
        c.mark(nc.tensor.transpose(psC[0:64, 0:128], cl[:, :], s.id_f[:, :]),
               Q + 'mm')
        c.wait(V, Q + 'mm')
        V.tensor_copy(ct_sb[0:64, cc * 128:(cc + 1) * 128], psC[0:64, 0:128])
        c.mark(V.drain(), Q + 'vv')
        c.wait(nc.sync, Q + 'mm')
        c.dma(nc.sync, cl[:, :], s.s4_C_im[rows, :], q=Q + 'ld')
        c.wait(nc.tensor, Q + 'ld')
        c.wait(nc.tensor, Q + 'vv')
        c.mark(nc.tensor.transpose(psC[0:64, 0:128], cl[:, :], s.id_f[:, :]),
               Q + 'mm')
        c.wait(V, Q + 'mm')
        V.tensor_scalar(kfs[0:64, 0:128], psC[0:64, 0:128], -1.0, None,
                        op0=AL.mult)
        c.mark(V.drain(), Q + 'vv')
        c.wait(nc.sync, Q + 'vv')
        c.dma(nc.sync, ct_sb[64:128, cc * 128:(cc + 1) * 128],
              kfs[0:64, 0:128], q=Q + 'ld')
    c.wait(nc.tensor, Q + 'ld')

    psCr = [s.pb[2], s.pb[0]]
    for k1 in range(128):
        po = (k1 % PACKE) * PSTR
        g = k1 // PACKE
        for comp in range(2):
            pc2 = psCr[comp]
            ev = V if comp == 0 else S
            evq = Q + ('vv' if comp == 0 else 'ss')
            c.wait(nc.tensor, evq)
            c.mark(nc.tensor.transpose(
                pc2[0:128, 0:128].bitcast(BF16)[0:128, 0:KB],
                Xmode[po:po + KB, (g * 2 + comp) * 128:(g * 2 + comp + 1) * 128],
                s.id_b[po:po + KB, po:po + KB]), Q + 'mm')
            c.wait(ev, Q + 'mm')
            if ev is V:
                V.tensor_copy(gfT[:, (k1 * 2 + comp) * KB:(k1 * 2 + comp + 1) * KB],
                              pc2[0:128, 0:128].bitcast(BF16)[0:128, 0:KB])
            else:
                S.copy(gfT[:, (k1 * 2 + comp) * KB:(k1 * 2 + comp + 1) * KB],
                       pc2[0:128, 0:128].bitcast(BF16)[0:128, 0:KB])
            c.mark(ev.drain(), evq)
    c.wait(nc.tensor, Q + 'vv')
    c.wait(nc.tensor, Q + 'ss')
    kfs2 = A.bf16(LB + 6752, 512, 64)
    kfsr = [kfs, kfs2]
    for k1 in range(128):
        po = (k1 % PACKE) * PSTR
        g = k1 // PACKE
        for comp in range(2):
            pc2 = psCr[comp]
            ev = V if comp == 0 else S
            evq = Q + ('vv' if comp == 0 else 'ss')
            kf_b = kfsr[comp]
            c.wait(nc.tensor, evq)
            m = nc.tensor.matmul(pc2[0:KB, 0:D // 2],
                                 gfT[:, (k1 * 2 + comp) * KB:(k1 * 2 + comp + 1) * KB],
                                 ct_sb[:, :], start=True, stop=True)
            c.mark(m, Q + 'mm')
            c.wait(ev, Q + 'mm')
            c.wait(ev, Q + 'ld')
            if ev is V:
                V.tensor_copy(kf_b[0:KB, 0:D // 2], pc2[0:KB, 0:D // 2])
            else:
                S.copy(kf_b[0:KB, 0:D // 2], pc2[0:KB, 0:D // 2])
            c.mark(ev.drain(), evq)
            c.wait(nc.sync, evq)
            c.dma(nc.sync, s.kf_dram[po:po + KB, g, comp, :], kf_b[0:KB, 0:D // 2],
                  q=Q + 'ld')
    gfnyc_f = cl[:, 0:1]
    gfnyc = s.gfnyc_sb
    c.wait(nc.sync, Q + 'ld')
    c.dma(nc.sync, s.rowtmp[0][None, 0:128], gfny[0:1, 0:128], q=Q + 'ld')
    c.chain(Q + 'ld')
    c.dma(nc.sync, gfnyc_f,
          s.rowtmp[0, 0:128].rearrange("(p x) -> p x", x=1),
          q=Q + 'ld', slow=True)
    c.wait(V, Q + 'ld')
    V.tensor_copy(gfnyc[:, :], gfnyc_f)
    c.mark(V.drain(), Q + 'vv')
    c.wait(nc.tensor, Q + 'vv')
    m = nc.tensor.matmul(psC[0:1, 0:D // 2], gfnyc[:, :], ct_sb[:, :],
                         start=True, stop=True)
    c.mark(m, Q + 'mm')
    c.wait(V, Q + 'mm')
    V.tensor_copy(s.kfny_rowb[0:1, :], psC[0:1, 0:D // 2])
    c.mark(V.drain(), Q + 'vv')

_BUILD_CACHE = None


def _build():
    global _BUILD_CACHE
    if _BUILD_CACHE is None:
        mk = MKBuild(8192, 1024, NCORES=8)
        nc, consts = mk.build(tail=emit_block1)
        _BUILD_CACHE = (mk, nc, consts)
    return _BUILD_CACHE


_IN_MAP = {
    "W_rkv": "L0_attn_W_rkv", "b_rkv": "L0_attn_b_rkv",
    "W_o": "L0_attn_W_o", "b_o": "L0_attn_b_o",
    "nu_log": "L0_attn_nu_log", "gamma_log": "L0_attn_gamma_log",
    "Wg": "L0_ffn_Wg", "ln_scale": "L0_ffn_ln_scale",
    "ln_bias": "L0_ffn_ln_bias", "W1": "L0_ffn_W1", "b1": "L0_ffn_b1",
    "W2": "L0_ffn_W2", "b2": "L0_ffn_b2",
    "l1w": "L1_ffn_weight", "l1b": "L1_ffn_bias",
    "Dskip": "L1_attn_D", "s4_step": "L1_attn_step",
    "s4_C_re": "L1_attn_C_re", "s4_C_im": "L1_attn_C_im",
    "s4_B_re": "L1_attn_B_re", "s4_B_im": "L1_attn_B_im",
    "s4_P_re": "L1_attn_P_re", "s4_P_im": "L1_attn_P_im",
    "s4_Q_re": "L1_attn_Q_re", "s4_Q_im": "L1_attn_Q_im",
    "s4_diag_re": "L1_attn_diag_re", "s4_diag_im": "L1_attn_diag_im",
}


def kernel(**inputs):
    from concourse.bass_utils import run_bass_kernel_spmd
    mk, nc, consts = _build()
    x = np.ascontiguousarray(np.asarray(inputs["L0_attn_inputs"], np.float32))
    B, SEQ, D = x.shape
    T = SEQ // 2
    common = {k: np.ascontiguousarray(np.asarray(inputs[v], np.float32))
              for k, v in _IN_MAP.items()}
    common.update(consts)
    in_maps = []
    for c in range(8):
        b, half = c // 2, c % 2
        m = dict(common)
        m["x_in"] = np.ascontiguousarray(x[b, half * T:(half + 1) * T, :])
        in_maps.append(m)
    res = run_bass_kernel_spmd(nc, in_maps, core_ids=list(range(8)))
    out = np.empty((B, SEQ, D), np.float32)
    for c in range(8):
        b, half = c // 2, c % 2
        out[b, half * T:(half + 1) * T, :] = res.results[c]["y_out"]
    return out
